# version 02
# baseline (speedup 1.0000x reference)
import numpy as np
import ml_dtypes
import concourse.bass as bass
import concourse.mybir as mybir
from concourse.bass_utils import run_bass_kernel_spmd

BF16NP = ml_dtypes.bfloat16
F32 = mybir.dt.float32
BF16 = mybir.dt.bfloat16
AF = mybir.ActivationFunctionType
ALU = mybir.AluOpType

B, S, D = 2, 8192, 1024
NH, DKH = 16, 64
HPC = 4
PAD = 512
SP = S + PAD
OWN = 2048
HALO = 512
NTOK = OWN + HALO
NE, TOPK, HF = 8, 4, 2730
NG, GS = 4, 4
WIN = 1024 - 512
TT = 512
NT_S = S // TT
NT_N = NTOK // TT
NT_O = OWN // TT


class Sem:
    def __init__(self, handle):
        self.h = handle
        self.n = 0

    def inc(self, k=1):
        self.n += k
        return self.n


class Ctx:
    ENGINES = ("sync", "tensor", "vector", "scalar", "gpsimd")

    def __init__(self, nc, stack):
        self.nc = nc
        self.stack = stack
        self.streams = {e: [] for e in self.ENGINES}
        self._sems = {}

    def sem(self, name):
        if name not in self._sems:
            h = self.stack.enter_context(self.nc.semaphore(name))
            self._sems[name] = Sem(h)
        return self._sems[name]

    def on(self, engine):
        def deco(fn):
            self.streams[engine].append(fn)
            return fn
        return deco

    def finalize(self, block):
        for eng in self.ENGINES:
            fns = self.streams[eng]
            deco = getattr(block, eng)

            def make(fns):
                def body(e):
                    for f in fns:
                        f(e)
                return body
            deco(make(fns))


def sbuf(stack, nc, name, shape, dtype):
    return stack.enter_context(nc.sbuf_tensor(name, shape, dtype))


def psum(stack, nc, name, shape, dtype=F32):
    return stack.enter_context(nc.psum_tensor(name, shape, dtype))


def _dft_consts():
    n = np.arange(D)
    ang = 2.0 * np.pi * np.outer(n, n) / D
    C = np.cos(ang).astype(np.float32)
    Sn = np.sin(ang).astype(np.float32)
    return (C.astype(BF16NP), Sn.astype(BF16NP),
            (C / D).astype(BF16NP), (-Sn / D).astype(BF16NP))


def _l0_diag_masks():
    r_ = np.arange(4)[:, None, None]
    k = np.arange(128)[None, :, None]
    q = np.arange(TT)[None, None, :]
    return ((128 * r_ + k) <= q).astype(BF16NP)


def _l1_masks(own_start):
    out = np.zeros((NT_O, 8, 128, TT), dtype=BF16NP)
    for qi in range(NT_O):
        q0 = own_start + qi * TT
        qa = q0 + np.arange(TT)[None, :]
        for r in range(8):
            ka = q0 - 512 + 128 * r + np.arange(128)[:, None]
            m = (ka <= qa) & (ka > qa - 512) & (ka >= 0)
            out[qi, r] = m.astype(BF16NP)
    return out


def _qperm():
    old = np.zeros(D, np.int64)
    for c in range(8):
        for half in range(2):
            g = 2 * (c // 4) + half
            m = c % 4
            h = 4 * g + m
            old[c * 128 + 64 * half: c * 128 + 64 * half + 64] = \
                np.arange(h * 64, h * 64 + 64)
    return old


_QPERM = _qperm()


def l1_head_slot(h):
    g, m = h // 4, h % 4
    return (g // 2) * 4 + m, 64 * (g % 2)


def prep_core_inputs(core, d):
    b, q = core // 4, core % 4
    h0 = HPC * q
    own_start = q * OWN

    X = d["L0_attn_X"]
    xt = np.zeros((D, SP), np.float32)
    xt[:, PAD:] = np.ascontiguousarray(X[b].T)

    WqkvT = d["L0_attn_W_qkv"].T
    wqkvT_c = np.ascontiguousarray(np.concatenate(
        [WqkvT[:, h0 * 64:(h0 + HPC) * 64],
         WqkvT[:, D + h0 * 64:D + (h0 + HPC) * 64],
         WqkvT[:, 2 * D + h0 * 64:2 * D + (h0 + HPC) * 64]], axis=1))

    Cb, Sb, C2b, nS2b = _dft_consts()

    inp = {
        "xt": xt,
        "xs": np.ascontiguousarray(xt[:, own_start:own_start + NTOK]),
        "wqkvT": wqkvT_c,
        "woutT": np.ascontiguousarray(d["L0_attn_W_out"].T),
        "wr": d["L0_ffn_w_freq_real"],
        "wi": d["L0_ffn_w_freq_imag"],
        "l0bias": d["L0_ffn_bias"],
        "l0bias_pc": np.ascontiguousarray(d["L0_ffn_bias"].reshape(8, 128).T),
        "cb": Cb, "sb": Sb, "c2b": C2b, "ns2b": nS2b,
        "l0dm": _l0_diag_masks(),
        "wq1": np.ascontiguousarray(d["L1_attn_Wq"][:, _QPERM]),
        "bq_pc": np.ascontiguousarray(
            d["L1_attn_bq"][_QPERM].reshape(8, 128).T),
        "wk1": d["L1_attn_Wk"],
        "bk_pc": np.ascontiguousarray(d["L1_attn_bk"].reshape(2, 128).T),
        "wv1": d["L1_attn_Wv"],
        "bv_row": np.ascontiguousarray(d["L1_attn_bv"].reshape(1, 256)),
        "wo1": np.ascontiguousarray(d["L1_attn_Wo"][_QPERM, :]),
        "bo_pc": np.ascontiguousarray(d["L1_attn_bo"].reshape(8, 128).T),
        "sel2": np.eye(2, dtype=np.float32),
        "l1m": _l1_masks(own_start),
        "wgT": np.ascontiguousarray(d["L1_ffn_Wg"].T),
        "rbias_c": np.ascontiguousarray(
            d["L1_ffn_routing_bias"].reshape(8, 1)),
        "we1": np.ascontiguousarray(np.concatenate(
            [d["L1_ffn_Ws1"].T[None], d["L1_ffn_W1"].transpose(0, 2, 1)])),
        "we2": np.ascontiguousarray(np.concatenate(
            [d["L1_ffn_Ws2"].T[None], d["L1_ffn_W2"].transpose(0, 2, 1)])),
        "we3": np.ascontiguousarray(np.concatenate(
            [d["L1_ffn_Ws3"].T[None], d["L1_ffn_W3"].transpose(0, 2, 1)])),
        "eye8": np.eye(8, dtype=np.float32),
        "eye128": np.eye(128, dtype=np.float32),
        "sel8": _sel8(),
    }
    return inp


def _sel8():
    s = np.zeros((8, 8 * 128), np.float32)
    for e in range(8):
        s[e, e * 128:(e + 1) * 128] = 1.0
    return s


def emit_p1(ctx, bufs, ap):
    nc = ctx.nc
    xt, wqkvT = ap["xt"], ap["wqkvT"]
    ktT, qtT, vkm = bufs["ktT"], bufs["qtT"], bufs["vkm"]
    nrm_all, nrm = bufs["nrm_all"], bufs["nrm"]
    wq_bf, wld = bufs["p1_wbf"], bufs["p1_wld"]
    xf, xb = bufs["p1_xf"], bufs["p1_xb"]
    sqb = bufs["p1_sqb"]
    ones2 = bufs["ones2"]
    ps_qk = [bufs["ps_a"], bufs["ps_b"]]
    ps_v = [bufs["ps_c"], bufs["ps_d"]]
    ps_n = bufs["ps_n"]

    s_wld = [ctx.sem("p1_wld0"), ctx.sem("p1_wld1")]
    s_wcv = ctx.sem("p1_wcv")
    s_x = [ctx.sem("p1_x0"), ctx.sem("p1_x1")]
    s_xcv = ctx.sem("p1_xcv")
    s_mm = ctx.sem("p1_mm")
    s_ev = ctx.sem("p1_ev")
    s_mmv = ctx.sem("p1_mmv")
    s_evv = ctx.sem("p1_evv")
    s_sq = ctx.sem("p1_sq")
    s_nm = ctx.sem("p1_nm")
    s_nr = ctx.sem("p1_nr")

    wld_after = [s_wld[i % 2].inc(16) for i in range(8)]
    wcv_after = [s_wcv.inc() for _ in range(8)]
    x_after = [s_x[i % 2].inc(16) for i in range(NT_S)]
    xcv_after = [s_xcv.inc() for _ in range(NT_S)]
    mm_after = {}
    ev_after = {}
    sq_after = {}
    nm_after = {}
    nr_after = {}
    mmv_after = {}
    evv_after = {}
    nm_after_flat = {}
    for tt in range(NT_S):
        for ec in range(4):
            mm_after[tt, ec] = s_mm.inc()
            ev_after[tt, ec] = s_ev.inc()
            sq_after[tt, ec] = s_sq.inc()
            nm_after[tt, ec] = s_nm.inc()
            nm_after_flat[tt * 4 + ec] = nm_after[tt, ec]
            nr_after[tt, ec] = s_nr.inc()
        for tsub in range(4):
            mmv_after[tt, tsub] = s_mmv.inc()
            evv_after[tt, tsub] = s_evv.inc()
    s_nr.inc()

    @ctx.on("sync")
    def _(q):
        for dc in range(8):
            if dc >= 2:
                q.wait_ge(s_wcv.h, wcv_after[dc - 2])
            q.dma_start(out=wld[dc % 2][:],
                        in_=wqkvT[dc * 128:(dc + 1) * 128, :]
                        ).then_inc(s_wld[dc % 2].h, 16)
        for tt in range(NT_S):
            if tt >= 2:
                q.wait_ge(s_xcv.h, xcv_after[tt - 2])
            col0 = PAD + tt * TT
            q.dma_start(
                out=xf[tt % 2][:],
                in_=xt.rearrange("(dc p) n -> p dc n", p=128)[:, :, col0:col0 + TT],
            ).then_inc(s_x[tt % 2].h, 16)

    @ctx.on("vector")
    def _(v):
        v.memset(ones2[:], 0.0)
        v.memset(ones2[0:64, 0:1], 1.0)
        v.memset(ones2[64:128, 1:2], 1.0)
        for h in range(HPC):
            v.memset(vkm[:, :, h * 65 + 64], 1.0)
        for dc in range(8):
            v.wait_ge(s_wld[dc % 2].h, wld_after[dc])
            v.tensor_copy(wq_bf[:, dc, :], wld[dc % 2][:]).then_inc(s_wcv.h, 1)
        for tt in range(NT_S):
            v.wait_ge(s_x[tt % 2].h, x_after[tt])
            if tt >= 2:
                v.wait_ge(s_mmv.h, mmv_after[tt - 2, 3])
            v.tensor_copy(xb[tt % 2][:], xf[tt % 2][:]).then_inc(s_xcv.h, 1)
            for ec in range(4):
                v.wait_ge(s_nm.h, nm_after[tt, ec])
                v.reduce_max(nrm_all[0:2, ec, tt:tt + 1], ps_n[0:2, :],
                             axis=mybir.AxisListType.X).then_inc(s_nr.h, 1)
            for tsub in range(4):
                v.wait_ge(s_mmv.h, mmv_after[tt, tsub])
                kc = tt * 4 + tsub
                v.tensor_copy(
                    vkm[:, kc, :].rearrange("p (h j) -> p h j", j=65)[:, :, 0:64],
                    ps_v[tsub % 2][:].rearrange("p (h j) -> p h j", j=64),
                ).then_inc(s_evv.h, 1)
        v.reduce_max(nrm[0:2, :], nrm_all[0:2, :, :],
                     axis=mybir.AxisListType.X).then_inc(s_nr.h, 1)

    @ctx.on("tensor")
    def _(t):
        t.wait_ge(s_wcv.h, 8)
        for tt in range(NT_S):
            t.wait_ge(s_xcv.h, xcv_after[tt])
            for ec in range(4):
                idx = tt * 4 + ec
                if idx >= 2:
                    t.wait_ge(s_sq.h, idx - 1)
                for dc in range(8):
                    ins = t.matmul(ps_qk[ec % 2][:],
                                   wq_bf[:, dc, ec * 128:(ec + 1) * 128],
                                   xb[tt % 2][:, dc, :],
                                   start=(dc == 0), stop=(dc == 7))
                    if dc == 7:
                        ins.then_inc(s_mm.h, 1)
                t.wait_ge(s_sq.h, sq_after[tt, ec])
                if idx >= 1:
                    t.wait_ge(s_nr.h, idx)
                t.matmul(ps_n[0:2, :], ones2[:], sqb[idx % 2][:],
                         start=True, stop=True).then_inc(s_nm.h, 1)
            for tsub in range(4):
                idx = tt * 4 + tsub
                if idx >= 2:
                    t.wait_ge(s_evv.h, idx - 1)
                for dc in range(8):
                    ins = t.matmul(ps_v[tsub % 2][:],
                                   xb[tt % 2][:, dc, tsub * 128:(tsub + 1) * 128],
                                   wq_bf[:, dc, 512:768],
                                   start=(dc == 0), stop=(dc == 7))
                    if dc == 7:
                        ins.then_inc(s_mmv.h, 1)

    @ctx.on("scalar")
    def _(s):
        for tt in range(NT_S):
            for ec in range(4):
                idx = tt * 4 + ec
                s.wait_ge(s_mm.h, mm_after[tt, ec])
                dst = qtT if ec < 2 else ktT
                c = ec % 2
                cols = slice(tt * TT, (tt + 1) * TT)
                s.activation(dst[:, c, cols], ps_qk[ec % 2][:],
                             AF.Identity).then_inc(s_ev.h, 1)
                if idx >= 2:
                    s.wait_ge(s_nm.h, nm_after_flat[idx - 2])
                s.activation(sqb[idx % 2][:], ps_qk[ec % 2][:],
                             AF.Square).then_inc(s_sq.h, 1)


def build_p1(core):
    import contextlib
    nc = bass.Bass("TRN2", target_bir_lowering=False)
    ap = {
        "xt": nc.dram_tensor("xt", [D, SP], F32, kind="ExternalInput").ap(),
        "wqkvT": nc.dram_tensor("wqkvT", [D, 768], F32, kind="ExternalInput").ap(),
    }
    kt_o = nc.dram_tensor("kt_o", [2, 128, S], BF16, kind="ExternalOutput").ap()
    qt_o = nc.dram_tensor("qt_o", [2, 128, S], BF16, kind="ExternalOutput").ap()
    vkm_o = nc.dram_tensor("vkm_o", [128, S // 128, HPC * 65], BF16,
                           kind="ExternalOutput").ap()
    nrm_o = nc.dram_tensor("nrm_o", [2, 4], F32, kind="ExternalOutput").ap()

    with contextlib.ExitStack() as stack:
        bufs = {
            "ktT": sbuf(stack, nc, "ktT", [128, 2, S], BF16),
            "qtT": sbuf(stack, nc, "qtT", [128, 2, S], BF16),
            "vkm": sbuf(stack, nc, "vkm", [128, S // 128, HPC * 65], BF16),
            "nrm_all": sbuf(stack, nc, "nrm_all", [2, 4, NT_S], F32),
            "nrm": sbuf(stack, nc, "nrm", [2, 4], F32),
            "p1_wbf": sbuf(stack, nc, "p1_wbf", [128, 8, 768], BF16),
            "p1_wld": [sbuf(stack, nc, f"p1_wld{i}", [128, 768], F32) for i in range(2)],
            "p1_xf": [sbuf(stack, nc, f"p1_xf{i}", [128, 8, TT], F32) for i in range(2)],
            "p1_xb": [sbuf(stack, nc, f"p1_xb{i}", [128, 8, TT], BF16) for i in range(2)],
            "p1_sqb": [sbuf(stack, nc, f"p1_sqb{i}", [128, TT], BF16) for i in range(2)],
            "ones2": sbuf(stack, nc, "ones2", [128, 2], BF16),
            "ps_a": psum(stack, nc, "ps_a", [128, TT]),
            "ps_b": psum(stack, nc, "ps_b", [128, TT]),
            "ps_c": psum(stack, nc, "ps_c", [128, HPC * 64]),
            "ps_d": psum(stack, nc, "ps_d", [128, HPC * 64]),
            "ps_n": psum(stack, nc, "ps_n", [2, TT]),
        }
        ctx = Ctx(nc, stack)
        emit_p1(ctx, bufs, ap)

        s_st = ctx.sem("p1_st")
        s_ev, s_evv, s_nr = ctx.sem("p1_ev"), ctx.sem("p1_evv"), ctx.sem("p1_nr")
        n_ev, n_evv, n_nr = s_ev.n, s_evv.n, s_nr.n

        @ctx.on("gpsimd")
        def _(g):
            g.wait_ge(s_ev.h, n_ev)
            g.wait_ge(s_evv.h, n_evv)
            g.wait_ge(s_nr.h, n_nr)
            for c in range(2):
                g.dma_start(out=kt_o[c], in_=bufs["ktT"][:, c, :]).then_inc(s_st.h, 16)
                g.dma_start(out=qt_o[c], in_=bufs["qtT"][:, c, :]).then_inc(s_st.h, 16)
            g.dma_start(out=vkm_o[:], in_=bufs["vkm"][:]).then_inc(s_st.h, 16)
            g.dma_start(out=nrm_o[:], in_=bufs["nrm"][:]).then_inc(s_st.h, 16)
            g.wait_ge(s_st.h, 16 * 6)

        with nc.Block() as block:
            ctx.finalize(block)
    return nc


def emit_p2(ctx, bufs, ap):
    nc = ctx.nc
    ktT, qtT, vkm, nrm = bufs["ktT"], bufs["qtT"], bufs["vkm"], bufs["nrm"]
    attnT = bufs["attnT"]
    dm = bufs["l0dm"]
    ones32 = bufs["ones32"]
    sel = bufs["sel2"]
    negbg = bufs["negbg"]
    rows = bufs["p2_rows"]
    den = [bufs["p2_den0"], bufs["p2_den1"]]
    bcs = [bufs["p2_bcs0"], bufs["p2_bcs1"]]
    P = [bufs["p2_p0"], bufs["p2_p1"]]
    ps_s = [bufs["ps_a"], bufs["ps_b"]]
    ps_av = [bufs["ps_e"], bufs["ps_f"]]
    ps_bc = bufs["ps_g"]
    ps_r = bufs["ps_n"]

    s_bg = ctx.sem("p2_bg")
    s_vf = ctx.sem("vfence")
    s_mm = ctx.sem("p2_mm")
    s_exp = ctx.sem("p2_exp")
    s_pr = ctx.sem("p2_pr")
    s_avm = ctx.sem("p2_avm")
    s_rc = ctx.sem("p2_rc")
    s_bc = ctx.sem("p2_bc")
    s_bce = ctx.sem("p2_bce")
    s_no = ctx.sem("p2_no")

    hqs = [(h, qt) for h in range(HPC) for qt in range(NT_S)]
    nch = {(h, qt): 4 * (qt + 1) for h, qt in hqs}
    hq_i = {hq: i for i, hq in enumerate(hqs)}

    def is_diag(h, qt, kc):
        return kc >= 4 * qt

    mm_after, exp_after, pr_after, avm_after = {}, {}, {}, {}
    av_after, rc_after, bc_after, bce_after, no_after = {}, {}, {}, {}, {}
    gidx = {}
    g = 0
    for hq in hqs:
        for kc in range(nch[hq]):
            gidx[hq, kc] = g
            mm_after[g] = s_mm.inc()
            if is_diag(*hq, kc):
                exp_after[g] = s_exp.inc()
            pr_after[g] = s_pr.inc()
            avm_after[g] = s_avm.inc()
            g += 1
    for hq in hqs:
        rc_after[hq] = s_rc.inc()
        bc_after[hq] = s_bc.inc()
        bce_after[hq] = s_bce.inc()
        no_after[hq] = s_no.inc()

    @ctx.on("tensor")
    def _(t):
        t.matmul(ps_r[0:1, 0:4], sel[:, 0:1], nrm[:, :], start=True, stop=True)
        t.matmul(ps_bc[0:1, 0:4], sel[:, 1:2], nrm[:, :], start=True,
                 stop=True).then_inc(s_bg.h, 2)
        t.wait_ge(s_bg.h, 6)
        t.matmul(ps_bc[0:64, 0:4], ones32[0:1, 0:64], rows[0:1, 4:8],
                 start=True, stop=True).then_inc(s_bg.h, 1)
        t.wait_ge(s_bg.h, 8)
        for hq in hqs:
            h, qt = hq
            i = hq_i[hq]
            c, pp = h // 2, 64 * (h % 2)
            for kc in range(nch[hq]):
                gg = gidx[hq, kc]
                if gg >= 2:
                    t.wait_ge(s_pr.h, pr_after[gg - 2])
                t.matmul(ps_s[gg % 2][:],
                         ktT[pp:pp + 64, c, kc * 128:(kc + 1) * 128],
                         qtT[pp:pp + 64, c, qt * TT:(qt + 1) * TT],
                         start=True, stop=True).then_inc(s_mm.h, 1)
                t.wait_ge(s_pr.h, pr_after[gg])
                if kc == 0 and i >= 2:
                    t.wait_ge(s_no.h, no_after[hqs[i - 2]])
                t.matmul(ps_av[i % 2][:], vkm[:, kc, h * 65:(h + 1) * 65],
                         P[gg % 2][:], start=(kc == 0),
                         stop=(kc == nch[hq] - 1),
                         skip_group_check=True).then_inc(s_avm.h, 1)
            t.wait_ge(s_rc.h, rc_after[hq])
            if i >= 1:
                t.wait_ge(s_bce.h, bce_after[hqs[i - 1]])
            t.matmul(ps_bc[:], ones32[0:1, 0:64], den[i % 2][:],
                     start=True, stop=True).then_inc(s_bc.h, 1)

    @ctx.on("vector")
    def _(v):
        v.wait_ge(s_bg.h, 2)
        v.tensor_copy(rows[0:1, 8:12], ps_r[0:1, 0:4])
        v.tensor_copy(rows[0:1, 12:16], ps_bc[0:1, 0:4]).then_inc(s_vf.h, 1)
        v.wait_ge(s_vf.h, s_vf.inc())
        v.tensor_mul(rows[0:1, 0:2], rows[0:1, 8:10], rows[0:1, 10:12])
        v.tensor_mul(rows[0:1, 2:4], rows[0:1, 12:14],
                     rows[0:1, 14:16]).then_inc(s_bg.h, 2)
        for hq in hqs:
            h, qt = hq
            i = hq_i[hq]
            for kc in range(4 * qt, nch[hq]):
                gg = gidx[hq, kc]
                r = kc - 4 * qt
                v.wait_ge(s_exp.h, exp_after[gg])
                v.tensor_mul(P[gg % 2][:], P[gg % 2][:], dm[:, r, :]).then_inc(s_pr.h, 1)
            v.wait_ge(s_avm.h, avm_after[gidx[hq, nch[hq] - 1]])
            if i >= 2:
                v.wait_ge(s_bc.h, bc_after[hqs[i - 2]])
            v.reciprocal(den[i % 2][:], ps_av[i % 2][64:65, :]).then_inc(s_rc.h, 1)
            v.wait_ge(s_bce.h, bce_after[hq])
            c, pp = h // 2, 64 * (h % 2)
            v.tensor_mul(attnT[pp:pp + 64, c, qt * TT:(qt + 1) * TT],
                         ps_av[i % 2][0:64, :], bcs[i % 2][:]).then_inc(s_no.h, 1)

    @ctx.on("scalar")
    def _(s):
        s.wait_ge(s_bg.h, 4)
        s.activation(rows[0:1, 4:8:2], rows[0:1, 0:2], AF.Sqrt)
        s.activation(rows[0:1, 5:8:2], rows[0:1, 2:4], AF.Sqrt)
        s.mul(rows[0:1, 4:8], rows[0:1, 4:8], -0.125).then_inc(s_bg.h, 2)
        s.wait_ge(s_bg.h, 7)
        s.copy(negbg[0:64, :], ps_bc[0:64, 0:4])
        s.copy(negbg[64:128, :], ps_bc[0:64, 0:4]).then_inc(s_bg.h, 1)
        for hq in hqs:
            h, qt = hq
            i = hq_i[hq]
            for kc in range(nch[hq]):
                gg = gidx[hq, kc]
                s.wait_ge(s_mm.h, mm_after[gg])
                if gg >= 2:
                    s.wait_ge(s_avm.h, avm_after[gg - 2])
                ins = s.activation(P[gg % 2][:], ps_s[gg % 2][:], AF.Exp,
                                   bias=negbg[:, h:h + 1], scale=0.125)
                if is_diag(h, qt, kc):
                    ins.then_inc(s_exp.h, 1)
                else:
                    ins.then_inc(s_pr.h, 1)
            s.wait_ge(s_bc.h, bc_after[hq])
            if i >= 2:
                s.wait_ge(s_no.h, no_after[hqs[i - 2]])
            s.copy(bcs[i % 2][:], ps_bc[:]).then_inc(s_bce.h, 1)


def build_p2(core):
    import contextlib
    nc = bass.Bass("TRN2", target_bir_lowering=False)
    kt_i = nc.dram_tensor("kt_i", [2, 128, S], BF16, kind="ExternalInput").ap()
    qt_i = nc.dram_tensor("qt_i", [2, 128, S], BF16, kind="ExternalInput").ap()
    vkm_i = nc.dram_tensor("vkm_i", [128, S // 128, HPC * 65], BF16,
                           kind="ExternalInput").ap()
    nrm_i = nc.dram_tensor("nrm_i", [2, 4], F32, kind="ExternalInput").ap()
    l0dm_i = nc.dram_tensor("l0dm_i", [4, 128, TT], BF16, kind="ExternalInput").ap()
    sel2_i = nc.dram_tensor("sel2_i", [2, 2], F32, kind="ExternalInput").ap()
    attn_o = nc.dram_tensor("attn_o", [2, 128, S], BF16, kind="ExternalOutput").ap()

    with contextlib.ExitStack() as stack:
        bufs = {
            "ktT": sbuf(stack, nc, "ktT", [128, 2, S], BF16),
            "qtT": sbuf(stack, nc, "qtT", [128, 2, S], BF16),
            "vkm": sbuf(stack, nc, "vkm", [128, S // 128, HPC * 65], BF16),
            "nrm": sbuf(stack, nc, "nrm", [2, 4], F32),
            "attnT": sbuf(stack, nc, "attnT", [128, 2, S], BF16),
            "l0dm": sbuf(stack, nc, "l0dm", [128, 4, TT], BF16),
            "ones32": sbuf(stack, nc, "ones32", [1, 128], F32),
            "sel2": sbuf(stack, nc, "sel2", [2, 2], F32),
            "negbg": sbuf(stack, nc, "negbg", [128, 4], F32),
            "p2_rows": sbuf(stack, nc, "p2_rows", [1, 16], F32),
            "p2_den0": sbuf(stack, nc, "p2_den0", [1, TT], F32),
            "p2_den1": sbuf(stack, nc, "p2_den1", [1, TT], F32),
            "p2_bcs0": sbuf(stack, nc, "p2_bcs0", [64, TT], F32),
            "p2_bcs1": sbuf(stack, nc, "p2_bcs1", [64, TT], F32),
            "p2_p0": sbuf(stack, nc, "p2_p0", [128, TT], BF16),
            "p2_p1": sbuf(stack, nc, "p2_p1", [128, TT], BF16),
            "ps_a": psum(stack, nc, "ps_a", [128, TT]),
            "ps_b": psum(stack, nc, "ps_b", [128, TT]),
            "ps_e": psum(stack, nc, "ps_e", [65, TT]),
            "ps_f": psum(stack, nc, "ps_f", [65, TT]),
            "ps_g": psum(stack, nc, "ps_g", [64, TT]),
            "ps_n": psum(stack, nc, "ps_n", [2, TT]),
        }
        ctx = Ctx(nc, stack)
        s_ld = ctx.sem("p2_ld")

        @ctx.on("sync")
        def _(q):
            dm3 = bufs["l0dm"]
            q.dma_start(out=bufs["ktT"][:],
                        in_=kt_i.rearrange("c p t -> p c t")).then_inc(s_ld.h, 16)
            q.dma_start(out=bufs["qtT"][:],
                        in_=qt_i.rearrange("c p t -> p c t")).then_inc(s_ld.h, 16)
            q.dma_start(out=bufs["vkm"][:], in_=vkm_i[:]).then_inc(s_ld.h, 16)
            q.dma_start(out=bufs["nrm"][:], in_=nrm_i[:]).then_inc(s_ld.h, 16)
            q.dma_start(out=dm3[:], in_=l0dm_i.rearrange("r p t -> p r t")).then_inc(s_ld.h, 16)
            q.dma_start(out=bufs["sel2"][:], in_=sel2_i[:]).then_inc(s_ld.h, 16)

        @ctx.on("vector")
        def _(v):
            v.wait_ge(s_ld.h, 16 * 6)
            v.memset(bufs["ones32"][:], 1.0).then_inc(ctx.sem("p2_init").h, 1)

        @ctx.on("tensor")
        def _(t):
            t.wait_ge(ctx.sem("p2_init").h, 1)

        @ctx.on("scalar")
        def _(s):
            s.wait_ge(ctx.sem("p2_init").h, 1)
        ctx.sem("p2_init").inc()

        emit_p2(ctx, bufs, {})

        s_no = ctx.sem("p2_no")
        n_no = s_no.n
        s_st = ctx.sem("p2_st")

        @ctx.on("gpsimd")
        def _(g):
            g.wait_ge(s_no.h, n_no)
            g.dma_start(out=attn_o.rearrange("c p t -> p c t"),
                        in_=bufs["attnT"][:]).then_inc(s_st.h, 16)
            g.wait_ge(s_st.h, 16)

        with nc.Block() as block:
            ctx.finalize(block)
    return nc


def emit_proj_ln(ctx, tag, n_tiles, lhsW, rhs, res, out_f32, out_bf,
                 bufs, bias_col=None, rhs_load=None, pre_wait=None,
                 n_oc=8, n_dc=8, proj=True, y_src=None, y_ready=None,
                 res_sbuf=None, sync_gate=None):
    nc = ctx.nc
    ones1 = bufs["ones1_bf"]
    ones32 = bufs["ones32"]
    xt2 = bufs[tag + "_x2"]
    xb2 = bufs[tag + "_xb"]
    sqb = bufs[tag + "_sq"]
    rowm = bufs[tag + "_rowm"]
    rowq = bufs[tag + "_rowq"]
    rowv = bufs[tag + "_rowv"]
    rowsd = bufs[tag + "_rowsd"]
    rowr = bufs[tag + "_rowr"]
    rowt = bufs[tag + "_rowt"]
    if proj:
        ps_p = [bufs["ps_a"], bufs["ps_b"]]
    rsf = None if res_sbuf is not None else bufs[tag + "_rs"]
    ps_m, ps_q = bufs["ps_m"], bufs["ps_q"]
    ps_bm, ps_br = bufs["ps_bm"], bufs["ps_br"]

    s_in = [ctx.sem(tag + "_in0"), ctx.sem(tag + "_in1")]
    s_rs = [ctx.sem(tag + "_rs0"), ctx.sem(tag + "_rs1")]
    s_pm = ctx.sem(tag + "_pm")
    s_ye = ctx.sem(tag + "_ye")
    s_x2 = ctx.sem(tag + "_x2")
    s_cb = ctx.sem(tag + "_cb")
    s_sq = ctx.sem(tag + "_sq")
    s_mq = ctx.sem(tag + "_mq")
    s_st = ctx.sem(tag + "_st")
    s_rw = ctx.sem(tag + "_rw")
    s_bc = ctx.sem(tag + "_bc")
    s_ap = ctx.sem(tag + "_ap")
    s_vf = ctx.sem("vfence")
    s_so = [ctx.sem(tag + "_so0"), ctx.sem(tag + "_so1")]

    in_after, rs_after = {}, {}
    pm_after, ye_after, x2_after, cb_after = {}, {}, {}, {}
    sq_after, mq_after, st_after = {}, {}, {}
    rw_after, bc_after, ap_after, ap2_after, so_after = {}, {}, {}, {}, {}
    for nt in range(n_tiles):
        if rhs_load is not None:
            in_after[nt] = s_in[nt % 2].inc(16 * rhs_load[1])
        if rsf is not None:
            rs_after[nt] = s_rs[nt % 2].inc(16)
        for oc in range(n_oc):
            if proj:
                pm_after[nt, oc] = s_pm.inc()
                ye_after[nt, oc] = s_ye.inc()
            x2_after[nt, oc] = s_x2.inc()
        cb_after[nt] = s_cb.inc()
        for oc in range(n_oc):
            sq_after[nt, oc] = s_sq.inc()
            mq_after[nt, oc] = s_mq.inc()
        st_after[nt] = s_st.inc()
        rw_after[nt, 1] = s_rw.inc()
        rw_after[nt, 2] = s_rw.inc()
        rw_after[nt, 3] = s_rw.inc()
        bc_after[nt] = s_bc.inc(2)
        ap_after[nt] = s_ap.inc()
        ap2_after[nt] = s_ap.inc() if out_bf is not None else ap_after[nt]
        if out_f32 is not None:
            so_after[nt] = s_so[nt % 2].inc(16)

    @ctx.on("sync")
    def _(q):
        if sync_gate is not None:
            sync_gate(q)
        for nt in range(n_tiles):
            if rhs_load is not None:
                if nt >= 2:
                    q.wait_ge(s_pm.h, pm_after[nt - 2, n_oc - 1])
                rhs_load[0](q, nt, s_in[nt % 2].h)
            if rsf is not None:
                if nt >= 2:
                    q.wait_ge(s_x2.h, x2_after[nt - 2, n_oc - 1])
                q.dma_start(out=rsf[nt % 2][:],
                            in_=res(nt)).then_inc(s_rs[nt % 2].h, 16)

    @ctx.on("tensor")
    def _(t):
        if pre_wait is not None:
            pre_wait(t)
        for nt in range(n_tiles):
            if proj:
                if rhs_load is not None:
                    t.wait_ge(s_in[nt % 2].h, in_after[nt])
                for oc in range(n_oc):
                    idx = nt * n_oc + oc
                    if idx >= 2:
                        t.wait_ge(s_ye.h, idx - 1)
                    for dc in range(n_dc):
                        ins = t.matmul(ps_p[oc % 2][:], lhsW(dc, oc),
                                       rhs(nt, dc),
                                       start=(dc == 0), stop=(dc == n_dc - 1),
                                       skip_group_check=True)
                        if dc == n_dc - 1:
                            ins.then_inc(s_pm.h, 1)
            if nt >= 1:
                t.wait_ge(s_rw.h, rw_after[nt - 1, 1])
            t.wait_ge(s_cb.h, cb_after[nt])
            for oc in range(n_oc):
                ins = t.matmul(ps_m[:], ones1[:], xb2[nt % 2][:, oc, :],
                               start=(oc == 0), stop=(oc == n_oc - 1),
                               skip_group_check=True)
                if oc == n_oc - 1:
                    ins.then_inc(s_st.h, 1)
                t.wait_ge(s_sq.h, sq_after[nt, oc])
                t.matmul(ps_q[:], ones1[:], sqb[(nt * n_oc + oc) % 2][:],
                         start=(oc == 0), stop=(oc == n_oc - 1),
                         skip_group_check=True).then_inc(s_mq.h, 1)
            t.wait_ge(s_rw.h, rw_after[nt, 3])
            if nt >= 1:
                t.wait_ge(s_ap.h, ap_after[nt - 1])
            t.matmul(ps_bm[:], ones32[0:1, :], rowm[nt % 2][:],
                     start=True, stop=True)
            t.matmul(ps_br[:], ones32[0:1, :], rowr[nt % 2][:],
                     start=True, stop=True).then_inc(s_bc.h, 2)

    @ctx.on("scalar")
    def _(s):
        for nt in range(n_tiles):
            if proj:
                for oc in range(n_oc):
                    s.wait_ge(s_pm.h, pm_after[nt, oc])
                    if nt >= 2 and oc == 0:
                        if out_f32 is not None:
                            s.wait_ge(s_so[nt % 2].h, so_after[nt - 2])
                        s.wait_ge(s_ap.h, ap2_after[nt - 2])
                    if bias_col is not None:
                        s.activation(xt2[nt % 2][:, oc, :], ps_p[oc % 2][:],
                                     AF.Identity, bias=bias_col(oc)
                                     ).then_inc(s_ye.h, 1)
                    else:
                        s.activation(xt2[nt % 2][:, oc, :], ps_p[oc % 2][:],
                                     AF.Identity).then_inc(s_ye.h, 1)
            s.wait_ge(s_cb.h, cb_after[nt])
            for oc in range(n_oc):
                idx = nt * n_oc + oc
                if idx >= 2:
                    p_nt, p_oc = (idx - 2) // n_oc, (idx - 2) % n_oc
                    s.wait_ge(s_mq.h, mq_after[p_nt, p_oc])
                s.activation(sqb[idx % 2][:], xb2[nt % 2][:, oc, :],
                             AF.Square).then_inc(s_sq.h, 1)
            s.wait_ge(s_rw.h, rw_after[nt, 1])
            if nt >= 2:
                s.wait_ge(s_rw.h, rw_after[nt - 2, 3])
            s.activation(rowsd[nt % 2][:], rowv[nt % 2][:],
                         AF.Sqrt, bias=bufs["eps_col"][0:1, 0:1]
                         ).then_inc(s_rw.h, 1)

    @ctx.on("vector")
    def _(v):
        for nt in range(n_tiles):
            for oc in range(n_oc):
                if proj:
                    v.wait_ge(s_ye.h, ye_after[nt, oc])
                    ysrc = xt2[nt % 2][:, oc, :]
                else:
                    v.wait_ge(y_ready[0].h, y_ready[1][nt, oc])
                    ysrc = y_src(nt, oc)
                if oc == 0:
                    if rsf is not None:
                        v.wait_ge(s_rs[nt % 2].h, rs_after[nt])
                    if not proj and nt >= 2:
                        if out_f32 is not None:
                            v.wait_ge(s_so[nt % 2].h, so_after[nt - 2])
                        v.wait_ge(s_ap.h, ap2_after[nt - 2])
                rr = (rsf[nt % 2][:, oc, :] if res_sbuf is None
                      else res_sbuf(nt, oc))
                v.tensor_add(xt2[nt % 2][:, oc, :], ysrc, rr
                             ).then_inc(s_x2.h, 1)
            if nt >= 2:
                v.wait_ge(s_st.h, st_after[nt - 2])
                v.wait_ge(s_sq.h, sq_after[nt - 2, n_oc - 1])
            v.tensor_copy(xb2[nt % 2][:], xt2[nt % 2][:]).then_inc(s_cb.h, 1)
            v.wait_ge(s_st.h, st_after[nt])
            v.wait_ge(s_mq.h, mq_after[nt, n_oc - 1])
            if nt >= 2:
                v.wait_ge(s_bc.h, bc_after[nt - 2])
            v.tensor_scalar_mul(rowm[nt % 2][:], ps_m[0:1, :], 1.0 / D)
            v.tensor_scalar_mul(rowq[nt % 2][:], ps_q[0:1, :],
                                1.0 / D).then_inc(s_vf.h, 1)
            v.wait_ge(s_vf.h, s_vf.inc())
            v.tensor_mul(rowt[:], rowm[nt % 2][:], rowm[nt % 2][:])
            v.tensor_sub(rowv[nt % 2][:], rowq[nt % 2][:],
                         rowt[:]).then_inc(s_rw.h, 1)
            v.wait_ge(s_rw.h, rw_after[nt, 2])
            v.reciprocal(rowr[nt % 2][:], rowsd[nt % 2][:]).then_inc(s_rw.h, 1)
            v.wait_ge(s_bc.h, bc_after[nt])
            for oc in range(n_oc):
                v.tensor_sub(xt2[nt % 2][:, oc, :], xt2[nt % 2][:, oc, :],
                             ps_bm[:])
                ins = v.tensor_mul(xt2[nt % 2][:, oc, :],
                                   xt2[nt % 2][:, oc, :], ps_br[:])
                if oc == n_oc - 1:
                    ins.then_inc(s_ap.h, 1)
            if out_bf is not None:
                v.tensor_copy(out_bf(nt), xt2[nt % 2][:]).then_inc(s_ap.h, 1)

    if out_f32 is not None:
        @ctx.on("gpsimd")
        def _(g):
            for nt in range(n_tiles):
                g.wait_ge(s_ap.h, ap_after[nt])
                g.dma_start(out=out_f32(nt),
                            in_=xt2[nt % 2][:]).then_inc(s_so[nt % 2].h, 16)

    return dict(s_ap=s_ap, ap_after=ap_after, s_so0=s_so[0], s_so1=s_so[1],
                so_after=so_after, s_x2=s_x2, x2_after=x2_after)


def emit_p3m(ctx, bufs, ap):
    nc = ctx.nc
    wr_h, wi_h = ap["wr"], ap["wi"]
    cb_h, sb_h = ap["cb"], ap["sb"]
    c2_h, ns2_h = ap["c2b"], ap["ns2b"]
    wrb, wib, nwrb = bufs["m_wrb"], bufs["m_wib"], bufs["m_nwrb"]
    cbs, sbs = bufs["m_cb"], bufs["m_sb"]
    c2s, ns2s = bufs["m_c2"], bufs["m_ns2"]
    AT, BT = bufs["m_AT"], bufs["m_BT"]
    M = bufs["m_M"]
    wld = bufs["m_wld"]
    ps = [bufs["ps_a"], bufs["ps_b"]]

    s_wl = [ctx.sem("m_wl0"), ctx.sem("m_wl1")]
    s_wc = ctx.sem("m_wc")
    s_cl = ctx.sem("m_cl")
    s_m1 = ctx.sem("m_m1")
    s_e1 = ctx.sem("m_e1")
    s_m2 = ctx.sem("m_m2")
    s_e2 = ctx.sem("m_e2")

    wl_after = {}
    wc_after = {}
    for i in range(16):
        wl_after[i] = s_wl[i % 2].inc(16)
    wc_seq = {}
    tot = 0
    for i in range(16):
        tot += 2 if i < 8 else 1
        wc_seq[i] = tot
    s_wc.inc(24)
    cl_after = s_cl.inc(16 * 4)
    m1_after, e1_after = {}, {}
    g = 0
    for ab in range(2):
        for oc in range(8):
            for n2 in range(2):
                m1_after[ab, oc, n2] = s_m1.inc()
                e1_after[ab, oc, n2] = s_e1.inc()
                g += 1
    m2_after, e2_after = {}, {}
    for mc in range(8):
        for n2 in range(2):
            m2_after[mc, n2] = s_m2.inc()
            e2_after[mc, n2] = s_e2.inc()

    @ctx.on("sync")
    def _(q):
        q.dma_start(out=cbs[:], in_=cb_h.rearrange("(c p) n -> p c n", p=128)
                    ).then_inc(s_cl.h, 16)
        q.dma_start(out=sbs[:], in_=sb_h.rearrange("(c p) n -> p c n", p=128)
                    ).then_inc(s_cl.h, 16)
        q.dma_start(out=c2s[:], in_=c2_h.rearrange("(c p) n -> p c n", p=128)
                    ).then_inc(s_cl.h, 16)
        q.dma_start(out=ns2s[:], in_=ns2_h.rearrange("(c p) n -> p c n", p=128)
                    ).then_inc(s_cl.h, 16)
        for i in range(16):
            src = wr_h if i < 8 else wi_h
            dc = i % 8
            if i >= 2:
                q.wait_ge(s_wc.h, wc_seq[i - 2])
            q.dma_start(out=wld[i % 2][:],
                        in_=src[dc * 128:(dc + 1) * 128, :]
                        ).then_inc(s_wl[i % 2].h, 16)

    @ctx.on("vector")
    def _(v):
        for i in range(16):
            dc = i % 8
            v.wait_ge(s_wl[i % 2].h, wl_after[i])
            if i < 8:
                v.tensor_copy(wrb[:, dc, :], wld[i % 2][:]).then_inc(s_wc.h, 1)
                v.tensor_scalar_mul(nwrb[:, dc, :], wld[i % 2][:], -1.0
                                    ).then_inc(s_wc.h, 1)
            else:
                v.tensor_copy(wib[:, dc, :], wld[i % 2][:]).then_inc(s_wc.h, 1)
        for ab in range(2):
            for oc in range(8):
                for n2 in range(2):
                    v.wait_ge(s_m1.h, m1_after[ab, oc, n2])
                    dst = AT if ab == 0 else BT
                    gg = (ab * 16 + oc * 2 + n2)
                    v.tensor_copy(dst[:, oc, n2 * 512:(n2 + 1) * 512],
                                  ps[gg % 2][:]).then_inc(s_e1.h, 1)
        for mc in range(8):
            for n2 in range(2):
                v.wait_ge(s_m2.h, m2_after[mc, n2])
                gg = mc * 2 + n2
                v.tensor_copy(M[:, mc, n2 * 512:(n2 + 1) * 512],
                              ps[gg % 2][:]).then_inc(s_e2.h, 1)

    @ctx.on("tensor")
    def _(t):
        t.wait_ge(s_cl.h, cl_after)
        t.wait_ge(s_wc.h, 24)
        for ab in range(2):
            for oc in range(8):
                for n2 in range(2):
                    gg = ab * 16 + oc * 2 + n2
                    if gg >= 2:
                        t.wait_ge(s_e1.h, gg - 1)
                    lhs1 = wrb if ab == 0 else wib
                    lhs2 = wib if ab == 0 else nwrb
                    for dc in range(8):
                        t.matmul(ps[gg % 2][:], lhs1[:, dc, oc * 128:(oc + 1) * 128],
                                 cbs[:, dc, n2 * 512:(n2 + 1) * 512],
                                 start=(dc == 0), stop=False,
                                 skip_group_check=True)
                    for dc in range(8):
                        ins = t.matmul(ps[gg % 2][:],
                                       lhs2[:, dc, oc * 128:(oc + 1) * 128],
                                       sbs[:, dc, n2 * 512:(n2 + 1) * 512],
                                       start=False, stop=(dc == 7),
                                       skip_group_check=True)
                        if dc == 7:
                            ins.then_inc(s_m1.h, 1)
        t.wait_ge(s_e1.h, 32)
        for mc in range(8):
            for n2 in range(2):
                gg = mc * 2 + n2
                if gg >= 2:
                    t.wait_ge(s_e2.h, gg - 1)
                for oc in range(8):
                    t.matmul(ps[gg % 2][:], AT[:, oc, mc * 128:(mc + 1) * 128],
                             c2s[:, oc, n2 * 512:(n2 + 1) * 512],
                             start=(oc == 0), stop=False, skip_group_check=True)
                for oc in range(8):
                    ins = t.matmul(ps[gg % 2][:],
                                   BT[:, oc, mc * 128:(mc + 1) * 128],
                                   ns2s[:, oc, n2 * 512:(n2 + 1) * 512],
                                   start=False, stop=(oc == 7),
                                   skip_group_check=True)
                    if oc == 7:
                        ins.then_inc(s_m2.h, 1)

    return dict(s_e2=s_e2, n_e2=s_e2.n)


def ln_bufs(stack, nc, tag, n_oc=8):
    b = {}
    for nm, shape, dt in [
        ("_x2", [128, n_oc, TT], F32),
        ("_xb", [128, n_oc, TT], BF16), ("_rs", [128, n_oc, TT], F32),
        ("_sq", [128, TT], BF16),
        ("_rowm", [1, TT], F32), ("_rowq", [1, TT], F32),
        ("_rowv", [1, TT], F32), ("_rowsd", [1, TT], F32),
        ("_rowr", [1, TT], F32),
    ]:
        b[tag + nm] = [sbuf(stack, nc, f"{tag}{nm}{i}", shape, dt)
                       for i in range(2)]
    b[tag + "_rowt"] = sbuf(stack, nc, tag + "_rowt", [1, TT], F32)
    return b


def shared_ln_consts(stack, nc, bufs):
    bufs["ones1_bf"] = sbuf(stack, nc, "ones1_bf", [128, 1], BF16)
    bufs["ones32"] = sbuf(stack, nc, "ones32", [1, 128], F32)
    bufs["eps_col"] = sbuf(stack, nc, "eps_col", [1, 1], F32)
    bufs["ps_m"] = psum(stack, nc, "ps_m", [1, TT])
    bufs["ps_q"] = psum(stack, nc, "ps_q", [1, TT])
    bufs["ps_bm"] = psum(stack, nc, "ps_bm", [128, TT])
    bufs["ps_br"] = psum(stack, nc, "ps_br", [128, TT])


def emit_init_ln_consts(ctx, bufs):
    s_i = ctx.sem("ln_init")

    @ctx.on("vector")
    def _(v):
        v.memset(bufs["ones1_bf"][:], 1.0)
        v.memset(bufs["eps_col"][:], 1e-5)
        v.memset(bufs["ones32"][:], 1.0).then_inc(s_i.h, 1)
    s_i.inc()

    @ctx.on("tensor")
    def _(t):
        t.wait_ge(s_i.h, 1)

    @ctx.on("scalar")
    def _(s):
        s.wait_ge(s_i.h, 1)


def emit_weight_load(ctx, tag, src_ap, wbf, wld, n_chunks, width, prev=None):
    nb = len(wld)
    s_l = [ctx.sem(tag + f"_wl{i}") for i in range(nb)]
    s_c = ctx.sem(tag + "_wc")
    l_after = [s_l[i % nb].inc(16) for i in range(n_chunks)]
    c_after = [s_c.inc() for _ in range(n_chunks)]

    @ctx.on("sync")
    def _(q):
        for dc in range(n_chunks):
            if dc >= nb:
                q.wait_ge(s_c.h, c_after[dc - nb])
            elif prev is not None:
                q.wait_ge(prev[0].h, prev[1])
            q.dma_start(out=wld[dc % nb][:, 0:width],
                        in_=src_ap[dc * 128:(dc + 1) * 128, :]
                        ).then_inc(s_l[dc % nb].h, 16)

    @ctx.on("vector")
    def _(v):
        for dc in range(n_chunks):
            v.wait_ge(s_l[dc % nb].h, l_after[dc])
            v.tensor_copy(wbf[:, dc, :], wld[dc % nb][:, 0:width]).then_inc(s_c.h, 1)

    return s_c, n_chunks


def build_p3a(core):
    import contextlib
    nc = bass.Bass("TRN2", target_bir_lowering=False)
    ags = nc.dram_tensor("ags", [D, NTOK], BF16, kind="ExternalInput").ap()
    xs = nc.dram_tensor("xs", [D, NTOK], F32, kind="ExternalInput").ap()
    woutT = nc.dram_tensor("woutT", [D, D], F32, kind="ExternalInput").ap()
    st1f_o = nc.dram_tensor("st1f_o", [128, 8, NTOK], F32, kind="ExternalOutput").ap()
    st1b_o = nc.dram_tensor("st1b_o", [128, 8, NTOK], BF16, kind="ExternalOutput").ap()

    with contextlib.ExitStack() as stack:
        bufs = {
            "ps_a": psum(stack, nc, "ps_a", [128, TT]),
            "ps_b": psum(stack, nc, "ps_b", [128, TT]),
            "woutb": sbuf(stack, nc, "woutb", [128, 8, D], BF16),
            "wld": [sbuf(stack, nc, f"wld{i}", [128, D], F32) for i in range(2)],
            "agt": [sbuf(stack, nc, f"agt{i}", [128, 8, TT], BF16) for i in range(2)],
            "st1b": sbuf(stack, nc, "st1b", [128, 8, NTOK], BF16),
        }
        shared_ln_consts(stack, nc, bufs)
        bufs.update(ln_bufs(stack, nc, "p3a"))
        ctx = Ctx(nc, stack)
        emit_init_ln_consts(ctx, bufs)
        s_wc, n_wc = emit_weight_load(ctx, "p3a", woutT, bufs["woutb"],
                                      bufs["wld"], 8, D)

        agr = ags.rearrange("(c p) n -> p c n", p=128)
        xsr = xs.rearrange("(c p) n -> p c n", p=128)

        def rhs_load_fn(q, nt, semh):
            q.dma_start(out=bufs["agt"][nt % 2][:],
                        in_=agr[:, :, nt * TT:(nt + 1) * TT]).then_inc(semh, 16)

        h = emit_proj_ln(
            ctx, "p3a", NT_N,
            lhsW=lambda dc, oc: bufs["woutb"][:, dc, oc * 128:(oc + 1) * 128],
            rhs=lambda nt, dc: bufs["agt"][nt % 2][:, dc, :],
            rhs_load=(rhs_load_fn, 1),
            res=lambda nt: xsr[:, :, nt * TT:(nt + 1) * TT],
            out_f32=lambda nt: st1f_o[:, :, nt * TT:(nt + 1) * TT],
            out_bf=lambda nt: bufs["st1b"][:, :, nt * TT:(nt + 1) * TT],
            bufs=bufs,
            pre_wait=lambda t: t.wait_ge(s_wc.h, n_wc),
        )
        s_st = ctx.sem("p3a_store")
        s_ap, n_ap = h["s_ap"], h["s_ap"].n
        so0, so1 = h["s_so0"], h["s_so1"]
        n_so0, n_so1 = so0.n, so1.n

        @ctx.on("gpsimd")
        def _(g):
            g.wait_ge(s_ap.h, n_ap)
            g.dma_start(out=st1b_o[:], in_=bufs["st1b"][:]).then_inc(s_st.h, 16)
            g.wait_ge(s_st.h, 16)
            g.wait_ge(so0.h, n_so0)
            g.wait_ge(so1.h, n_so1)

        with nc.Block() as block:
            ctx.finalize(block)
    return nc


def build_p3m(core):
    import contextlib
    nc = bass.Bass("TRN2", target_bir_lowering=False)
    ap = {
        "wr": nc.dram_tensor("wr", [D, D], F32, kind="ExternalInput").ap(),
        "wi": nc.dram_tensor("wi", [D, D], F32, kind="ExternalInput").ap(),
        "cb": nc.dram_tensor("cb", [D, D], BF16, kind="ExternalInput").ap(),
        "sb": nc.dram_tensor("sb", [D, D], BF16, kind="ExternalInput").ap(),
        "c2b": nc.dram_tensor("c2b", [D, D], BF16, kind="ExternalInput").ap(),
        "ns2b": nc.dram_tensor("ns2b", [D, D], BF16, kind="ExternalInput").ap(),
    }
    m_o = nc.dram_tensor("m_o", [128, 8, D], BF16, kind="ExternalOutput").ap()

    with contextlib.ExitStack() as stack:
        bufs = {
            "ps_a": psum(stack, nc, "ps_a", [128, TT]),
            "ps_b": psum(stack, nc, "ps_b", [128, TT]),
        }
        for nm in ["m_wrb", "m_wib", "m_nwrb", "m_cb", "m_sb", "m_c2",
                   "m_ns2", "m_AT", "m_BT", "m_M"]:
            bufs[nm] = sbuf(stack, nc, nm, [128, 8, D], BF16)
        bufs["m_wld"] = [sbuf(stack, nc, f"m_wld{i}", [128, D], F32)
                         for i in range(2)]
        ctx = Ctx(nc, stack)
        h = emit_p3m(ctx, bufs, ap)
        s_e2, n_e2 = h["s_e2"], h["n_e2"]
        s_st = ctx.sem("p3m_store")

        @ctx.on("gpsimd")
        def _(g):
            g.wait_ge(s_e2.h, n_e2)
            g.dma_start(out=m_o[:], in_=bufs["m_M"][:]).then_inc(s_st.h, 16)
            g.wait_ge(s_st.h, 16)

        with nc.Block() as block:
            ctx.finalize(block)
    return nc


def build_p3c(core):
    import contextlib
    nc = bass.Bass("TRN2", target_bir_lowering=False)
    m_i = nc.dram_tensor("m_i", [128, 8, D], BF16, kind="ExternalInput").ap()
    st1f_i = nc.dram_tensor("st1f_i", [128, 8, NTOK], F32, kind="ExternalInput").ap()
    st1b_i = nc.dram_tensor("st1b_i", [128, 8, NTOK], BF16, kind="ExternalInput").ap()
    l0b_i = nc.dram_tensor("l0b_i", [128, 8], F32, kind="ExternalInput").ap()
    st2f_o = nc.dram_tensor("st2f_o", [128, 8, NTOK], F32, kind="ExternalOutput").ap()
    st2b_o = nc.dram_tensor("st2b_o", [128, 8, NTOK], BF16, kind="ExternalOutput").ap()

    with contextlib.ExitStack() as stack:
        bufs = {
            "ps_a": psum(stack, nc, "ps_a", [128, TT]),
            "ps_b": psum(stack, nc, "ps_b", [128, TT]),
            "m_M": sbuf(stack, nc, "m_M", [128, 8, D], BF16),
            "st1b": sbuf(stack, nc, "st1b", [128, 8, NTOK], BF16),
            "st2b": sbuf(stack, nc, "st2b", [128, 8, NTOK], BF16),
            "l0b": sbuf(stack, nc, "l0b", [128, 8], F32),
        }
        shared_ln_consts(stack, nc, bufs)
        bufs.update(ln_bufs(stack, nc, "p3c"))
        ctx = Ctx(nc, stack)
        emit_init_ln_consts(ctx, bufs)
        s_ld = ctx.sem("p3c_ld")

        @ctx.on("sync")
        def _(q):
            q.dma_start(out=bufs["m_M"][:], in_=m_i[:]).then_inc(s_ld.h, 16)
            q.dma_start(out=bufs["st1b"][:], in_=st1b_i[:]).then_inc(s_ld.h, 16)
            q.dma_start(out=bufs["l0b"][:], in_=l0b_i[:]).then_inc(s_ld.h, 16)

        h = emit_proj_ln(
            ctx, "p3c", NT_N,
            lhsW=lambda dc, oc: bufs["m_M"][:, dc, oc * 128:(oc + 1) * 128],
            rhs=lambda nt, dc: bufs["st1b"][:, dc, nt * TT:(nt + 1) * TT],
            rhs_load=None,
            res=lambda nt: st1f_i[:, :, nt * TT:(nt + 1) * TT],
            out_f32=lambda nt: st2f_o[:, :, nt * TT:(nt + 1) * TT],
            out_bf=lambda nt: bufs["st2b"][:, :, nt * TT:(nt + 1) * TT],
            bufs=bufs,
            bias_col=lambda oc: bufs["l0b"][:, oc:oc + 1],
            pre_wait=lambda t: t.wait_ge(s_ld.h, 48),
        )
        s_st = ctx.sem("p3c_store")
        s_ap, n_ap = h["s_ap"], h["s_ap"].n
        so0, so1 = h["s_so0"], h["s_so1"]
        n_so0, n_so1 = so0.n, so1.n

        @ctx.on("gpsimd")
        def _(g):
            g.wait_ge(s_ap.h, n_ap)
            g.dma_start(out=st2b_o[:], in_=bufs["st2b"][:]).then_inc(s_st.h, 16)
            g.wait_ge(s_st.h, 16)
            g.wait_ge(so0.h, n_so0)
            g.wait_ge(so1.h, n_so1)

        with nc.Block() as block:
            ctx.finalize(block)
    return nc


def aview(arena, byte_off, shape, dtype):
    es = mybir.dt.size(dtype)
    n = 1
    for s in shape[1:]:
        n *= s
    ap = arena[0:shape[0], byte_off:byte_off + n * es].bitcast(dtype)
    if len(shape) == 3:
        ap = ap.rearrange("p (a b) -> p a b", a=shape[1], b=shape[2])
    return ap


def emit_p4(ctx, bufs, ap):
    nc = ctx.nc
    st2b = bufs["st2b"]
    wqb, wkb, wvb = bufs["wq1b"], bufs["wk1b"], bufs["wv1b"]
    q1T = bufs["q1T"]
    k1T = bufs["k1T"]
    v1km = bufs["v1km"]
    bqc, bkc = bufs["bq_pc"], bufs["bk_pc"]
    bvrow = bufs["bv_row"]
    onesr = bufs["ones_r"]
    ones2 = bufs["ones2"]
    ones32 = bufs["ones32"]
    sel2 = bufs["sel2"]
    sqb = bufs["p4_sqn"]
    nq_all = bufs["p4_nqall"]
    nk_all = bufs["p4_nkall"]
    nrm1 = bufs["p4_nrm"]
    rows = bufs["p4_rows"]
    negbg = bufs["p4_negbg"]
    mask = bufs["p4_m0"]
    P = [bufs["p4_p0"], bufs["p4_p1"]]
    den = [bufs["p4_den0"], bufs["p4_den1"]]
    bcs = [bufs["p4_bcs0"], bufs["p4_bcs1"]]
    ps_p = [bufs["ps_a"], bufs["ps_b"]]
    ps_av = [bufs["ps_e"], bufs["ps_f"]]
    ps_bc = bufs["ps_g"]
    ps_n = bufs["ps_n"]
    ap_l1m = ap["l1m"]

    s_ld = ctx.sem("p4_ld")
    s_pm = ctx.sem("p4_pm")
    s_ev = ctx.sem("p4_ev")
    s_sq = ctx.sem("p4_sq")
    s_nm = ctx.sem("p4_nm")
    s_nr = ctx.sem("p4_nr")
    s_bg = ctx.sem("p4_bg")
    s_vf = ctx.sem("vfence")
    s_mk = ctx.sem("p4_mk")
    s_mm = ctx.sem("p4_mm")
    s_exp = ctx.sem("p4_exp")
    s_pr = ctx.sem("p4_pr")
    s_avm = ctx.sem("p4_avm")
    s_rc = ctx.sem("p4_rc")
    s_bc = ctx.sem("p4_bc")
    s_bce = ctx.sem("p4_bce")
    s_no = ctx.sem("p4_no")

    jobs = ([("q", nt, oc) for nt in range(NT_O) for oc in range(8)]
            + [("k", nt, oc) for nt in range(NT_N) for oc in range(2)]
            + [("v", t, 0) for t in range(NTOK // 128)])
    jidx = {job: j for j, job in enumerate(jobs)}
    norm_jobs = [job for job in jobs if job[0] in ("q", "k")]
    nsq = {job: i for i, job in enumerate(norm_jobs)}

    pm_after, ev_after = {}, {}
    sq_after, nm_after, nr_after = {}, {}, {}
    for job in jobs:
        pm_after[job] = s_pm.inc()
        ev_after[job] = s_ev.inc()
        if job[0] in ("q", "k"):
            sq_after[job] = s_sq.inc()
            nm_after[job] = s_nm.inc()
            nr_after[job] = s_nr.inc()
    last_job = jobs[-1]

    hqs = [(qt, h) for qt in range(NT_O) for h in range(NH)]
    hq_i = {hq: i for i, hq in enumerate(hqs)}
    mk_after = {qt: s_mk.inc(16) for qt in range(NT_O)}
    mm_after, exp_after, pr_after, avm_after = {}, {}, {}, {}
    rc_after, bc_after, bce_after, no_after = {}, {}, {}, {}
    gidx = {}
    g = 0
    for hq in hqs:
        for r in range(8):
            gidx[hq, r] = g
            mm_after[g] = s_mm.inc()
            exp_after[g] = s_exp.inc()
            pr_after[g] = s_pr.inc()
            avm_after[g] = s_avm.inc()
            g += 1
    for hq in hqs:
        rc_after[hq] = s_rc.inc()
        bc_after[hq] = s_bc.inc()
        bce_after[hq] = s_bce.inc()
        no_after[hq] = s_no.inc()

    @ctx.on("sync")
    def _(q):
        q.dma_start(out=bqc[:], in_=ap["bq_pc"]).then_inc(s_ld.h, 16)
        q.dma_start(out=bkc[:], in_=ap["bk_pc"]).then_inc(s_ld.h, 16)
        q.dma_start(out=sel2[:], in_=ap["sel2"]).then_inc(s_ld.h, 16)
        for qt in range(NT_O):
            if qt >= 1:
                q.wait_ge(s_pr.h, pr_after[gidx[(qt - 1, NH - 1), 7]])
            else:
                q.wait_ge(s_pm.h, pm_after[last_job])
            q.dma_start(out=mask[:],
                        in_=ap_l1m[qt].rearrange("r p t -> p r t")
                        ).then_inc(s_mk.h, 16)

    @ctx.on("tensor")
    def _(t):
        t.wait_ge(s_ld.h, 16 * 3)
        for j, job in enumerate(jobs):
            if j >= 2:
                prev = jobs[j - 2]
                t.wait_ge(s_ev.h, ev_after[prev])
                if prev[0] in ("q", "k"):
                    t.wait_ge(s_sq.h, sq_after[prev])
            kind, a, b_ = job
            if kind == "q":
                for dc in range(8):
                    ins = t.matmul(ps_p[j % 2][:],
                                   wqb[:, dc, b_ * 128:(b_ + 1) * 128],
                                   st2b[:, dc, HALO + a * TT:HALO + (a + 1) * TT],
                                   start=(dc == 0), stop=(dc == 7),
                                   skip_group_check=True)
            elif kind == "k":
                for dc in range(8):
                    ins = t.matmul(ps_p[j % 2][:],
                                   wkb[:, dc, b_ * 128:(b_ + 1) * 128],
                                   st2b[:, dc, a * TT:(a + 1) * TT],
                                   start=(dc == 0), stop=(dc == 7),
                                   skip_group_check=True)
            else:
                for dc in range(8):
                    t.matmul(ps_p[j % 2][:, 0:256],
                             st2b[:, dc, a * 128:(a + 1) * 128],
                             wvb[:, dc, :],
                             start=(dc == 0), stop=False, skip_group_check=True)
                ins = t.matmul(ps_p[j % 2][:, 0:256], onesr[0:1, :],
                               bvrow[0:1, :], start=False, stop=True,
                               skip_group_check=True)
            ins.then_inc(s_pm.h, 1)
            if kind in ("q", "k"):
                t.wait_ge(s_sq.h, sq_after[job])
                i = nsq[job]
                if i >= 1:
                    t.wait_ge(s_nr.h, i)
                t.matmul(ps_n[0:2, :], ones2[:], sqb[i % 2][:],
                         start=True, stop=True).then_inc(s_nm.h, 1)
        t.wait_ge(s_bg.h, 1)
        t.matmul(ps_n[0:1, 0:10], sel2[:, 0:1], nrm1[:, :],
                 start=True, stop=True)
        t.matmul(ps_bc[0:1, 0:10], sel2[:, 1:2], nrm1[:, :],
                 start=True, stop=True).then_inc(s_bg.h, 2)
        t.wait_ge(s_bg.h, 7)
        t.matmul(ps_bc[0:64, 0:16], ones32[0:1, 0:64], rows[0:1, 36:52],
                 start=True, stop=True).then_inc(s_bg.h, 1)
        t.wait_ge(s_bg.h, 9)
        for hq in hqs:
            qt, h = hq
            i = hq_i[hq]
            gq = h // 4
            cq, pq = l1_head_slot(h)
            for r in range(8):
                gg = gidx[hq, r]
                relk = qt * TT + 128 * r
                if gg >= 2:
                    t.wait_ge(s_exp.h, exp_after[gg - 2])
                else:
                    t.wait_ge(s_ev.h, ev_after[last_job])
                    t.wait_ge(s_sq.h, sq_after[norm_jobs[-1]])
                t.matmul(ps_p[gg % 2][:],
                         k1T[pq:pq + 64, gq // 2, relk:relk + 128],
                         q1T[pq:pq + 64, cq, qt * TT:(qt + 1) * TT],
                         start=True, stop=True,
                         skip_group_check=True).then_inc(s_mm.h, 1)
                t.wait_ge(s_pr.h, pr_after[gg])
                if r == 0 and i >= 2:
                    t.wait_ge(s_no.h, no_after[hqs[i - 2]])
                kc = 4 * qt + r
                t.matmul(ps_av[i % 2][0:65, :],
                         v1km[:, kc, gq * 65:(gq + 1) * 65],
                         P[gg % 2][:], start=(r == 0), stop=(r == 7),
                         skip_group_check=True).then_inc(s_avm.h, 1)
            t.wait_ge(s_rc.h, rc_after[hq])
            if i >= 1:
                t.wait_ge(s_bce.h, bce_after[hqs[i - 1]])
            t.matmul(ps_bc[:, :], ones32[0:1, 0:64], den[i % 2][:],
                     start=True, stop=True).then_inc(s_bc.h, 1)

    @ctx.on("scalar")
    def _(s):
        s.wait_ge(s_ld.h, 16 * 3)
        for j, job in enumerate(jobs):
            kind, a, b_ = job
            s.wait_ge(s_pm.h, pm_after[job])
            if kind == "q":
                s.activation(q1T[:, b_, a * TT:(a + 1) * TT],
                             ps_p[j % 2][:], AF.Identity,
                             bias=bqc[:, b_:b_ + 1]).then_inc(s_ev.h, 1)
                i = nsq[job]
                if i >= 2:
                    s.wait_ge(s_nm.h, nm_after[norm_jobs[i - 2]])
                s.activation(sqb[i % 2][:], q1T[:, b_, a * TT:(a + 1) * TT],
                             AF.Square).then_inc(s_sq.h, 1)
            elif kind == "k":
                s.activation(k1T[:, b_, a * TT:(a + 1) * TT],
                             ps_p[j % 2][:], AF.Identity,
                             bias=bkc[:, b_:b_ + 1]).then_inc(s_ev.h, 1)
                i = nsq[job]
                if i >= 2:
                    s.wait_ge(s_nm.h, nm_after[norm_jobs[i - 2]])
                s.activation(sqb[i % 2][:], k1T[:, b_, a * TT:(a + 1) * TT],
                             AF.Square).then_inc(s_sq.h, 1)
            else:
                s.activation(
                    v1km[:, a, :].rearrange("p (h j) -> p h j", j=65)[:, :, 0:64],
                    ps_p[j % 2][:, 0:256].rearrange("p (h j) -> p h j", j=64),
                    AF.Identity).then_inc(s_ev.h, 1)
        s.wait_ge(s_bg.h, 5)
        s.activation(rows[0:1, 36:52], rows[0:1, 20:36], AF.Sqrt)
        s.mul(rows[0:1, 36:52], rows[0:1, 36:52], -0.125).then_inc(s_bg.h, 2)
        s.wait_ge(s_bg.h, 8)
        s.copy(negbg[0:64, :], ps_bc[0:64, 0:16])
        s.copy(negbg[64:128, :], ps_bc[0:64, 0:16]).then_inc(s_bg.h, 1)
        for hq in hqs:
            qt, h = hq
            i = hq_i[hq]
            for r in range(8):
                gg = gidx[hq, r]
                s.wait_ge(s_mm.h, mm_after[gg])
                if gg >= 2:
                    s.wait_ge(s_avm.h, avm_after[gg - 2])
                s.activation(P[gg % 2][:], ps_p[gg % 2][:], AF.Exp,
                             bias=negbg[:, h:h + 1],
                             scale=0.125).then_inc(s_exp.h, 1)
            s.wait_ge(s_bc.h, bc_after[hq])
            if i >= 2:
                s.wait_ge(s_no.h, no_after[hqs[i - 2]])
            s.copy(bcs[i % 2][:], ps_bc[0:64, :]).then_inc(s_bce.h, 1)

    @ctx.on("vector")
    def _(v):
        v.memset(onesr[:], 1.0)
        v.memset(ones2[:], 0.0)
        v.memset(ones2[0:64, 0:1], 1.0)
        v.memset(ones2[64:128, 1:2], 1.0)
        for hh in range(NG):
            v.memset(v1km[:, :, hh * 65 + 64], 1.0)
        for job in norm_jobs:
            kind, a, b_ = job
            v.wait_ge(s_nm.h, nm_after[job])
            dst = (nq_all[0:2, b_, a:a + 1] if kind == "q"
                   else nk_all[0:2, b_, a:a + 1])
            v.reduce_max(dst, ps_n[0:2, :],
                         axis=mybir.AxisListType.X).then_inc(s_nr.h, 1)
        v.reduce_max(nrm1[0:2, 0:8], nq_all[0:2, :, :],
                     axis=mybir.AxisListType.X)
        v.reduce_max(nrm1[0:2, 8:10], nk_all[0:2, :, :],
                     axis=mybir.AxisListType.X).then_inc(s_bg.h, 1)
        v.wait_ge(s_bg.h, 3)
        v.tensor_copy(rows[0:1, 0:10], ps_n[0:1, 0:10])
        v.tensor_copy(rows[0:1, 10:20], ps_bc[0:1, 0:10]).then_inc(s_bg.h, 1)
        v.wait_ge(s_bg.h, 4)
        for h in range(NH):
            cq, pq = l1_head_slot(h)
            qsrc = (rows[0:1, cq:cq + 1] if pq == 0
                    else rows[0:1, 10 + cq:11 + cq])
            ins = v.tensor_copy(rows[0:1, 20 + h:21 + h], qsrc)
            if h == NH - 1:
                ins.then_inc(s_vf.h, 1)
        v.wait_ge(s_vf.h, s_vf.inc())
        for h in range(NH):
            gq = h // 4
            ksrc = (rows[0:1, 8 + gq // 2:9 + gq // 2] if gq % 2 == 0
                    else rows[0:1, 18 + gq // 2:19 + gq // 2])
            ins = v.tensor_mul(rows[0:1, 20 + h:21 + h],
                               rows[0:1, 20 + h:21 + h], ksrc)
            if h == NH - 1:
                ins.then_inc(s_bg.h, 1)
        for hq in hqs:
            qt, h = hq
            i = hq_i[hq]
            cq, pq = l1_head_slot(h)
            for r in range(8):
                gg = gidx[hq, r]
                v.wait_ge(s_exp.h, exp_after[gg])
                if h == 0 and r == 0:
                    v.wait_ge(s_mk.h, mk_after[qt])
                v.tensor_mul(P[gg % 2][:], P[gg % 2][:],
                             mask[:, r, :]).then_inc(s_pr.h, 1)
            v.wait_ge(s_avm.h, avm_after[gidx[hq, 7]])
            if i >= 2:
                v.wait_ge(s_bc.h, bc_after[hqs[i - 2]])
            v.reciprocal(den[i % 2][:], ps_av[i % 2][64:65, :]).then_inc(s_rc.h, 1)
            v.wait_ge(s_bce.h, bce_after[hq])
            v.tensor_mul(q1T[pq:pq + 64, cq, qt * TT:(qt + 1) * TT],
                         ps_av[i % 2][0:64, :], bcs[i % 2][:]).then_inc(s_no.h, 1)

    return dict(s_no=s_no, n_no=s_no.n, no_after=no_after, hqs=hqs)


def build_p4(core):
    import contextlib
    nc = bass.Bass("TRN2", target_bir_lowering=False)
    st2f_i = nc.dram_tensor("st2f_i", [128, 8, NTOK], F32, kind="ExternalInput").ap()
    st2b_i = nc.dram_tensor("st2b_i", [128, 8, NTOK], BF16, kind="ExternalInput").ap()
    wq1 = nc.dram_tensor("wq1", [D, D], F32, kind="ExternalInput").ap()
    wk1 = nc.dram_tensor("wk1", [D, 256], F32, kind="ExternalInput").ap()
    wv1 = nc.dram_tensor("wv1", [D, 256], F32, kind="ExternalInput").ap()
    wo1 = nc.dram_tensor("wo1", [D, D], F32, kind="ExternalInput").ap()
    bq_i = nc.dram_tensor("bq_pc", [128, 8], F32, kind="ExternalInput").ap()
    bk_i = nc.dram_tensor("bk_pc", [128, 2], F32, kind="ExternalInput").ap()
    bv_i = nc.dram_tensor("bv_row", [1, 256], F32, kind="ExternalInput").ap()
    bo_i = nc.dram_tensor("bo_pc", [128, 8], F32, kind="ExternalInput").ap()
    l1m = nc.dram_tensor("l1m", [NT_O, 8, 128, TT], BF16, kind="ExternalInput").ap()
    sel2_i = nc.dram_tensor("sel2_i", [2, 2], F32, kind="ExternalInput").ap()
    st3f_o = nc.dram_tensor("st3f_o", [128, 8, OWN], F32, kind="ExternalOutput").ap()
    st3b_o = nc.dram_tensor("st3b_o", [128, 8, OWN], BF16, kind="ExternalOutput").ap()

    KB = 1024
    with contextlib.ExitStack() as stack:
        ov1 = sbuf(stack, nc, "ov1", [128, 80 * KB], mybir.dt.uint8)
        ov2 = sbuf(stack, nc, "ov2", [128, 72 * KB], mybir.dt.uint8)
        bufs = {
            "ps_a": psum(stack, nc, "ps_a", [128, TT]),
            "ps_b": psum(stack, nc, "ps_b", [128, TT]),
            "ps_e": psum(stack, nc, "ps_e", [128, TT]),
            "ps_f": psum(stack, nc, "ps_f", [128, TT]),
            "ps_g": psum(stack, nc, "ps_g", [64, TT]),
            "ps_n": psum(stack, nc, "ps_n", [2, TT]),
            "q1T": sbuf(stack, nc, "q1T", [128, 8, OWN], BF16),
            "k1T": sbuf(stack, nc, "k1T", [128, 2, NTOK], BF16),
            "v1km": sbuf(stack, nc, "v1km", [128, NTOK // 128, NG * 65], BF16),
            "p4_nqall": sbuf(stack, nc, "p4_nqall", [2, 8, NT_O], F32),
            "p4_nkall": sbuf(stack, nc, "p4_nkall", [2, 2, NT_N], F32),
            "p4_nrm": sbuf(stack, nc, "p4_nrm", [2, 10], F32),
            "p4_rows": sbuf(stack, nc, "p4_rows", [1, 64], F32),
            "p4_negbg": sbuf(stack, nc, "p4_negbg", [128, 16], F32),
            "bq_pc": sbuf(stack, nc, "bq_pc_s", [128, 8], F32),
            "bk_pc": sbuf(stack, nc, "bk_pc_s", [128, 2], F32),
            "bv_f32": sbuf(stack, nc, "bv_f32", [1, 256], F32),
            "bv_row": sbuf(stack, nc, "bv_row_s", [1, 256], BF16),
            "bo_pc": sbuf(stack, nc, "bo_pc_s", [128, 8], F32),
            "ones_r": sbuf(stack, nc, "ones_r", [1, 128], BF16),
            "ones2": sbuf(stack, nc, "ones2", [128, 2], BF16),
            "sel2": sbuf(stack, nc, "sel2", [2, 2], F32),
        }
        bufs["st2b"] = aview(ov1, 0, [128, 8, NTOK], BF16)
        bufs["wq1b"] = aview(ov1, 40 * KB, [128, 8, D], BF16)
        bufs["wk1b"] = aview(ov1, 56 * KB, [128, 8, 256], BF16)
        bufs["wv1b"] = aview(ov1, 60 * KB, [128, 8, 256], BF16)
        bufs["p4_m0"] = aview(ov1, 0, [128, 8, TT], BF16)
        bufs["p4_p0"] = aview(ov1, 8 * KB, [128, TT], BF16)
        bufs["p4_p1"] = aview(ov1, 9 * KB, [128, TT], BF16)
        bufs["p4_den0"] = aview(ov1, 10 * KB, [1, TT], F32)
        bufs["p4_den1"] = aview(ov1, 12 * KB, [1, TT], F32)
        bufs["p4_bcs0"] = aview(ov1, 14 * KB, [64, TT], F32)
        bufs["p4_bcs1"] = aview(ov1, 16 * KB, [64, TT], F32)
        bufs["p4c_x2"] = [aview(ov1, i * 16 * KB, [128, 8, TT], F32) for i in range(2)]
        bufs["p4c_rs"] = [aview(ov1, 32 * KB + i * 16 * KB, [128, 8, TT], F32)
                          for i in range(2)]
        bufs["wo1b"] = aview(ov1, 64 * KB, [128, 8, D], BF16)
        bufs["p4_sqn"] = [aview(ov2, i * KB, [128, TT], BF16) for i in range(2)]
        bufs["wld"] = [aview(ov2, 2 * KB, [128, D], F32)]
        bufs["p4c_xb"] = [aview(ov2, i * 8 * KB, [128, 8, TT], BF16) for i in range(2)]
        bufs["p4c_sq"] = [aview(ov2, 16 * KB + i * KB, [128, TT], BF16) for i in range(2)]
        off = 18 * KB
        for nm in ["p4c_rowm", "p4c_rowq", "p4c_rowsd", "p4c_rowr"]:
            bufs[nm] = [aview(ov2, off, [1, TT], F32),
                        aview(ov2, off + 2 * KB, [1, TT], F32)]
            off += 4 * KB
        bufs["p4c_rowv"] = bufs["p4c_rowq"]
        bufs["p4c_rowt"] = aview(ov2, off, [1, TT], F32)
        off += 2 * KB
        bufs["st3b"] = aview(ov2, off, [128, 8, OWN], BF16)
        bufs["ones1_bf"] = sbuf(stack, nc, "ones1_bf", [128, 1], BF16)
        bufs["ones32"] = sbuf(stack, nc, "ones32", [1, 128], F32)
        bufs["eps_col"] = sbuf(stack, nc, "eps_col", [1, 1], F32)
        bufs["ps_m"] = bufs["ps_n"][0:1, :]
        bufs["ps_q"] = bufs["ps_g"][0:1, :]
        bufs["ps_bm"] = bufs["ps_e"]
        bufs["ps_br"] = bufs["ps_f"]

        ctx = Ctx(nc, stack)
        emit_init_ln_consts(ctx, bufs)
        s_ld0 = ctx.sem("p4_ld0")

        @ctx.on("sync")
        def _(q):
            q.dma_start(out=bufs["st2b"], in_=st2b_i[:]).then_inc(s_ld0.h, 16)
            q.dma_start(out=bufs["bv_f32"][:], in_=bv_i[:]).then_inc(s_ld0.h, 16)
            q.dma_start(out=bufs["bo_pc"][:], in_=bo_i[:]).then_inc(s_ld0.h, 16)

        @ctx.on("vector")
        def _(v):
            v.wait_ge(s_ld0.h, 48)
            v.tensor_copy(bufs["bv_row"][:], bufs["bv_f32"][:]).then_inc(s_ld0.h, 1)

        cq_, nq_ = emit_weight_load(ctx, "p4wq", wq1, bufs["wq1b"], bufs["wld"], 8, D)
        ck_, nk_ = emit_weight_load(ctx, "p4wk", wk1, bufs["wk1b"], bufs["wld"], 8, 256,
                                    prev=(cq_, nq_))
        cv_, nv_ = emit_weight_load(ctx, "p4wv", wv1, bufs["wv1b"], bufs["wld"], 8, 256,
                                    prev=(ck_, nk_))

        ap = {"l1m": l1m, "bq_pc": bq_i, "bk_pc": bk_i, "sel2": sel2_i}
        ctx.streams["tensor"].append(
            lambda t: (t.wait_ge(cq_.h, nq_), t.wait_ge(ck_.h, nk_),
                       t.wait_ge(cv_.h, nv_), t.wait_ge(s_ld0.h, 49)))
        h4 = emit_p4(ctx, bufs, ap)
        s_no, n_no = h4["s_no"], h4["n_no"]

        co_, no_cnt = emit_weight_load(ctx, "p4wo", wo1, bufs["wo1b"], bufs["wld"],
                                       8, D, prev=(cv_, nv_))

        h = emit_proj_ln(
            ctx, "p4c", NT_O,
            lhsW=lambda dc, oc: bufs["wo1b"][:, dc, oc * 128:(oc + 1) * 128],
            rhs=lambda nt, dc: bufs["q1T"][:, dc, nt * TT:(nt + 1) * TT],
            rhs_load=None,
            res=lambda nt: st2f_i[:, :, HALO + nt * TT:HALO + (nt + 1) * TT],
            out_f32=lambda nt: st3f_o[:, :, nt * TT:(nt + 1) * TT],
            out_bf=lambda nt: bufs["st3b"][:, :, nt * TT:(nt + 1) * TT],
            bufs=bufs,
            bias_col=lambda oc: bufs["bo_pc"][:, oc:oc + 1],
            pre_wait=lambda t: (t.wait_ge(s_no.h, n_no), t.wait_ge(co_.h, no_cnt)),
            sync_gate=lambda q: q.wait_ge(s_no.h, n_no),
        )
        s_ap, n_ap = h["s_ap"], h["s_ap"].n
        so0, so1 = h["s_so0"], h["s_so1"]
        n_so0, n_so1 = so0.n, so1.n
        s_st = ctx.sem("p4_store")

        @ctx.on("gpsimd")
        def _(g):
            g.wait_ge(s_ap.h, n_ap)
            g.dma_start(out=st3b_o[:], in_=bufs["st3b"]).then_inc(s_st.h, 16)
            g.wait_ge(s_st.h, 16)
            g.wait_ge(so0.h, n_so0)
            g.wait_ge(so1.h, n_so1)

        with nc.Block() as block:
            ctx.finalize(block)
    return nc


def build_p5w(core):
    import contextlib
    nc = bass.Bass("TRN2", target_bir_lowering=False)
    mats = [
        ("we1", [NE + 1, D, HF], "w1b"),
        ("we2", [NE + 1, D, HF], "w2b"),
        ("we3", [NE + 1, HF, D], "w3b"),
    ]
    with contextlib.ExitStack() as stack:
        ctx = Ctx(nc, stack)
        lds = [sbuf(stack, nc, f"c{i}", [128, HF], F32) for i in range(2)]
        obs = [sbuf(stack, nc, f"o{i}", [128, HF], BF16) for i in range(2)]
        s_l = [ctx.sem("w_l0"), ctx.sem("w_l1")]
        s_c = ctx.sem("w_c")
        s_s = [ctx.sem("w_s0"), ctx.sem("w_s1")]
        chunks = []
        for nm, shape, onm in mats:
            ap_in = nc.dram_tensor(nm, shape, F32, kind="ExternalInput").ap()
            ap_out = nc.dram_tensor(onm, shape, BF16, kind="ExternalOutput").ap()
            rows = shape[0] * shape[1]
            width = shape[2]
            fi = ap_in.rearrange("e r c -> (e r) c")
            fo = ap_out.rearrange("e r c -> (e r) c")
            nch = rows // 128
            for i in range(nch):
                chunks.append((fi[i * 128:(i + 1) * 128, :],
                               fo[i * 128:(i + 1) * 128, :], width))
        l_after = [s_l[i % 2].inc(16) for i in range(len(chunks))]
        c_after = [s_c.inc() for _ in range(len(chunks))]
        st_after = [s_s[i % 2].inc(16) for i in range(len(chunks))]

        @ctx.on("sync")
        def _(q):
            for i, (fi, fo, w) in enumerate(chunks):
                if i >= 2:
                    q.wait_ge(s_c.h, c_after[i - 2])
                q.dma_start(out=lds[i % 2][:, 0:w], in_=fi).then_inc(s_l[i % 2].h, 16)

        @ctx.on("vector")
        def _(v):
            for i, (fi, fo, w) in enumerate(chunks):
                v.wait_ge(s_l[i % 2].h, l_after[i])
                if i >= 2:
                    v.wait_ge(s_s[i % 2].h, st_after[i - 2])
                v.tensor_copy(obs[i % 2][:, 0:w], lds[i % 2][:, 0:w]).then_inc(s_c.h, 1)

        @ctx.on("gpsimd")
        def _(g):
            for i, (fi, fo, w) in enumerate(chunks):
                g.wait_ge(s_c.h, c_after[i])
                g.dma_start(out=fo, in_=obs[i % 2][:, 0:w]).then_inc(s_s[i % 2].h, 16)
            g.wait_ge(s_s[0].h, s_s[0].n)
            g.wait_ge(s_s[1].h, s_s[1].n)

        with nc.Block() as block:
            ctx.finalize(block)
    return nc


HSC_LIST = [(0, 1024), (1024, 1024), (2048, 682)]


def _hcs(width):
    out = []
    off = 0
    while off < width:
        out.append((off, min(128, width - off)))
        off += 128
    return out


def emit_p5(ctx, bufs, ap):
    nc = ctx.nc
    st3b = bufs["st3b"]
    wgb = bufs["wgb"]
    wgf = bufs["wg_f32"]
    rbc = bufs["rbias_c"]
    eye8 = bufs["eye8"]
    eye128 = bufs["eye128"]
    sel8 = bufs["sel8"]
    gate8 = bufs["gate8"]
    gtm = bufs["gtm"]
    gsr = bufs["gsr"]
    gtmp = bufs["gtmp"]
    gexp = bufs["gexp"]
    gmsk = bufs["gmsk"]
    gden = bufs["gden"]
    wtsT = bufs["wtsT"]
    w1c, w2c, w3c = bufs["w1c"], bufs["w2c"], bufs["w3c"]
    gbuf = bufs["gbuf"]
    silu = bufs["silu"]
    wbc = bufs["wbc"]
    tmp5 = bufs["tmp5"]
    acc = bufs["acc"]
    ps_a, ps_b = bufs["ps_a"], bufs["ps_b"]
    ps_h1 = [bufs["ps_a"], bufs["ps_h"]]
    ps_h2 = [bufs["ps_b"], bufs["ps_i"]]
    ps_c = [bufs["ps_e"], bufs["ps_f"]]
    ps_w = bufs["ps_g"]
    ap_w1, ap_w2, ap_w3 = ap["w1b"], ap["w2b"], ap["w3b"]

    s_ld = ctx.sem("p5_ld")
    s_wgc = ctx.sem("p5_wgc")
    s_gmm = ctx.sem("p5_gmm")
    s_gev = ctx.sem("p5_gev")
    s_te1 = ctx.sem("p5_te1")
    s_srt = ctx.sem("p5_srt")
    s_te2 = ctx.sem("p5_te2")
    s_wl = [ctx.sem("p5_wl0"), ctx.sem("p5_wl1")]
    s_h1 = ctx.sem("p5_h1")
    s_sil = ctx.sem("p5_sil")
    s_h2 = ctx.sem("p5_h2")
    s_g = ctx.sem("p5_g")
    s_eo = ctx.sem("p5_eo")
    s_wb = ctx.sem("p5_wb")
    s_wbe = ctx.sem("p5_wbe")
    s_acc = ctx.sem("p5_acc")
    s_sub = ctx.sem("p5_sub")
    s_exp5 = ctx.sem("p5_exp")
    s_sl8 = ctx.sem("p5_sl8")
    s_vf = ctx.sem("vfence")

    NS = len(HSC_LIST)

    gmm_after = {nt: s_gmm.inc() for nt in range(NT_O)}
    gev_after = {nt: s_gev.inc() for nt in range(NT_O)}
    te1_after = {tt: s_te1.inc() for tt in range(16)}
    te2_after = {tt: s_te2.inc() for tt in range(16)}
    wtiles = [(e, si, m) for e in range(NE + 1) for si in range(NS)
              for m in range(3)]
    wl_after = {}
    wl_par = {}
    for i, (e, si, m) in enumerate(wtiles):
        hw = HSC_LIST[si][1]
        ndma = 2 if (m == 2 and hw % 128 != 0) else 1
        wl_par[e, si, m] = i % 2
        wl_after[e, si, m] = s_wl[i % 2].inc(16 * ndma)
    hgroups = [(e, si, nt, hi)
               for e in range(NE + 1) for si in range(NS)
               for nt in range(NT_O)
               for hi in range(len(_hcs(HSC_LIST[si][1])))]
    hg_i = {k: i for i, k in enumerate(hgroups)}
    h1_after, sil_after, h2_after, g_after = {}, {}, {}, {}
    for k in hgroups:
        h1_after[k] = s_h1.inc()
        sil_after[k] = s_sil.inc()
        h2_after[k] = s_h2.inc()
        g_after[k] = s_g.inc()
    gtiles = [(e, si, nt) for e in range(NE + 1) for si in range(NS)
              for nt in range(NT_O)]
    gt_i = {k: i for i, k in enumerate(gtiles)}
    eoops = [(e, si, nt, dc) for e in range(NE + 1) for si in range(NS)
             for nt in range(NT_O) for dc in range(8)]
    eo_i = {k: i for i, k in enumerate(eoops)}
    eo_after, acc_after = {}, {}
    for k in eoops:
        eo_after[k] = s_eo.inc()
        acc_after[k] = s_acc.inc()
    n_acc_total = s_acc.n
    wslots = [(e, nt) for e in range(1, NE + 1) for nt in range(NT_O)]
    wb_after = {k: s_wb.inc() for k in wslots}
    wbe_after = {k: s_wbe.inc() for k in wslots}

    def acc_done_for(nt, oc):
        return acc_after[NE, NS - 1, nt, oc]

    @ctx.on("sync")
    def _(q):
        q.dma_start(out=wgf[:], in_=ap["wgT"].rearrange("(dc p) e -> p dc e", p=128)
                    ).then_inc(s_ld.h, 16)
        q.dma_start(out=rbc[:], in_=ap["rbias_c"]).then_inc(s_ld.h, 16)
        q.dma_start(out=eye8[:], in_=ap["eye8"]).then_inc(s_ld.h, 16)
        q.dma_start(out=eye128[:], in_=ap["eye128"]).then_inc(s_ld.h, 16)
        q.wait_ge(s_te2.h, te2_after[15])
        q.dma_start(out=sel8[:], in_=ap["sel8"]).then_inc(s_sl8.h, 16)
        if _P5_NO_MOE:
            return
        for e in range(NE + 1):
            for si in range(NS):
                h0, hw = HSC_LIST[si]
                for m in range(3):
                    i = wtiles.index((e, si, m))
                    if i >= 3:
                        pe, psi, _ = wtiles[i - 3]
                        q.wait_ge(s_eo.h, eo_after[pe, psi, NT_O - 1, 7])
                    par = wl_par[e, si, m]
                    if m < 2:
                        w = ap_w1 if m == 0 else ap_w2
                        dst = w1c if m == 0 else w2c
                        q.dma_start(
                            out=dst[:, :, 0:hw],
                            in_=w[e].rearrange("(dc p) h -> p dc h", p=128)[:, :, h0:h0 + hw],
                        ).then_inc(s_wl[par].h, 16)
                    else:
                        full = (hw // 128) * 128
                        ins = q.dma_start(
                            out=w3c[:, 0:hw // 128, :],
                            in_=ap_w3[e][h0:h0 + full, :].rearrange(
                                "(hc p) d -> p hc d", p=128))
                        if hw % 128 != 0:
                            ins.then_inc(s_wl[par].h, 16)
                            q.dma_start(
                                out=w3c[0:hw - full, hw // 128, :],
                                in_=ap_w3[e][h0 + full:h0 + hw, :],
                            ).then_inc(s_wl[par].h, 16)
                        else:
                            ins.then_inc(s_wl[par].h, 16)

    @ctx.on("tensor")
    def _(t):
        pre = bufs.get("p5_pre_wait")
        if pre is not None:
            pre(t)
        t.wait_ge(s_ld.h, 16 * 4)
        t.wait_ge(s_wgc.h, 1)
        for nt in range(NT_O):
            if nt >= 1:
                t.wait_ge(s_gev.h, gev_after[nt - 1])
            for dc in range(8):
                ins = t.matmul(ps_w[0:8, :], wgb[:, dc, :],
                               st3b[:, dc, nt * TT:(nt + 1) * TT],
                               start=(dc == 0), stop=(dc == 7),
                               skip_group_check=True)
            ins.then_inc(s_gmm.h, 1)
        t.wait_ge(s_gev.h, gev_after[NT_O - 1])
        for tt in range(16):
            if tt >= 2:
                t.wait_ge(s_te1.h, te1_after[tt - 2])
            pbank = ps_a if tt % 2 == 0 else ps_b
            t.transpose(pbank[0:128, 0:8], gate8[0:8, tt * 128:(tt + 1) * 128],
                        eye8[:, :]).then_inc(s_gmm.h, 1)
        t.wait_ge(s_srt.h, 1)
        for tt in range(16):
            if tt >= 2:
                t.wait_ge(s_te2.h, te2_after[tt - 2])
            pbank = ps_a if tt % 2 == 0 else ps_b
            t.transpose(pbank[0:8, 0:128], gtm[:, tt, :],
                        eye128[:, :]).then_inc(s_gmm.h, 1)
        if _P5_NO_MOE:
            return
        t.wait_ge(s_te2.h, te2_after[15])
        t.wait_ge(s_sl8.h, 16)
        for e in range(NE + 1):
            for si in range(NS):
                h0, hw = HSC_LIST[si]
                hcs = _hcs(hw)
                t.wait_ge(s_wl[wl_par[e, si, 0]].h, wl_after[e, si, 0])
                t.wait_ge(s_wl[wl_par[e, si, 1]].h, wl_after[e, si, 1])
                t.wait_ge(s_wl[wl_par[e, si, 2]].h, wl_after[e, si, 2])
                for nt in range(NT_O):
                    if e >= 1 and si == 0:
                        wi = wslots.index((e, nt))
                        if wi >= 1:
                            t.wait_ge(s_wbe.h, wbe_after[wslots[wi - 1]])
                        t.matmul(ps_w[:, :], sel8[:, (e - 1) * 128:e * 128],
                                 wtsT[:, nt * TT:(nt + 1) * TT],
                                 start=True, stop=True,
                                 skip_group_check=True).then_inc(s_wb.h, 1)
                    for hi, (hco, hcw) in enumerate(hcs):
                        k = (e, si, nt, hi)
                        ih = hg_i[k]
                        if ih >= 2:
                            t.wait_ge(s_sil.h, sil_after[hgroups[ih - 2]])
                        for dc in range(8):
                            ins = t.matmul(
                                ps_h1[ih % 2][0:hcw, :],
                                w1c[:, dc, hco:hco + hcw],
                                st3b[:, dc, nt * TT:(nt + 1) * TT],
                                start=(dc == 0), stop=(dc == 7),
                                skip_group_check=True)
                        ins.then_inc(s_h1.h, 1)
                        if ih >= 2:
                            t.wait_ge(s_g.h, g_after[hgroups[ih - 2]])
                        for dc in range(8):
                            ins = t.matmul(
                                ps_h2[ih % 2][0:hcw, :],
                                w2c[:, dc, hco:hco + hcw],
                                st3b[:, dc, nt * TT:(nt + 1) * TT],
                                start=(dc == 0), stop=(dc == 7),
                                skip_group_check=True)
                        ins.then_inc(s_h2.h, 1)
                    for dc in range(8):
                        k = (e, si, nt, dc)
                        ie = eo_i[k]
                        if ie >= 2:
                            t.wait_ge(s_acc.h, acc_after[eoops[ie - 2]])
                        for hi, (hco, hcw) in enumerate(hcs):
                            t.wait_ge(s_g.h, g_after[e, si, nt, hi])
                            ins = t.matmul(
                                ps_c[ie % 2][0:128, :],
                                w3c[0:hcw, hi, dc * 128:(dc + 1) * 128],
                                gbuf[nt % 2][0:hcw, hi, :],
                                start=(hi == 0), stop=(hi == len(hcs) - 1),
                                skip_group_check=True)
                        ins.then_inc(s_eo.h, 1)

    @ctx.on("scalar")
    def _(s):
        s.wait_ge(s_ld.h, 16 * 4)
        for nt in range(NT_O):
            s.wait_ge(s_gmm.h, gmm_after[nt])
            s.activation(gate8[0:8, nt * TT:(nt + 1) * TT], ps_w[0:8, :],
                         AF.Identity, bias=rbc[0:8, 0:1]).then_inc(s_gev.h, 1)
        s.wait_ge(s_sub.h, 1)
        s.activation(gexp[:], gexp[:], AF.Exp).then_inc(s_exp5.h, 1)
        if _P5_NO_MOE:
            return
        for e in range(NE + 1):
            for si in range(NS):
                hcs = _hcs(HSC_LIST[si][1])
                for nt in range(NT_O):
                    if e >= 1 and si == 0:
                        s.wait_ge(s_wb.h, wb_after[e, nt])
                        if e >= 2:
                            s.wait_ge(s_acc.h, acc_after[e - 1, NS - 1, nt, 7])
                        s.copy(wbc[nt][:], ps_w[:, :]).then_inc(s_wbe.h, 1)
                    for hi, (hco, hcw) in enumerate(hcs):
                        k = (e, si, nt, hi)
                        ih = hg_i[k]
                        s.wait_ge(s_h1.h, h1_after[k])
                        if ih >= 2:
                            s.wait_ge(s_g.h, g_after[hgroups[ih - 2]])
                        s.activation(silu[ih % 2][0:hcw, :],
                                     ps_h1[ih % 2][0:hcw, :],
                                     AF.Silu).then_inc(s_sil.h, 1)

    @ctx.on("vector")
    def _(v):
        v.wait_ge(s_ld.h, 16 * 4)
        v.tensor_copy(wgb[:], wgf[:]).then_inc(s_wgc.h, 1)
        for tt in range(16):
            v.wait_ge(s_gmm.h, gmm_after[NT_O - 1] + tt + 1)
            pbank = ps_a if tt % 2 == 0 else ps_b
            v.tensor_copy(gtm[:, tt, :], pbank[0:128, 0:8]).then_inc(s_te1.h, 1)
        v.tensor_copy(gsr[:], gtm[:]).then_inc(s_vf.h, 1)
        v.wait_ge(s_vf.h, s_vf.inc())
        for rnd in range(8):
            pairs = ([(0, 1), (2, 3), (4, 5), (6, 7)] if rnd % 2 == 0
                     else [(1, 2), (3, 4), (5, 6)])
            for (i, j) in pairs:
                v.tensor_max(gtmp[:], gsr[:, :, i], gsr[:, :, j])
                v.tensor_tensor(gsr[:, :, j], gsr[:, :, i], gsr[:, :, j],
                                ALU.min)
                v.tensor_copy(gsr[:, :, i], gtmp[:])
        v.tensor_copy(gtmp[:], gsr[:, :, 0]).then_inc(s_vf.h, 1)
        v.wait_ge(s_vf.h, s_vf.inc())
        for e in range(NE):
            v.tensor_tensor(gmsk[:, :, e], gtm[:, :, e], gsr[:, :, 3],
                            ALU.is_ge)
            ins = v.tensor_sub(gexp[:, :, e], gtm[:, :, e], gsr[:, :, 0])
            if e == NE - 1:
                ins.then_inc(s_sub.h, 1)
        v.wait_ge(s_exp5.h, 1)
        v.tensor_mul(gexp[:], gexp[:], gmsk[:]).then_inc(s_vf.h, 1)
        v.wait_ge(s_vf.h, s_vf.inc())
        v.reduce_sum(gden[:], gexp[:], axis=mybir.AxisListType.X).then_inc(s_vf.h, 1)
        v.wait_ge(s_vf.h, s_vf.inc())
        v.reciprocal(gden[:], gden[:]).then_inc(s_vf.h, 1)
        v.wait_ge(s_vf.h, s_vf.inc())
        for e in range(NE):
            v.tensor_mul(gtm[:, :, e], gexp[:, :, e], gden[:, :, 0]
                         ).then_inc(s_srt.h, 1) if e == NE - 1 else \
                v.tensor_mul(gtm[:, :, e], gexp[:, :, e], gden[:, :, 0])
        for tt in range(16):
            v.wait_ge(s_gmm.h, gmm_after[NT_O - 1] + 16 + tt + 1)
            pbank = ps_a if tt % 2 == 0 else ps_b
            v.tensor_copy(wtsT[0:8, tt * 128:(tt + 1) * 128],
                          pbank[0:8, 0:128]).then_inc(s_te2.h, 1)
        if _P5_NO_MOE:
            return
        for e in range(NE + 1):
            for si in range(NS):
                hcs = _hcs(HSC_LIST[si][1])
                for nt in range(NT_O):
                    for hi, (hco, hcw) in enumerate(hcs):
                        k = (e, si, nt, hi)
                        ih = hg_i[k]
                        v.wait_ge(s_sil.h, sil_after[k])
                        v.wait_ge(s_h2.h, h2_after[k])
                        ig = gt_i[e, si, nt]
                        if ig >= 2 and hi == 0:
                            v.wait_ge(s_eo.h, eo_after[gtiles[ig - 2] + (7,)])
                        v.tensor_mul(gbuf[nt % 2][0:hcw, hi, :],
                                     silu[ih % 2][0:hcw, :],
                                     ps_h2[ih % 2][0:hcw, :]).then_inc(s_g.h, 1)
                    for dc in range(8):
                        k = (e, si, nt, dc)
                        ie = eo_i[k]
                        v.wait_ge(s_eo.h, eo_after[k])
                        cols = slice(nt * TT, (nt + 1) * TT)
                        if e == 0 and si == 0:
                            v.tensor_copy(acc[:, dc, cols],
                                          ps_c[ie % 2][:]).then_inc(s_acc.h, 1)
                        elif e == 0:
                            v.tensor_add(acc[:, dc, cols], acc[:, dc, cols],
                                         ps_c[ie % 2][:]).then_inc(s_acc.h, 1)
                        else:
                            if dc == 0:
                                v.wait_ge(s_wbe.h, wbe_after[e, nt])
                            v.tensor_mul(tmp5[dc % 2][:], ps_c[ie % 2][:],
                                         wbc[nt][:])
                            v.tensor_add(acc[:, dc, cols], acc[:, dc, cols],
                                         tmp5[dc % 2][:]).then_inc(s_acc.h, 1)

    return dict(s_acc=s_acc, n_acc=n_acc_total, acc_done_for=acc_done_for)


_P5_DEBUG = False
_P5_NO_MOE = False


def build_p5(core):
    import contextlib
    nc = bass.Bass("TRN2", target_bir_lowering=False)
    st3f_i = nc.dram_tensor("st3f_i", [128, 8, OWN], F32, kind="ExternalInput").ap()
    st3b_i = nc.dram_tensor("st3b_i", [128, 8, OWN], BF16, kind="ExternalInput").ap()
    wgT = nc.dram_tensor("wgT", [D, 8], F32, kind="ExternalInput").ap()
    rbias_c = nc.dram_tensor("rbias_c", [8, 1], F32, kind="ExternalInput").ap()
    eye8 = nc.dram_tensor("eye8", [8, 8], F32, kind="ExternalInput").ap()
    eye128 = nc.dram_tensor("eye128", [128, 128], F32, kind="ExternalInput").ap()
    sel8 = nc.dram_tensor("sel8", [8, 1024], F32, kind="ExternalInput").ap()
    w1b = nc.dram_tensor("w1b", [NE + 1, D, HF], BF16, kind="ExternalInput").ap()
    w2b = nc.dram_tensor("w2b", [NE + 1, D, HF], BF16, kind="ExternalInput").ap()
    w3b = nc.dram_tensor("w3b", [NE + 1, HF, D], BF16, kind="ExternalInput").ap()
    out_o = nc.dram_tensor("out_o", [128, 8, OWN], F32, kind="ExternalOutput").ap()

    KB = 1024
    with contextlib.ExitStack() as stack:
        ov = sbuf(stack, nc, "ov", [128, 96 * KB], mybir.dt.uint8)
        bufs = {
            "ps_a": psum(stack, nc, "ps_a", [128, TT]),
            "ps_b": psum(stack, nc, "ps_b", [128, TT]),
            "ps_e": psum(stack, nc, "ps_e", [128, TT]),
            "ps_f": psum(stack, nc, "ps_f", [128, TT]),
            "ps_g": psum(stack, nc, "ps_g", [128, TT]),
            "ps_h": psum(stack, nc, "ps_h", [128, TT]),
            "ps_i": psum(stack, nc, "ps_i", [128, TT]),
            "acc": sbuf(stack, nc, "acc", [128, 8, OWN], F32),
            "st3b": sbuf(stack, nc, "st3b", [128, 8, OWN], BF16),
            "wtsT": sbuf(stack, nc, "wtsT", [8, OWN], F32),
            "wgb": sbuf(stack, nc, "wgb", [128, 8, 8], BF16),
            "wg_f32": sbuf(stack, nc, "wg_f32", [128, 8, 8], F32),
            "rbias_c": sbuf(stack, nc, "rbias_s", [8, 1], F32),
            "ones1_bf": sbuf(stack, nc, "ones1_bf", [128, 1], BF16),
            "ones32": sbuf(stack, nc, "ones32", [1, 128], F32),
            "eps_col": sbuf(stack, nc, "eps_col", [1, 1], F32),
        }
        bufs["gate8"] = aview(ov, 0, [8, OWN], F32)
        bufs["gtm"] = aview(ov, 8 * KB, [128, 16, 8], F32)
        bufs["gsr"] = aview(ov, 9 * KB, [128, 16, 8], F32)
        bufs["gexp"] = aview(ov, 10 * KB, [128, 16, 8], F32)
        bufs["gmsk"] = aview(ov, 11 * KB, [128, 16, 8], F32)
        bufs["gtmp"] = aview(ov, 12 * KB, [128, 16], F32)
        bufs["gden"] = aview(ov, 13 * KB, [128, 16, 1], F32)
        bufs["eye128"] = aview(ov, 14 * KB, [128, 128], F32)
        bufs["eye8"] = aview(ov, 15 * KB, [8, 8], F32)
        bufs["sel8"] = aview(ov, 0, [8, 1024], F32)
        bufs["w1c"] = aview(ov, 16 * KB, [128, 8, 1024], BF16)
        bufs["w2c"] = aview(ov, 32 * KB, [128, 8, 1024], BF16)
        bufs["w3c"] = aview(ov, 48 * KB, [128, 8, 1024], BF16)
        bufs["gbuf"] = [aview(ov, 64 * KB + i * 8 * KB, [128, 8, TT], BF16)
                        for i in range(2)]
        bufs["silu"] = [aview(ov, 80 * KB + i * 2 * KB, [128, TT], F32)
                        for i in range(2)]
        bufs["wbc"] = [aview(ov, 84 * KB + i * 2 * KB, [128, TT], F32)
                       for i in range(4)]
        bufs["tmp5"] = [aview(ov, 92 * KB + i * 2 * KB, [128, TT], F32)
                        for i in range(2)]
        bufs["p5c_x2"] = [aview(ov, i * 16 * KB, [128, 8, TT], F32) for i in range(2)]
        bufs["p5c_rs"] = [aview(ov, 32 * KB + i * 16 * KB, [128, 8, TT], F32)
                          for i in range(2)]
        bufs["p5c_xb"] = [aview(ov, 64 * KB + i * 8 * KB, [128, 8, TT], BF16)
                          for i in range(2)]
        bufs["p5c_sq"] = [sbuf(stack, nc, f"p5c_sq{i}", [128, TT], BF16)
                          for i in range(2)]
        off = 80 * KB
        for nm in ["p5c_rowm", "p5c_rowq", "p5c_rowsd", "p5c_rowr"]:
            bufs[nm] = [aview(ov, off, [1, TT], F32),
                        aview(ov, off + 2 * KB, [1, TT], F32)]
            off += 4 * KB
        bufs["p5c_rowv"] = bufs["p5c_rowq"]
        bufs["p5c_rowt"] = sbuf(stack, nc, "p5c_rowt", [1, TT], F32)
        bufs["ps_m"] = bufs["ps_a"][0:1, :]
        bufs["ps_q"] = bufs["ps_b"][0:1, :]
        bufs["ps_bm"] = bufs["ps_e"]
        bufs["ps_br"] = bufs["ps_f"]

        ctx = Ctx(nc, stack)
        emit_init_ln_consts(ctx, bufs)
        s_ld0 = ctx.sem("p5_ld0")

        @ctx.on("sync")
        def _(q):
            q.dma_start(out=bufs["st3b"][:], in_=st3b_i[:]).then_inc(s_ld0.h, 16)

        ctx.streams["tensor"].append(lambda t: t.wait_ge(s_ld0.h, 16))

        ap = {"wgT": wgT, "rbias_c": rbias_c, "eye8": eye8, "eye128": eye128,
              "sel8": sel8, "w1b": w1b, "w2b": w2b, "w3b": w3b}
        h5 = emit_p5(ctx, bufs, ap)
        s_acc, n_acc = h5["s_acc"], h5["n_acc"]
        adf = h5["acc_done_for"]

        if _P5_NO_MOE:
            s_te2x = ctx.sem("p5_te2")
            nte2x = s_te2x.n
            dbg_g = nc.dram_tensor("dbg_gate", [8, OWN], F32,
                                   kind="ExternalOutput").ap()
            dbg_t = nc.dram_tensor("dbg_gtm", [128, 16, 8], F32,
                                   kind="ExternalOutput").ap()
            dbg_w = nc.dram_tensor("dbg_wts", [8, OWN], F32,
                                   kind="ExternalOutput").ap()
            dbg_sr = nc.dram_tensor("dbg_gsr", [128, 16, 8], F32,
                                    kind="ExternalOutput").ap()
            dbg_ex = nc.dram_tensor("dbg_gexp", [128, 16, 8], F32,
                                    kind="ExternalOutput").ap()
            dbg_mk = nc.dram_tensor("dbg_gmsk", [128, 16, 8], F32,
                                    kind="ExternalOutput").ap()
            dbg_dn = nc.dram_tensor("dbg_gden", [128, 16, 1], F32,
                                    kind="ExternalOutput").ap()
            s_dbg = ctx.sem("p5_dbg")
            s_gevd = ctx.sem("p5_gev")
            s_te1d = ctx.sem("p5_te1")
            s_srtd = ctx.sem("p5_srt")

            @ctx.on("gpsimd")
            def _(g):
                g.wait_ge(s_gevd.h, 4)
                g.dma_start(out=dbg_g[:], in_=bufs["gate8"]).then_inc(s_dbg.h, 16)
                g.wait_ge(s_te1d.h, 16)
                g.dma_start(out=dbg_t[:], in_=bufs["gtm"]).then_inc(s_dbg.h, 16)
                g.wait_ge(s_srtd.h, 1)
                g.dma_start(out=dbg_sr[:], in_=bufs["gsr"]).then_inc(s_dbg.h, 16)
                g.dma_start(out=dbg_ex[:], in_=bufs["gexp"]).then_inc(s_dbg.h, 16)
                g.dma_start(out=dbg_mk[:], in_=bufs["gmsk"]).then_inc(s_dbg.h, 16)
                g.dma_start(out=dbg_dn[:], in_=bufs["gden"]).then_inc(s_dbg.h, 16)
                g.wait_ge(s_te2x.h, nte2x)
                g.dma_start(out=dbg_w[:], in_=bufs["wtsT"][:]).then_inc(s_dbg.h, 16)
                g.wait_ge(s_dbg.h, 112)

            with nc.Block() as block:
                ctx.finalize(block)
            return nc

        h = emit_proj_ln(
            ctx, "p5c", NT_O,
            lhsW=None, rhs=None, rhs_load=None,
            res=lambda nt: st3f_i[:, :, nt * TT:(nt + 1) * TT],
            out_f32=lambda nt: out_o[:, :, nt * TT:(nt + 1) * TT],
            out_bf=None,
            bufs=bufs,
            proj=False,
            y_src=lambda nt, oc: bufs["acc"][:, oc, nt * TT:(nt + 1) * TT],
            y_ready=(s_acc, {(nt, oc): n_acc
                             for nt in range(NT_O) for oc in range(8)}),
            pre_wait=lambda t: t.wait_ge(s_acc.h, n_acc),
            sync_gate=lambda q: q.wait_ge(s_acc.h, n_acc),
        )
        so0, so1 = h["s_so0"], h["s_so1"]
        n_so0, n_so1 = so0.n, so1.n

        if _P5_DEBUG:
            dbg_g = nc.dram_tensor("dbg_gate", [8, OWN], F32,
                                   kind="ExternalOutput").ap()
            dbg_w = nc.dram_tensor("dbg_wts", [8, OWN], F32,
                                   kind="ExternalOutput").ap()
            dbg_a = nc.dram_tensor("dbg_acc", [128, 8, OWN], F32,
                                   kind="ExternalOutput").ap()
            dbg_s = nc.dram_tensor("dbg_gsr", [128, 16, 8], F32,
                                   kind="ExternalOutput").ap()
            dbg_t = nc.dram_tensor("dbg_gtm", [128, 16, 8], F32,
                                   kind="ExternalOutput").ap()
            s_dbg = ctx.sem("p5_dbg")
            s_te2d = ctx.sem("p5_te2")
            s_te1d = ctx.sem("p5_te1")
            s_gevd = ctx.sem("p5_gev")
            s_srtd = ctx.sem("p5_srt")
            s_accd = ctx.sem("p5_acc")
            nte2, nacc = s_te2d.n, s_accd.n

            @ctx.on("gpsimd")
            def _(g):
                g.wait_ge(s_gevd.h, 4)
                g.dma_start(out=dbg_g[:], in_=bufs["gate8"]).then_inc(s_dbg.h, 16)
                g.wait_ge(s_te1d.h, 16)
                g.dma_start(out=dbg_t[:], in_=bufs["gtm"]).then_inc(s_dbg.h, 16)
                g.wait_ge(s_srtd.h, 1)
                g.dma_start(out=dbg_s[:], in_=bufs["gsr"]).then_inc(s_dbg.h, 16)
                g.wait_ge(s_te2d.h, nte2)
                g.dma_start(out=dbg_w[:], in_=bufs["wtsT"][:]).then_inc(s_dbg.h, 16)
                g.wait_ge(s_accd.h, nacc)
                g.dma_start(out=dbg_a[:], in_=bufs["acc"][:]).then_inc(s_dbg.h, 16)
                g.wait_ge(s_dbg.h, 80)

        @ctx.on("gpsimd")
        def _(g):
            g.wait_ge(so0.h, n_so0)
            g.wait_ge(so1.h, n_so1)

        with nc.Block() as block:
            ctx.finalize(block)
    return nc


_PROGS = {}


def _prog(name, builder):
    if name not in _PROGS:
        _PROGS[name] = builder(0)
    return _PROGS[name]


def kernel(**inputs):
    cores = list(range(8))
    im = [prep_core_inputs(c, inputs) for c in range(8)]

    r1 = run_bass_kernel_spmd(_prog("p1", build_p1),
                              [{k: m[k] for k in ("xt", "wqkvT")} for m in im],
                              core_ids=cores)
    r2 = run_bass_kernel_spmd(_prog("p2", build_p2), [
        {"kt_i": r1.results[c]["kt_o"], "qt_i": r1.results[c]["qt_o"],
         "vkm_i": r1.results[c]["vkm_o"], "nrm_i": r1.results[c]["nrm_o"],
         "l0dm_i": im[c]["l0dm"], "sel2_i": im[c]["sel2"]}
        for c in range(8)], core_ids=cores)

    ag = np.zeros((B, D, SP), dtype=BF16NP)
    for c in range(8):
        bb, q = c // 4, c % 4
        ag[bb, q * 256:(q + 1) * 256, PAD:] = \
            r2.results[c]["attn_o"].reshape(256, S)

    sub3 = []
    for c in range(8):
        bb, q = c // 4, c % 4
        own = q * OWN
        sub3.append({"ags": np.ascontiguousarray(ag[bb][:, own:own + NTOK]),
                     "xs": im[c]["xs"], "woutT": im[c]["woutT"]})
    r3a = run_bass_kernel_spmd(_prog("p3a", build_p3a), sub3, core_ids=cores)
    r3m = run_bass_kernel_spmd(_prog("p3m", build_p3m), [
        {k: im[c][k] for k in ("wr", "wi", "cb", "sb", "c2b", "ns2b")}
        for c in range(8)], core_ids=cores)
    r3c = run_bass_kernel_spmd(_prog("p3c", build_p3c), [
        {"m_i": r3m.results[c]["m_o"], "st1f_i": r3a.results[c]["st1f_o"],
         "st1b_i": r3a.results[c]["st1b_o"], "l0b_i": im[c]["l0bias_pc"]}
        for c in range(8)], core_ids=cores)

    r4 = run_bass_kernel_spmd(_prog("p4", build_p4), [
        {"st2f_i": r3c.results[c]["st2f_o"], "st2b_i": r3c.results[c]["st2b_o"],
         "wq1": im[c]["wq1"], "wk1": im[c]["wk1"], "wv1": im[c]["wv1"],
         "wo1": im[c]["wo1"], "bq_pc": im[c]["bq_pc"], "bk_pc": im[c]["bk_pc"],
         "bv_row": im[c]["bv_row"], "bo_pc": im[c]["bo_pc"],
         "l1m": im[c]["l1m"], "sel2_i": im[c]["sel2"]}
        for c in range(8)], core_ids=cores)

    r5w = run_bass_kernel_spmd(_prog("p5w", build_p5w), [
        {k: im[c][k] for k in ("we1", "we2", "we3")} for c in range(8)],
        core_ids=cores)
    r5 = run_bass_kernel_spmd(_prog("p5", build_p5), [
        {"st3f_i": r4.results[c]["st3f_o"], "st3b_i": r4.results[c]["st3b_o"],
         "wgT": im[c]["wgT"], "rbias_c": im[c]["rbias_c"], "eye8": im[c]["eye8"],
         "eye128": im[c]["eye128"], "sel8": im[c]["sel8"],
         "w1b": r5w.results[c]["w1b"], "w2b": r5w.results[c]["w2b"],
         "w3b": r5w.results[c]["w3b"]}
        for c in range(8)], core_ids=cores)

    out = np.empty((B, S, D), np.float32)
    for c in range(8):
        bb, q = c // 4, c % 4
        own = q * OWN
        oc = r5.results[c]["out_o"].reshape(128, 8, OWN)
        out[bb, own:own + OWN, :] = oc.transpose(2, 1, 0).reshape(OWN, D)
    return out


# version 03 (final)
# speedup vs baseline: 1.0327x; 1.0327x over previous
import numpy as np
import ml_dtypes
import concourse.bass as bass
import concourse.mybir as mybir
from concourse.bass_utils import run_bass_kernel_spmd

BF16NP = ml_dtypes.bfloat16
F32 = mybir.dt.float32
BF16 = mybir.dt.bfloat16
AF = mybir.ActivationFunctionType
ALU = mybir.AluOpType

B, S, D = 2, 8192, 1024
NH, DKH = 16, 64
HPC = 4
PAD = 512
SP = S + PAD
OWN = 2048
HALO = 512
NTOK = OWN + HALO
NE, TOPK, HF = 8, 4, 2730
NG, GS = 4, 4
WIN = 1024 - 512
TT = 512
NT_S = S // TT
NT_N = NTOK // TT
NT_O = OWN // TT


class Sem:
    def __init__(self, handle):
        self.h = handle
        self.n = 0

    def inc(self, k=1):
        self.n += k
        return self.n


class Ctx:
    ENGINES = ("sync", "tensor", "vector", "scalar", "gpsimd")

    def __init__(self, nc, stack):
        self.nc = nc
        self.stack = stack
        self.streams = {e: [] for e in self.ENGINES}
        self._sems = {}

    def sem(self, name):
        if name not in self._sems:
            h = self.stack.enter_context(self.nc.semaphore(name))
            self._sems[name] = Sem(h)
        return self._sems[name]

    def on(self, engine):
        def deco(fn):
            self.streams[engine].append(fn)
            return fn
        return deco

    def finalize(self, block):
        for eng in self.ENGINES:
            fns = self.streams[eng]
            deco = getattr(block, eng)

            def make(fns):
                def body(e):
                    for f in fns:
                        f(e)
                return body
            deco(make(fns))


def sbuf(stack, nc, name, shape, dtype):
    return stack.enter_context(nc.sbuf_tensor(name, shape, dtype))


def psum(stack, nc, name, shape, dtype=F32):
    return stack.enter_context(nc.psum_tensor(name, shape, dtype))


def _dft_consts():
    n = np.arange(D)
    ang = 2.0 * np.pi * np.outer(n, n) / D
    C = np.cos(ang).astype(np.float32)
    Sn = np.sin(ang).astype(np.float32)
    return (C.astype(BF16NP), Sn.astype(BF16NP),
            (C / D).astype(BF16NP), (-Sn / D).astype(BF16NP))


def _l0_diag_masks():
    r_ = np.arange(4)[:, None, None]
    k = np.arange(128)[None, :, None]
    q = np.arange(TT)[None, None, :]
    return ((128 * r_ + k) <= q).astype(BF16NP)


def _l1_masks(own_start):
    out = np.zeros((NT_O, 8, 128, TT), dtype=BF16NP)
    for qi in range(NT_O):
        q0 = own_start + qi * TT
        qa = q0 + np.arange(TT)[None, :]
        for r in range(8):
            ka = q0 - 512 + 128 * r + np.arange(128)[:, None]
            m = (ka <= qa) & (ka > qa - 512) & (ka >= 0)
            out[qi, r] = m.astype(BF16NP)
    return out


def _qperm():
    old = np.zeros(D, np.int64)
    for c in range(8):
        for half in range(2):
            g = 2 * (c // 4) + half
            m = c % 4
            h = 4 * g + m
            old[c * 128 + 64 * half: c * 128 + 64 * half + 64] = \
                np.arange(h * 64, h * 64 + 64)
    return old


_QPERM = _qperm()


def l1_head_slot(h):
    g, m = h // 4, h % 4
    return (g // 2) * 4 + m, 64 * (g % 2)


def prep_core_inputs(core, d):
    b, q = core // 4, core % 4
    h0 = HPC * q
    own_start = q * OWN

    X = d["L0_attn_X"]
    xt = np.zeros((D, SP), np.float32)
    xt[:, PAD:] = np.ascontiguousarray(X[b].T)

    WqkvT = d["L0_attn_W_qkv"].T
    wqkvT_c = np.ascontiguousarray(np.concatenate(
        [WqkvT[:, h0 * 64:(h0 + HPC) * 64],
         WqkvT[:, D + h0 * 64:D + (h0 + HPC) * 64],
         WqkvT[:, 2 * D + h0 * 64:2 * D + (h0 + HPC) * 64]], axis=1))

    Cb, Sb, C2b, nS2b = _dft_consts()

    inp = {
        "xt": xt,
        "xs": np.ascontiguousarray(xt[:, own_start:own_start + NTOK]),
        "wqkvT": wqkvT_c,
        "woutT": np.ascontiguousarray(d["L0_attn_W_out"].T),
        "wr": d["L0_ffn_w_freq_real"],
        "wi": d["L0_ffn_w_freq_imag"],
        "l0bias": d["L0_ffn_bias"],
        "l0bias_pc": np.ascontiguousarray(d["L0_ffn_bias"].reshape(8, 128).T),
        "cb": Cb, "sb": Sb, "c2b": C2b, "ns2b": nS2b,
        "l0dm": _l0_diag_masks(),
        "wq1": np.ascontiguousarray(d["L1_attn_Wq"][:, _QPERM]),
        "bq_pc": np.ascontiguousarray(
            d["L1_attn_bq"][_QPERM].reshape(8, 128).T),
        "wk1": d["L1_attn_Wk"],
        "bk_pc": np.ascontiguousarray(d["L1_attn_bk"].reshape(2, 128).T),
        "wv1": d["L1_attn_Wv"],
        "bv_row": np.ascontiguousarray(d["L1_attn_bv"].reshape(1, 256)),
        "wo1": np.ascontiguousarray(d["L1_attn_Wo"][_QPERM, :]),
        "bo_pc": np.ascontiguousarray(d["L1_attn_bo"].reshape(8, 128).T),
        "sel2": np.eye(2, dtype=np.float32),
        "l1m": _l1_masks(own_start),
        "wgT": np.ascontiguousarray(d["L1_ffn_Wg"].T),
        "rbias_c": np.ascontiguousarray(
            d["L1_ffn_routing_bias"].reshape(8, 1)),
        "we1": np.ascontiguousarray(np.concatenate(
            [d["L1_ffn_Ws1"].T[None], d["L1_ffn_W1"].transpose(0, 2, 1)])),
        "we2": np.ascontiguousarray(np.concatenate(
            [d["L1_ffn_Ws2"].T[None], d["L1_ffn_W2"].transpose(0, 2, 1)])),
        "we3": np.ascontiguousarray(np.concatenate(
            [d["L1_ffn_Ws3"].T[None], d["L1_ffn_W3"].transpose(0, 2, 1)])),
        "eye8": np.eye(8, dtype=np.float32),
        "eye128": np.eye(128, dtype=np.float32),
        "sel8": _sel8(),
    }
    return inp


def _sel8():
    s = np.zeros((8, 8 * 128), np.float32)
    for e in range(8):
        s[e, e * 128:(e + 1) * 128] = 1.0
    return s


def emit_p1(ctx, bufs, ap):
    nc = ctx.nc
    xt, wqkvT = ap["xt"], ap["wqkvT"]
    ktT, qtT, vkm = bufs["ktT"], bufs["qtT"], bufs["vkm"]
    nrm_all, nrm = bufs["nrm_all"], bufs["nrm"]
    wq_bf, wld = bufs["p1_wbf"], bufs["p1_wld"]
    xf, xb = bufs["p1_xf"], bufs["p1_xb"]
    sqb = bufs["p1_sqb"]
    ones2 = bufs["ones2"]
    ps_qk = [bufs["ps_a"], bufs["ps_b"]]
    ps_v = [bufs["ps_c"], bufs["ps_d"]]
    ps_n = bufs["ps_n"]

    s_wld = [ctx.sem("p1_wld0"), ctx.sem("p1_wld1")]
    s_wcv = ctx.sem("p1_wcv")
    s_x = [ctx.sem("p1_x0"), ctx.sem("p1_x1")]
    s_xcv = ctx.sem("p1_xcv")
    s_mm = ctx.sem("p1_mm")
    s_ev = ctx.sem("p1_ev")
    s_mmv = ctx.sem("p1_mmv")
    s_evv = ctx.sem("p1_evv")
    s_sq = ctx.sem("p1_sq")
    s_nm = ctx.sem("p1_nm")
    s_nr = ctx.sem("p1_nr")

    wld_after = [s_wld[i % 2].inc(16) for i in range(8)]
    wcv_after = [s_wcv.inc() for _ in range(8)]
    x_after = [s_x[i % 2].inc(16) for i in range(NT_S)]
    xcv_after = [s_xcv.inc() for _ in range(NT_S)]
    mm_after = {}
    ev_after = {}
    sq_after = {}
    nm_after = {}
    nr_after = {}
    mmv_after = {}
    evv_after = {}
    nm_after_flat = {}
    for tt in range(NT_S):
        for ec in range(4):
            mm_after[tt, ec] = s_mm.inc()
            ev_after[tt, ec] = s_ev.inc()
            sq_after[tt, ec] = s_sq.inc()
            nm_after[tt, ec] = s_nm.inc()
            nm_after_flat[tt * 4 + ec] = nm_after[tt, ec]
            nr_after[tt, ec] = s_nr.inc()
        for tsub in range(4):
            mmv_after[tt, tsub] = s_mmv.inc()
            evv_after[tt, tsub] = s_evv.inc()
    s_nr.inc()

    @ctx.on("sync")
    def _(q):
        for dc in range(8):
            if dc >= 2:
                q.wait_ge(s_wcv.h, wcv_after[dc - 2])
            q.dma_start(out=wld[dc % 2][:],
                        in_=wqkvT[dc * 128:(dc + 1) * 128, :]
                        ).then_inc(s_wld[dc % 2].h, 16)
        for tt in range(NT_S):
            if tt >= 2:
                q.wait_ge(s_xcv.h, xcv_after[tt - 2])
            col0 = PAD + tt * TT
            q.dma_start(
                out=xf[tt % 2][:],
                in_=xt.rearrange("(dc p) n -> p dc n", p=128)[:, :, col0:col0 + TT],
            ).then_inc(s_x[tt % 2].h, 16)

    @ctx.on("vector")
    def _(v):
        v.memset(ones2[:], 0.0)
        v.memset(ones2[0:64, 0:1], 1.0)
        v.memset(ones2[64:128, 1:2], 1.0)
        for h in range(HPC):
            v.memset(vkm[:, :, h * 65 + 64], 1.0)
        for dc in range(8):
            v.wait_ge(s_wld[dc % 2].h, wld_after[dc])
            v.tensor_copy(wq_bf[:, dc, :], wld[dc % 2][:]).then_inc(s_wcv.h, 1)
        for tt in range(NT_S):
            v.wait_ge(s_x[tt % 2].h, x_after[tt])
            if tt >= 2:
                v.wait_ge(s_mmv.h, mmv_after[tt - 2, 3])
            v.tensor_copy(xb[tt % 2][:], xf[tt % 2][:]).then_inc(s_xcv.h, 1)
            for ec in range(4):
                v.wait_ge(s_nm.h, nm_after[tt, ec])
                v.reduce_max(nrm_all[0:2, ec, tt:tt + 1], ps_n[0:2, :],
                             axis=mybir.AxisListType.X).then_inc(s_nr.h, 1)
            for tsub in range(4):
                v.wait_ge(s_mmv.h, mmv_after[tt, tsub])
                kc = tt * 4 + tsub
                v.tensor_copy(
                    vkm[:, kc, :].rearrange("p (h j) -> p h j", j=65)[:, :, 0:64],
                    ps_v[tsub % 2][:].rearrange("p (h j) -> p h j", j=64),
                ).then_inc(s_evv.h, 1)
        v.reduce_max(nrm[0:2, :], nrm_all[0:2, :, :],
                     axis=mybir.AxisListType.X).then_inc(s_nr.h, 1)

    @ctx.on("tensor")
    def _(t):
        t.wait_ge(s_wcv.h, 8)
        for tt in range(NT_S):
            t.wait_ge(s_xcv.h, xcv_after[tt])
            for ec in range(4):
                idx = tt * 4 + ec
                if idx >= 2:
                    t.wait_ge(s_sq.h, idx - 1)
                for dc in range(8):
                    ins = t.matmul(ps_qk[ec % 2][:],
                                   wq_bf[:, dc, ec * 128:(ec + 1) * 128],
                                   xb[tt % 2][:, dc, :],
                                   start=(dc == 0), stop=(dc == 7))
                    if dc == 7:
                        ins.then_inc(s_mm.h, 1)
                t.wait_ge(s_sq.h, sq_after[tt, ec])
                if idx >= 1:
                    t.wait_ge(s_nr.h, idx)
                t.matmul(ps_n[0:2, :], ones2[:], sqb[idx % 2][:],
                         start=True, stop=True).then_inc(s_nm.h, 1)
            for tsub in range(4):
                idx = tt * 4 + tsub
                if idx >= 2:
                    t.wait_ge(s_evv.h, idx - 1)
                for dc in range(8):
                    ins = t.matmul(ps_v[tsub % 2][:],
                                   xb[tt % 2][:, dc, tsub * 128:(tsub + 1) * 128],
                                   wq_bf[:, dc, 512:768],
                                   start=(dc == 0), stop=(dc == 7))
                    if dc == 7:
                        ins.then_inc(s_mmv.h, 1)

    @ctx.on("scalar")
    def _(s):
        for tt in range(NT_S):
            for ec in range(4):
                idx = tt * 4 + ec
                s.wait_ge(s_mm.h, mm_after[tt, ec])
                dst = qtT if ec < 2 else ktT
                c = ec % 2
                cols = slice(tt * TT, (tt + 1) * TT)
                s.activation(dst[:, c, cols], ps_qk[ec % 2][:],
                             AF.Identity).then_inc(s_ev.h, 1)
                if idx >= 2:
                    s.wait_ge(s_nm.h, nm_after_flat[idx - 2])
                s.activation(sqb[idx % 2][:], ps_qk[ec % 2][:],
                             AF.Square).then_inc(s_sq.h, 1)


def build_p1(core):
    import contextlib
    nc = bass.Bass("TRN2", target_bir_lowering=False)
    ap = {
        "xt": nc.dram_tensor("xt", [D, SP], F32, kind="ExternalInput").ap(),
        "wqkvT": nc.dram_tensor("wqkvT", [D, 768], F32, kind="ExternalInput").ap(),
    }
    kt_o = nc.dram_tensor("kt_o", [2, 128, S], BF16, kind="ExternalOutput").ap()
    qt_o = nc.dram_tensor("qt_o", [2, 128, S], BF16, kind="ExternalOutput").ap()
    vkm_o = nc.dram_tensor("vkm_o", [128, S // 128, HPC * 65], BF16,
                           kind="ExternalOutput").ap()
    nrm_o = nc.dram_tensor("nrm_o", [2, 4], F32, kind="ExternalOutput").ap()

    with contextlib.ExitStack() as stack:
        bufs = {
            "ktT": sbuf(stack, nc, "ktT", [128, 2, S], BF16),
            "qtT": sbuf(stack, nc, "qtT", [128, 2, S], BF16),
            "vkm": sbuf(stack, nc, "vkm", [128, S // 128, HPC * 65], BF16),
            "nrm_all": sbuf(stack, nc, "nrm_all", [2, 4, NT_S], F32),
            "nrm": sbuf(stack, nc, "nrm", [2, 4], F32),
            "p1_wbf": sbuf(stack, nc, "p1_wbf", [128, 8, 768], BF16),
            "p1_wld": [sbuf(stack, nc, f"p1_wld{i}", [128, 768], F32) for i in range(2)],
            "p1_xf": [sbuf(stack, nc, f"p1_xf{i}", [128, 8, TT], F32) for i in range(2)],
            "p1_xb": [sbuf(stack, nc, f"p1_xb{i}", [128, 8, TT], BF16) for i in range(2)],
            "p1_sqb": [sbuf(stack, nc, f"p1_sqb{i}", [128, TT], BF16) for i in range(2)],
            "ones2": sbuf(stack, nc, "ones2", [128, 2], BF16),
            "ps_a": psum(stack, nc, "ps_a", [128, TT]),
            "ps_b": psum(stack, nc, "ps_b", [128, TT]),
            "ps_c": psum(stack, nc, "ps_c", [128, HPC * 64]),
            "ps_d": psum(stack, nc, "ps_d", [128, HPC * 64]),
            "ps_n": psum(stack, nc, "ps_n", [2, TT]),
        }
        ctx = Ctx(nc, stack)
        emit_p1(ctx, bufs, ap)

        s_st = ctx.sem("p1_st")
        s_ev, s_evv, s_nr = ctx.sem("p1_ev"), ctx.sem("p1_evv"), ctx.sem("p1_nr")
        n_ev, n_evv, n_nr = s_ev.n, s_evv.n, s_nr.n

        @ctx.on("gpsimd")
        def _(g):
            g.wait_ge(s_ev.h, n_ev)
            g.wait_ge(s_evv.h, n_evv)
            g.wait_ge(s_nr.h, n_nr)
            for c in range(2):
                g.dma_start(out=kt_o[c], in_=bufs["ktT"][:, c, :]).then_inc(s_st.h, 16)
                g.dma_start(out=qt_o[c], in_=bufs["qtT"][:, c, :]).then_inc(s_st.h, 16)
            g.dma_start(out=vkm_o[:], in_=bufs["vkm"][:]).then_inc(s_st.h, 16)
            g.dma_start(out=nrm_o[:], in_=bufs["nrm"][:]).then_inc(s_st.h, 16)
            g.wait_ge(s_st.h, 16 * 6)

        with nc.Block() as block:
            ctx.finalize(block)
    return nc


def emit_p2(ctx, bufs, ap):
    nc = ctx.nc
    ktT, qtT, vkm, nrm = bufs["ktT"], bufs["qtT"], bufs["vkm"], bufs["nrm"]
    attnT = bufs["attnT"]
    dm = bufs["l0dm"]
    ones32 = bufs["ones32"]
    sel = bufs["sel2"]
    negbg = bufs["negbg"]
    rows = bufs["p2_rows"]
    den = [bufs["p2_den0"], bufs["p2_den1"]]
    bcs = [bufs["p2_bcs0"], bufs["p2_bcs1"]]
    P = [bufs["p2_p0"], bufs["p2_p1"]]
    ps_s = [bufs["ps_a"], bufs["ps_b"]]
    ps_av = [bufs["ps_e"], bufs["ps_f"]]
    ps_bc = bufs["ps_g"]
    ps_r = bufs["ps_n"]

    s_bg = ctx.sem("p2_bg")
    s_vf = ctx.sem("vfence")
    s_mm = ctx.sem("p2_mm")
    s_exp = ctx.sem("p2_exp")
    s_pr = ctx.sem("p2_pr")
    s_avm = ctx.sem("p2_avm")
    s_rc = ctx.sem("p2_rc")
    s_bc = ctx.sem("p2_bc")
    s_bce = ctx.sem("p2_bce")
    s_no = ctx.sem("p2_no")

    hqs = [(h, qt) for h in range(HPC) for qt in range(NT_S)]
    nch = {(h, qt): 4 * (qt + 1) for h, qt in hqs}
    hq_i = {hq: i for i, hq in enumerate(hqs)}

    def is_diag(h, qt, kc):
        return kc >= 4 * qt

    mm_after, exp_after, pr_after, avm_after = {}, {}, {}, {}
    av_after, rc_after, bc_after, bce_after, no_after = {}, {}, {}, {}, {}
    gidx = {}
    g = 0
    for hq in hqs:
        for kc in range(nch[hq]):
            gidx[hq, kc] = g
            mm_after[g] = s_mm.inc()
            if is_diag(*hq, kc):
                exp_after[g] = s_exp.inc()
            pr_after[g] = s_pr.inc()
            avm_after[g] = s_avm.inc()
            g += 1
    for hq in hqs:
        rc_after[hq] = s_rc.inc()
        bc_after[hq] = s_bc.inc()
        bce_after[hq] = s_bce.inc()
        no_after[hq] = s_no.inc()

    @ctx.on("tensor")
    def _(t):
        t.matmul(ps_r[0:1, 0:4], sel[:, 0:1], nrm[:, :], start=True, stop=True)
        t.matmul(ps_bc[0:1, 0:4], sel[:, 1:2], nrm[:, :], start=True,
                 stop=True).then_inc(s_bg.h, 2)
        t.wait_ge(s_bg.h, 6)
        t.matmul(ps_bc[0:64, 0:4], ones32[0:1, 0:64], rows[0:1, 4:8],
                 start=True, stop=True).then_inc(s_bg.h, 1)
        t.wait_ge(s_bg.h, 8)
        for hq in hqs:
            h, qt = hq
            i = hq_i[hq]
            c, pp = h // 2, 64 * (h % 2)
            for kc in range(nch[hq]):
                gg = gidx[hq, kc]
                if gg >= 2:
                    t.wait_ge(s_pr.h, pr_after[gg - 2])
                t.matmul(ps_s[gg % 2][:],
                         ktT[pp:pp + 64, c, kc * 128:(kc + 1) * 128],
                         qtT[pp:pp + 64, c, qt * TT:(qt + 1) * TT],
                         start=True, stop=True).then_inc(s_mm.h, 1)
                t.wait_ge(s_pr.h, pr_after[gg])
                if kc == 0 and i >= 2:
                    t.wait_ge(s_no.h, no_after[hqs[i - 2]])
                t.matmul(ps_av[i % 2][:], vkm[:, kc, h * 65:(h + 1) * 65],
                         P[gg % 2][:], start=(kc == 0),
                         stop=(kc == nch[hq] - 1),
                         skip_group_check=True).then_inc(s_avm.h, 1)
            t.wait_ge(s_rc.h, rc_after[hq])
            if i >= 1:
                t.wait_ge(s_bce.h, bce_after[hqs[i - 1]])
            t.matmul(ps_bc[:], ones32[0:1, 0:64], den[i % 2][:],
                     start=True, stop=True).then_inc(s_bc.h, 1)

    @ctx.on("vector")
    def _(v):
        v.wait_ge(s_bg.h, 2)
        v.tensor_copy(rows[0:1, 8:12], ps_r[0:1, 0:4])
        v.tensor_copy(rows[0:1, 12:16], ps_bc[0:1, 0:4]).then_inc(s_vf.h, 1)
        v.wait_ge(s_vf.h, s_vf.inc())
        v.tensor_mul(rows[0:1, 0:2], rows[0:1, 8:10], rows[0:1, 10:12])
        v.tensor_mul(rows[0:1, 2:4], rows[0:1, 12:14],
                     rows[0:1, 14:16]).then_inc(s_bg.h, 2)
        for hq in hqs:
            h, qt = hq
            i = hq_i[hq]
            for kc in range(4 * qt, nch[hq]):
                gg = gidx[hq, kc]
                r = kc - 4 * qt
                v.wait_ge(s_exp.h, exp_after[gg])
                v.tensor_mul(P[gg % 2][:], P[gg % 2][:], dm[:, r, :]).then_inc(s_pr.h, 1)
            v.wait_ge(s_avm.h, avm_after[gidx[hq, nch[hq] - 1]])
            if i >= 2:
                v.wait_ge(s_bc.h, bc_after[hqs[i - 2]])
            v.reciprocal(den[i % 2][:], ps_av[i % 2][64:65, :]).then_inc(s_rc.h, 1)
            v.wait_ge(s_bce.h, bce_after[hq])
            c, pp = h // 2, 64 * (h % 2)
            v.tensor_mul(attnT[pp:pp + 64, c, qt * TT:(qt + 1) * TT],
                         ps_av[i % 2][0:64, :], bcs[i % 2][:]).then_inc(s_no.h, 1)

    @ctx.on("scalar")
    def _(s):
        s.wait_ge(s_bg.h, 4)
        s.activation(rows[0:1, 4:8:2], rows[0:1, 0:2], AF.Sqrt)
        s.activation(rows[0:1, 5:8:2], rows[0:1, 2:4], AF.Sqrt)
        s.mul(rows[0:1, 4:8], rows[0:1, 4:8], -0.125).then_inc(s_bg.h, 2)
        s.wait_ge(s_bg.h, 7)
        s.copy(negbg[0:64, :], ps_bc[0:64, 0:4])
        s.copy(negbg[64:128, :], ps_bc[0:64, 0:4]).then_inc(s_bg.h, 1)
        for hq in hqs:
            h, qt = hq
            i = hq_i[hq]
            for kc in range(nch[hq]):
                gg = gidx[hq, kc]
                s.wait_ge(s_mm.h, mm_after[gg])
                if gg >= 2:
                    s.wait_ge(s_avm.h, avm_after[gg - 2])
                ins = s.activation(P[gg % 2][:], ps_s[gg % 2][:], AF.Exp,
                                   bias=negbg[:, h:h + 1], scale=0.125)
                if is_diag(h, qt, kc):
                    ins.then_inc(s_exp.h, 1)
                else:
                    ins.then_inc(s_pr.h, 1)
            s.wait_ge(s_bc.h, bc_after[hq])
            if i >= 2:
                s.wait_ge(s_no.h, no_after[hqs[i - 2]])
            s.copy(bcs[i % 2][:], ps_bc[:]).then_inc(s_bce.h, 1)


def build_p2(core):
    import contextlib
    nc = bass.Bass("TRN2", target_bir_lowering=False)
    kt_i = nc.dram_tensor("kt_i", [2, 128, S], BF16, kind="ExternalInput").ap()
    qt_i = nc.dram_tensor("qt_i", [2, 128, S], BF16, kind="ExternalInput").ap()
    vkm_i = nc.dram_tensor("vkm_i", [128, S // 128, HPC * 65], BF16,
                           kind="ExternalInput").ap()
    nrm_i = nc.dram_tensor("nrm_i", [2, 4], F32, kind="ExternalInput").ap()
    l0dm_i = nc.dram_tensor("l0dm_i", [4, 128, TT], BF16, kind="ExternalInput").ap()
    sel2_i = nc.dram_tensor("sel2_i", [2, 2], F32, kind="ExternalInput").ap()
    attn_o = nc.dram_tensor("attn_o", [2, 128, S], BF16, kind="ExternalOutput").ap()

    with contextlib.ExitStack() as stack:
        bufs = {
            "ktT": sbuf(stack, nc, "ktT", [128, 2, S], BF16),
            "qtT": sbuf(stack, nc, "qtT", [128, 2, S], BF16),
            "vkm": sbuf(stack, nc, "vkm", [128, S // 128, HPC * 65], BF16),
            "nrm": sbuf(stack, nc, "nrm", [2, 4], F32),
            "attnT": sbuf(stack, nc, "attnT", [128, 2, S], BF16),
            "l0dm": sbuf(stack, nc, "l0dm", [128, 4, TT], BF16),
            "ones32": sbuf(stack, nc, "ones32", [1, 128], F32),
            "sel2": sbuf(stack, nc, "sel2", [2, 2], F32),
            "negbg": sbuf(stack, nc, "negbg", [128, 4], F32),
            "p2_rows": sbuf(stack, nc, "p2_rows", [1, 16], F32),
            "p2_den0": sbuf(stack, nc, "p2_den0", [1, TT], F32),
            "p2_den1": sbuf(stack, nc, "p2_den1", [1, TT], F32),
            "p2_bcs0": sbuf(stack, nc, "p2_bcs0", [64, TT], F32),
            "p2_bcs1": sbuf(stack, nc, "p2_bcs1", [64, TT], F32),
            "p2_p0": sbuf(stack, nc, "p2_p0", [128, TT], BF16),
            "p2_p1": sbuf(stack, nc, "p2_p1", [128, TT], BF16),
            "ps_a": psum(stack, nc, "ps_a", [128, TT]),
            "ps_b": psum(stack, nc, "ps_b", [128, TT]),
            "ps_e": psum(stack, nc, "ps_e", [65, TT]),
            "ps_f": psum(stack, nc, "ps_f", [65, TT]),
            "ps_g": psum(stack, nc, "ps_g", [64, TT]),
            "ps_n": psum(stack, nc, "ps_n", [2, TT]),
        }
        ctx = Ctx(nc, stack)
        s_ld = ctx.sem("p2_ld")

        @ctx.on("sync")
        def _(q):
            dm3 = bufs["l0dm"]
            q.dma_start(out=bufs["ktT"][:],
                        in_=kt_i.rearrange("c p t -> p c t")).then_inc(s_ld.h, 16)
            q.dma_start(out=bufs["qtT"][:],
                        in_=qt_i.rearrange("c p t -> p c t")).then_inc(s_ld.h, 16)
            q.dma_start(out=bufs["vkm"][:], in_=vkm_i[:]).then_inc(s_ld.h, 16)
            q.dma_start(out=bufs["nrm"][:], in_=nrm_i[:]).then_inc(s_ld.h, 16)
            q.dma_start(out=dm3[:], in_=l0dm_i.rearrange("r p t -> p r t")).then_inc(s_ld.h, 16)
            q.dma_start(out=bufs["sel2"][:], in_=sel2_i[:]).then_inc(s_ld.h, 16)

        @ctx.on("vector")
        def _(v):
            v.wait_ge(s_ld.h, 16 * 6)
            v.memset(bufs["ones32"][:], 1.0).then_inc(ctx.sem("p2_init").h, 1)

        @ctx.on("tensor")
        def _(t):
            t.wait_ge(ctx.sem("p2_init").h, 1)

        @ctx.on("scalar")
        def _(s):
            s.wait_ge(ctx.sem("p2_init").h, 1)
        ctx.sem("p2_init").inc()

        emit_p2(ctx, bufs, {})

        s_no = ctx.sem("p2_no")
        n_no = s_no.n
        s_st = ctx.sem("p2_st")

        @ctx.on("gpsimd")
        def _(g):
            g.wait_ge(s_no.h, n_no)
            g.dma_start(out=attn_o.rearrange("c p t -> p c t"),
                        in_=bufs["attnT"][:]).then_inc(s_st.h, 16)
            g.wait_ge(s_st.h, 16)

        with nc.Block() as block:
            ctx.finalize(block)
    return nc


def emit_proj_ln(ctx, tag, n_tiles, lhsW, rhs, res, out_f32, out_bf,
                 bufs, bias_col=None, rhs_load=None, pre_wait=None,
                 n_oc=8, n_dc=8, proj=True, y_src=None, y_ready=None,
                 res_sbuf=None, sync_gate=None):
    nc = ctx.nc
    ones1 = bufs["ones1_bf"]
    ones32 = bufs["ones32"]
    xt2 = bufs[tag + "_x2"]
    xb2 = bufs[tag + "_xb"]
    sqb = bufs[tag + "_sq"]
    rowm = bufs[tag + "_rowm"]
    rowq = bufs[tag + "_rowq"]
    rowv = bufs[tag + "_rowv"]
    rowsd = bufs[tag + "_rowsd"]
    rowr = bufs[tag + "_rowr"]
    rowt = bufs[tag + "_rowt"]
    if proj:
        ps_p = [bufs["ps_a"], bufs["ps_b"]]
    rsf = None if res_sbuf is not None else bufs[tag + "_rs"]
    ps_m, ps_q = bufs["ps_m"], bufs["ps_q"]
    ps_bm, ps_br = bufs["ps_bm"], bufs["ps_br"]

    s_in = [ctx.sem(tag + "_in0"), ctx.sem(tag + "_in1")]
    s_rs = [ctx.sem(tag + "_rs0"), ctx.sem(tag + "_rs1")]
    s_pm = ctx.sem(tag + "_pm")
    s_ye = ctx.sem(tag + "_ye")
    s_x2 = ctx.sem(tag + "_x2")
    s_cb = ctx.sem(tag + "_cb")
    s_sq = ctx.sem(tag + "_sq")
    s_mq = ctx.sem(tag + "_mq")
    s_st = ctx.sem(tag + "_st")
    s_rw = ctx.sem(tag + "_rw")
    s_bc = ctx.sem(tag + "_bc")
    s_ap = ctx.sem(tag + "_ap")
    s_vf = ctx.sem("vfence")
    s_so = [ctx.sem(tag + "_so0"), ctx.sem(tag + "_so1")]

    in_after, rs_after = {}, {}
    pm_after, ye_after, x2_after, cb_after = {}, {}, {}, {}
    sq_after, mq_after, st_after = {}, {}, {}
    rw_after, bc_after, ap_after, ap2_after, so_after = {}, {}, {}, {}, {}
    for nt in range(n_tiles):
        if rhs_load is not None:
            in_after[nt] = s_in[nt % 2].inc(16 * rhs_load[1])
        if rsf is not None:
            rs_after[nt] = s_rs[nt % 2].inc(16)
        for oc in range(n_oc):
            if proj:
                pm_after[nt, oc] = s_pm.inc()
                ye_after[nt, oc] = s_ye.inc()
            x2_after[nt, oc] = s_x2.inc()
        cb_after[nt] = s_cb.inc()
        for oc in range(n_oc):
            sq_after[nt, oc] = s_sq.inc()
            mq_after[nt, oc] = s_mq.inc()
        st_after[nt] = s_st.inc()
        rw_after[nt, 1] = s_rw.inc()
        rw_after[nt, 2] = s_rw.inc()
        rw_after[nt, 3] = s_rw.inc()
        bc_after[nt] = s_bc.inc(2)
        ap_after[nt] = s_ap.inc()
        ap2_after[nt] = s_ap.inc() if out_bf is not None else ap_after[nt]
        if out_f32 is not None:
            so_after[nt] = s_so[nt % 2].inc(16)

    @ctx.on("sync")
    def _(q):
        if sync_gate is not None:
            sync_gate(q)
        for nt in range(n_tiles):
            if rhs_load is not None:
                if nt >= 2:
                    q.wait_ge(s_pm.h, pm_after[nt - 2, n_oc - 1])
                rhs_load[0](q, nt, s_in[nt % 2].h)
            if rsf is not None:
                if nt >= 2:
                    q.wait_ge(s_x2.h, x2_after[nt - 2, n_oc - 1])
                q.dma_start(out=rsf[nt % 2][:],
                            in_=res(nt)).then_inc(s_rs[nt % 2].h, 16)

    @ctx.on("tensor")
    def _(t):
        if pre_wait is not None:
            pre_wait(t)
        for nt in range(n_tiles):
            if proj:
                if rhs_load is not None:
                    t.wait_ge(s_in[nt % 2].h, in_after[nt])
                for oc in range(n_oc):
                    idx = nt * n_oc + oc
                    if idx >= 2:
                        t.wait_ge(s_ye.h, idx - 1)
                    for dc in range(n_dc):
                        ins = t.matmul(ps_p[oc % 2][:], lhsW(dc, oc),
                                       rhs(nt, dc),
                                       start=(dc == 0), stop=(dc == n_dc - 1),
                                       skip_group_check=True)
                        if dc == n_dc - 1:
                            ins.then_inc(s_pm.h, 1)
            if nt >= 1:
                t.wait_ge(s_rw.h, rw_after[nt - 1, 1])
            t.wait_ge(s_cb.h, cb_after[nt])
            for oc in range(n_oc):
                ins = t.matmul(ps_m[:], ones1[:], xb2[nt % 2][:, oc, :],
                               start=(oc == 0), stop=(oc == n_oc - 1),
                               skip_group_check=True)
                if oc == n_oc - 1:
                    ins.then_inc(s_st.h, 1)
                t.wait_ge(s_sq.h, sq_after[nt, oc])
                t.matmul(ps_q[:], ones1[:], sqb[(nt * n_oc + oc) % 2][:],
                         start=(oc == 0), stop=(oc == n_oc - 1),
                         skip_group_check=True).then_inc(s_mq.h, 1)
            t.wait_ge(s_rw.h, rw_after[nt, 3])
            if nt >= 1:
                t.wait_ge(s_ap.h, ap_after[nt - 1])
            t.matmul(ps_bm[:], ones32[0:1, :], rowm[nt % 2][:],
                     start=True, stop=True)
            t.matmul(ps_br[:], ones32[0:1, :], rowr[nt % 2][:],
                     start=True, stop=True).then_inc(s_bc.h, 2)

    @ctx.on("scalar")
    def _(s):
        for nt in range(n_tiles):
            if proj:
                for oc in range(n_oc):
                    s.wait_ge(s_pm.h, pm_after[nt, oc])
                    if nt >= 2 and oc == 0:
                        if out_f32 is not None:
                            s.wait_ge(s_so[nt % 2].h, so_after[nt - 2])
                        s.wait_ge(s_ap.h, ap2_after[nt - 2])
                    if bias_col is not None:
                        s.activation(xt2[nt % 2][:, oc, :], ps_p[oc % 2][:],
                                     AF.Identity, bias=bias_col(oc)
                                     ).then_inc(s_ye.h, 1)
                    else:
                        s.activation(xt2[nt % 2][:, oc, :], ps_p[oc % 2][:],
                                     AF.Identity).then_inc(s_ye.h, 1)
            s.wait_ge(s_cb.h, cb_after[nt])
            for oc in range(n_oc):
                idx = nt * n_oc + oc
                if idx >= 2:
                    p_nt, p_oc = (idx - 2) // n_oc, (idx - 2) % n_oc
                    s.wait_ge(s_mq.h, mq_after[p_nt, p_oc])
                s.activation(sqb[idx % 2][:], xb2[nt % 2][:, oc, :],
                             AF.Square).then_inc(s_sq.h, 1)
            s.wait_ge(s_rw.h, rw_after[nt, 1])
            if nt >= 2:
                s.wait_ge(s_rw.h, rw_after[nt - 2, 3])
            s.activation(rowsd[nt % 2][:], rowv[nt % 2][:],
                         AF.Sqrt, bias=bufs["eps_col"][0:1, 0:1]
                         ).then_inc(s_rw.h, 1)

    @ctx.on("vector")
    def _(v):
        for nt in range(n_tiles):
            for oc in range(n_oc):
                if proj:
                    v.wait_ge(s_ye.h, ye_after[nt, oc])
                    ysrc = xt2[nt % 2][:, oc, :]
                else:
                    v.wait_ge(y_ready[0].h, y_ready[1][nt, oc])
                    ysrc = y_src(nt, oc)
                if oc == 0:
                    if rsf is not None:
                        v.wait_ge(s_rs[nt % 2].h, rs_after[nt])
                    if not proj and nt >= 2:
                        if out_f32 is not None:
                            v.wait_ge(s_so[nt % 2].h, so_after[nt - 2])
                        v.wait_ge(s_ap.h, ap2_after[nt - 2])
                rr = (rsf[nt % 2][:, oc, :] if res_sbuf is None
                      else res_sbuf(nt, oc))
                v.tensor_add(xt2[nt % 2][:, oc, :], ysrc, rr
                             ).then_inc(s_x2.h, 1)
            if nt >= 2:
                v.wait_ge(s_st.h, st_after[nt - 2])
                v.wait_ge(s_sq.h, sq_after[nt - 2, n_oc - 1])
            v.tensor_copy(xb2[nt % 2][:], xt2[nt % 2][:]).then_inc(s_cb.h, 1)
            v.wait_ge(s_st.h, st_after[nt])
            v.wait_ge(s_mq.h, mq_after[nt, n_oc - 1])
            if nt >= 2:
                v.wait_ge(s_bc.h, bc_after[nt - 2])
            v.tensor_scalar_mul(rowm[nt % 2][:], ps_m[0:1, :], 1.0 / D)
            v.tensor_scalar_mul(rowq[nt % 2][:], ps_q[0:1, :],
                                1.0 / D).then_inc(s_vf.h, 1)
            v.wait_ge(s_vf.h, s_vf.inc())
            v.tensor_mul(rowt[:], rowm[nt % 2][:], rowm[nt % 2][:])
            v.tensor_sub(rowv[nt % 2][:], rowq[nt % 2][:],
                         rowt[:]).then_inc(s_rw.h, 1)
            v.wait_ge(s_rw.h, rw_after[nt, 2])
            v.reciprocal(rowr[nt % 2][:], rowsd[nt % 2][:]).then_inc(s_rw.h, 1)
            v.wait_ge(s_bc.h, bc_after[nt])
            for oc in range(n_oc):
                v.tensor_sub(xt2[nt % 2][:, oc, :], xt2[nt % 2][:, oc, :],
                             ps_bm[:])
                ins = v.tensor_mul(xt2[nt % 2][:, oc, :],
                                   xt2[nt % 2][:, oc, :], ps_br[:])
                if oc == n_oc - 1:
                    ins.then_inc(s_ap.h, 1)
            if out_bf is not None:
                v.tensor_copy(out_bf(nt), xt2[nt % 2][:]).then_inc(s_ap.h, 1)

    if out_f32 is not None:
        @ctx.on("gpsimd")
        def _(g):
            for nt in range(n_tiles):
                g.wait_ge(s_ap.h, ap_after[nt])
                g.dma_start(out=out_f32(nt),
                            in_=xt2[nt % 2][:]).then_inc(s_so[nt % 2].h, 16)

    return dict(s_ap=s_ap, ap_after=ap_after, s_so0=s_so[0], s_so1=s_so[1],
                so_after=so_after, s_x2=s_x2, x2_after=x2_after)


def emit_p3m(ctx, bufs, ap):
    nc = ctx.nc
    wr_h, wi_h = ap["wr"], ap["wi"]
    cb_h, sb_h = ap["cb"], ap["sb"]
    c2_h, ns2_h = ap["c2b"], ap["ns2b"]
    wrb, wib, nwrb = bufs["m_wrb"], bufs["m_wib"], bufs["m_nwrb"]
    cbs, sbs = bufs["m_cb"], bufs["m_sb"]
    c2s, ns2s = bufs["m_c2"], bufs["m_ns2"]
    AT, BT = bufs["m_AT"], bufs["m_BT"]
    M = bufs["m_M"]
    wld = bufs["m_wld"]
    ps = [bufs["ps_a"], bufs["ps_b"]]

    s_wl = [ctx.sem("m_wl0"), ctx.sem("m_wl1")]
    s_wc = ctx.sem("m_wc")
    s_cl = ctx.sem("m_cl")
    s_m1 = ctx.sem("m_m1")
    s_e1 = ctx.sem("m_e1")
    s_m2 = ctx.sem("m_m2")
    s_e2 = ctx.sem("m_e2")

    wl_after = {}
    wc_after = {}
    for i in range(16):
        wl_after[i] = s_wl[i % 2].inc(16)
    wc_seq = {}
    tot = 0
    for i in range(16):
        tot += 2 if i < 8 else 1
        wc_seq[i] = tot
    s_wc.inc(24)
    cl_after = s_cl.inc(16 * 4)
    m1_after, e1_after = {}, {}
    g = 0
    for ab in range(2):
        for oc in range(8):
            for n2 in range(2):
                m1_after[ab, oc, n2] = s_m1.inc()
                e1_after[ab, oc, n2] = s_e1.inc()
                g += 1
    m2_after, e2_after = {}, {}
    for mc in range(8):
        for n2 in range(2):
            m2_after[mc, n2] = s_m2.inc()
            e2_after[mc, n2] = s_e2.inc()

    @ctx.on("sync")
    def _(q):
        q.dma_start(out=cbs[:], in_=cb_h.rearrange("(c p) n -> p c n", p=128)
                    ).then_inc(s_cl.h, 16)
        q.dma_start(out=sbs[:], in_=sb_h.rearrange("(c p) n -> p c n", p=128)
                    ).then_inc(s_cl.h, 16)
        q.dma_start(out=c2s[:], in_=c2_h.rearrange("(c p) n -> p c n", p=128)
                    ).then_inc(s_cl.h, 16)
        q.dma_start(out=ns2s[:], in_=ns2_h.rearrange("(c p) n -> p c n", p=128)
                    ).then_inc(s_cl.h, 16)
        for i in range(16):
            src = wr_h if i < 8 else wi_h
            dc = i % 8
            if i >= 2:
                q.wait_ge(s_wc.h, wc_seq[i - 2])
            q.dma_start(out=wld[i % 2][:],
                        in_=src[dc * 128:(dc + 1) * 128, :]
                        ).then_inc(s_wl[i % 2].h, 16)

    @ctx.on("vector")
    def _(v):
        for i in range(16):
            dc = i % 8
            v.wait_ge(s_wl[i % 2].h, wl_after[i])
            if i < 8:
                v.tensor_copy(wrb[:, dc, :], wld[i % 2][:]).then_inc(s_wc.h, 1)
                v.tensor_scalar_mul(nwrb[:, dc, :], wld[i % 2][:], -1.0
                                    ).then_inc(s_wc.h, 1)
            else:
                v.tensor_copy(wib[:, dc, :], wld[i % 2][:]).then_inc(s_wc.h, 1)
        for ab in range(2):
            for oc in range(8):
                for n2 in range(2):
                    v.wait_ge(s_m1.h, m1_after[ab, oc, n2])
                    dst = AT if ab == 0 else BT
                    gg = (ab * 16 + oc * 2 + n2)
                    v.tensor_copy(dst[:, oc, n2 * 512:(n2 + 1) * 512],
                                  ps[gg % 2][:]).then_inc(s_e1.h, 1)
        for mc in range(8):
            for n2 in range(2):
                v.wait_ge(s_m2.h, m2_after[mc, n2])
                gg = mc * 2 + n2
                v.tensor_copy(M[:, mc, n2 * 512:(n2 + 1) * 512],
                              ps[gg % 2][:]).then_inc(s_e2.h, 1)

    @ctx.on("tensor")
    def _(t):
        t.wait_ge(s_cl.h, cl_after)
        t.wait_ge(s_wc.h, 24)
        for ab in range(2):
            for oc in range(8):
                for n2 in range(2):
                    gg = ab * 16 + oc * 2 + n2
                    if gg >= 2:
                        t.wait_ge(s_e1.h, gg - 1)
                    lhs1 = wrb if ab == 0 else wib
                    lhs2 = wib if ab == 0 else nwrb
                    for dc in range(8):
                        t.matmul(ps[gg % 2][:], lhs1[:, dc, oc * 128:(oc + 1) * 128],
                                 cbs[:, dc, n2 * 512:(n2 + 1) * 512],
                                 start=(dc == 0), stop=False,
                                 skip_group_check=True)
                    for dc in range(8):
                        ins = t.matmul(ps[gg % 2][:],
                                       lhs2[:, dc, oc * 128:(oc + 1) * 128],
                                       sbs[:, dc, n2 * 512:(n2 + 1) * 512],
                                       start=False, stop=(dc == 7),
                                       skip_group_check=True)
                        if dc == 7:
                            ins.then_inc(s_m1.h, 1)
        t.wait_ge(s_e1.h, 32)
        for mc in range(8):
            for n2 in range(2):
                gg = mc * 2 + n2
                if gg >= 2:
                    t.wait_ge(s_e2.h, gg - 1)
                for oc in range(8):
                    t.matmul(ps[gg % 2][:], AT[:, oc, mc * 128:(mc + 1) * 128],
                             c2s[:, oc, n2 * 512:(n2 + 1) * 512],
                             start=(oc == 0), stop=False, skip_group_check=True)
                for oc in range(8):
                    ins = t.matmul(ps[gg % 2][:],
                                   BT[:, oc, mc * 128:(mc + 1) * 128],
                                   ns2s[:, oc, n2 * 512:(n2 + 1) * 512],
                                   start=False, stop=(oc == 7),
                                   skip_group_check=True)
                    if oc == 7:
                        ins.then_inc(s_m2.h, 1)

    return dict(s_e2=s_e2, n_e2=s_e2.n)


def ln_bufs(stack, nc, tag, n_oc=8):
    b = {}
    for nm, shape, dt in [
        ("_x2", [128, n_oc, TT], F32),
        ("_xb", [128, n_oc, TT], BF16), ("_rs", [128, n_oc, TT], F32),
        ("_sq", [128, TT], BF16),
        ("_rowm", [1, TT], F32), ("_rowq", [1, TT], F32),
        ("_rowv", [1, TT], F32), ("_rowsd", [1, TT], F32),
        ("_rowr", [1, TT], F32),
    ]:
        b[tag + nm] = [sbuf(stack, nc, f"{tag}{nm}{i}", shape, dt)
                       for i in range(2)]
    b[tag + "_rowt"] = sbuf(stack, nc, tag + "_rowt", [1, TT], F32)
    return b


def shared_ln_consts(stack, nc, bufs):
    bufs["ones1_bf"] = sbuf(stack, nc, "ones1_bf", [128, 1], BF16)
    bufs["ones32"] = sbuf(stack, nc, "ones32", [1, 128], F32)
    bufs["eps_col"] = sbuf(stack, nc, "eps_col", [1, 1], F32)
    bufs["ps_m"] = psum(stack, nc, "ps_m", [1, TT])
    bufs["ps_q"] = psum(stack, nc, "ps_q", [1, TT])
    bufs["ps_bm"] = psum(stack, nc, "ps_bm", [128, TT])
    bufs["ps_br"] = psum(stack, nc, "ps_br", [128, TT])


def emit_init_ln_consts(ctx, bufs):
    s_i = ctx.sem("ln_init")

    @ctx.on("vector")
    def _(v):
        v.memset(bufs["ones1_bf"][:], 1.0)
        v.memset(bufs["eps_col"][:], 1e-5)
        v.memset(bufs["ones32"][:], 1.0).then_inc(s_i.h, 1)
    s_i.inc()

    @ctx.on("tensor")
    def _(t):
        t.wait_ge(s_i.h, 1)

    @ctx.on("scalar")
    def _(s):
        s.wait_ge(s_i.h, 1)


def emit_weight_load(ctx, tag, src_ap, wbf, wld, n_chunks, width, prev=None):
    nb = len(wld)
    s_l = [ctx.sem(tag + f"_wl{i}") for i in range(nb)]
    s_c = ctx.sem(tag + "_wc")
    l_after = [s_l[i % nb].inc(16) for i in range(n_chunks)]
    c_after = [s_c.inc() for _ in range(n_chunks)]

    @ctx.on("sync")
    def _(q):
        for dc in range(n_chunks):
            if dc >= nb:
                q.wait_ge(s_c.h, c_after[dc - nb])
            elif prev is not None:
                q.wait_ge(prev[0].h, prev[1])
            q.dma_start(out=wld[dc % nb][:, 0:width],
                        in_=src_ap[dc * 128:(dc + 1) * 128, :]
                        ).then_inc(s_l[dc % nb].h, 16)

    @ctx.on("vector")
    def _(v):
        for dc in range(n_chunks):
            v.wait_ge(s_l[dc % nb].h, l_after[dc])
            v.tensor_copy(wbf[:, dc, :], wld[dc % nb][:, 0:width]).then_inc(s_c.h, 1)

    return s_c, n_chunks


def build_p3a(core):
    import contextlib
    nc = bass.Bass("TRN2", target_bir_lowering=False)
    ags = nc.dram_tensor("ags", [D, NTOK], BF16, kind="ExternalInput").ap()
    xs = nc.dram_tensor("xs", [D, NTOK], F32, kind="ExternalInput").ap()
    woutT = nc.dram_tensor("woutT", [D, D], F32, kind="ExternalInput").ap()
    st1f_o = nc.dram_tensor("st1f_o", [128, 8, NTOK], F32, kind="ExternalOutput").ap()
    st1b_o = nc.dram_tensor("st1b_o", [128, 8, NTOK], BF16, kind="ExternalOutput").ap()

    with contextlib.ExitStack() as stack:
        bufs = {
            "ps_a": psum(stack, nc, "ps_a", [128, TT]),
            "ps_b": psum(stack, nc, "ps_b", [128, TT]),
            "woutb": sbuf(stack, nc, "woutb", [128, 8, D], BF16),
            "wld": [sbuf(stack, nc, f"wld{i}", [128, D], F32) for i in range(2)],
            "agt": [sbuf(stack, nc, f"agt{i}", [128, 8, TT], BF16) for i in range(2)],
            "st1b": sbuf(stack, nc, "st1b", [128, 8, NTOK], BF16),
        }
        shared_ln_consts(stack, nc, bufs)
        bufs.update(ln_bufs(stack, nc, "p3a"))
        ctx = Ctx(nc, stack)
        emit_init_ln_consts(ctx, bufs)
        s_wc, n_wc = emit_weight_load(ctx, "p3a", woutT, bufs["woutb"],
                                      bufs["wld"], 8, D)

        agr = ags.rearrange("(c p) n -> p c n", p=128)
        xsr = xs.rearrange("(c p) n -> p c n", p=128)

        def rhs_load_fn(q, nt, semh):
            q.dma_start(out=bufs["agt"][nt % 2][:],
                        in_=agr[:, :, nt * TT:(nt + 1) * TT]).then_inc(semh, 16)

        h = emit_proj_ln(
            ctx, "p3a", NT_N,
            lhsW=lambda dc, oc: bufs["woutb"][:, dc, oc * 128:(oc + 1) * 128],
            rhs=lambda nt, dc: bufs["agt"][nt % 2][:, dc, :],
            rhs_load=(rhs_load_fn, 1),
            res=lambda nt: xsr[:, :, nt * TT:(nt + 1) * TT],
            out_f32=lambda nt: st1f_o[:, :, nt * TT:(nt + 1) * TT],
            out_bf=lambda nt: bufs["st1b"][:, :, nt * TT:(nt + 1) * TT],
            bufs=bufs,
            pre_wait=lambda t: t.wait_ge(s_wc.h, n_wc),
        )
        s_st = ctx.sem("p3a_store")
        s_ap, n_ap = h["s_ap"], h["s_ap"].n
        so0, so1 = h["s_so0"], h["s_so1"]
        n_so0, n_so1 = so0.n, so1.n

        @ctx.on("gpsimd")
        def _(g):
            g.wait_ge(s_ap.h, n_ap)
            g.dma_start(out=st1b_o[:], in_=bufs["st1b"][:]).then_inc(s_st.h, 16)
            g.wait_ge(s_st.h, 16)
            g.wait_ge(so0.h, n_so0)
            g.wait_ge(so1.h, n_so1)

        with nc.Block() as block:
            ctx.finalize(block)
    return nc


def build_p3m(core):
    import contextlib
    nc = bass.Bass("TRN2", target_bir_lowering=False)
    ap = {
        "wr": nc.dram_tensor("wr", [D, D], F32, kind="ExternalInput").ap(),
        "wi": nc.dram_tensor("wi", [D, D], F32, kind="ExternalInput").ap(),
        "cb": nc.dram_tensor("cb", [D, D], BF16, kind="ExternalInput").ap(),
        "sb": nc.dram_tensor("sb", [D, D], BF16, kind="ExternalInput").ap(),
        "c2b": nc.dram_tensor("c2b", [D, D], BF16, kind="ExternalInput").ap(),
        "ns2b": nc.dram_tensor("ns2b", [D, D], BF16, kind="ExternalInput").ap(),
    }
    m_o = nc.dram_tensor("m_o", [128, 8, D], BF16, kind="ExternalOutput").ap()

    with contextlib.ExitStack() as stack:
        bufs = {
            "ps_a": psum(stack, nc, "ps_a", [128, TT]),
            "ps_b": psum(stack, nc, "ps_b", [128, TT]),
        }
        for nm in ["m_wrb", "m_wib", "m_nwrb", "m_cb", "m_sb", "m_c2",
                   "m_ns2", "m_AT", "m_BT", "m_M"]:
            bufs[nm] = sbuf(stack, nc, nm, [128, 8, D], BF16)
        bufs["m_wld"] = [sbuf(stack, nc, f"m_wld{i}", [128, D], F32)
                         for i in range(2)]
        ctx = Ctx(nc, stack)
        h = emit_p3m(ctx, bufs, ap)
        s_e2, n_e2 = h["s_e2"], h["n_e2"]
        s_st = ctx.sem("p3m_store")

        @ctx.on("gpsimd")
        def _(g):
            g.wait_ge(s_e2.h, n_e2)
            g.dma_start(out=m_o[:], in_=bufs["m_M"][:]).then_inc(s_st.h, 16)
            g.wait_ge(s_st.h, 16)

        with nc.Block() as block:
            ctx.finalize(block)
    return nc


def build_p3c(core):
    import contextlib
    nc = bass.Bass("TRN2", target_bir_lowering=False)
    m_i = nc.dram_tensor("m_i", [128, 8, D], BF16, kind="ExternalInput").ap()
    st1f_i = nc.dram_tensor("st1f_i", [128, 8, NTOK], F32, kind="ExternalInput").ap()
    st1b_i = nc.dram_tensor("st1b_i", [128, 8, NTOK], BF16, kind="ExternalInput").ap()
    l0b_i = nc.dram_tensor("l0b_i", [128, 8], F32, kind="ExternalInput").ap()
    st2f_o = nc.dram_tensor("st2f_o", [128, 8, NTOK], F32, kind="ExternalOutput").ap()
    st2b_o = nc.dram_tensor("st2b_o", [128, 8, NTOK], BF16, kind="ExternalOutput").ap()

    with contextlib.ExitStack() as stack:
        bufs = {
            "ps_a": psum(stack, nc, "ps_a", [128, TT]),
            "ps_b": psum(stack, nc, "ps_b", [128, TT]),
            "m_M": sbuf(stack, nc, "m_M", [128, 8, D], BF16),
            "st1b": sbuf(stack, nc, "st1b", [128, 8, NTOK], BF16),
            "st2b": sbuf(stack, nc, "st2b", [128, 8, NTOK], BF16),
            "l0b": sbuf(stack, nc, "l0b", [128, 8], F32),
        }
        shared_ln_consts(stack, nc, bufs)
        bufs.update(ln_bufs(stack, nc, "p3c"))
        ctx = Ctx(nc, stack)
        emit_init_ln_consts(ctx, bufs)
        s_ld = ctx.sem("p3c_ld")

        @ctx.on("sync")
        def _(q):
            q.dma_start(out=bufs["m_M"][:], in_=m_i[:]).then_inc(s_ld.h, 16)
            q.dma_start(out=bufs["st1b"][:], in_=st1b_i[:]).then_inc(s_ld.h, 16)
            q.dma_start(out=bufs["l0b"][:], in_=l0b_i[:]).then_inc(s_ld.h, 16)

        h = emit_proj_ln(
            ctx, "p3c", NT_N,
            lhsW=lambda dc, oc: bufs["m_M"][:, dc, oc * 128:(oc + 1) * 128],
            rhs=lambda nt, dc: bufs["st1b"][:, dc, nt * TT:(nt + 1) * TT],
            rhs_load=None,
            res=lambda nt: st1f_i[:, :, nt * TT:(nt + 1) * TT],
            out_f32=lambda nt: st2f_o[:, :, nt * TT:(nt + 1) * TT],
            out_bf=lambda nt: bufs["st2b"][:, :, nt * TT:(nt + 1) * TT],
            bufs=bufs,
            bias_col=lambda oc: bufs["l0b"][:, oc:oc + 1],
            pre_wait=lambda t: t.wait_ge(s_ld.h, 48),
        )
        s_st = ctx.sem("p3c_store")
        s_ap, n_ap = h["s_ap"], h["s_ap"].n
        so0, so1 = h["s_so0"], h["s_so1"]
        n_so0, n_so1 = so0.n, so1.n

        @ctx.on("gpsimd")
        def _(g):
            g.wait_ge(s_ap.h, n_ap)
            g.dma_start(out=st2b_o[:], in_=bufs["st2b"][:]).then_inc(s_st.h, 16)
            g.wait_ge(s_st.h, 16)
            g.wait_ge(so0.h, n_so0)
            g.wait_ge(so1.h, n_so1)

        with nc.Block() as block:
            ctx.finalize(block)
    return nc


def aview(arena, byte_off, shape, dtype):
    es = mybir.dt.size(dtype)
    n = 1
    for s in shape[1:]:
        n *= s
    ap = arena[0:shape[0], byte_off:byte_off + n * es].bitcast(dtype)
    if len(shape) == 3:
        ap = ap.rearrange("p (a b) -> p a b", a=shape[1], b=shape[2])
    return ap


def emit_p4(ctx, bufs, ap):
    nc = ctx.nc
    st2b = bufs["st2b"]
    wqb, wkb, wvb = bufs["wq1b"], bufs["wk1b"], bufs["wv1b"]
    q1T = bufs["q1T"]
    k1T = bufs["k1T"]
    v1km = bufs["v1km"]
    bqc, bkc = bufs["bq_pc"], bufs["bk_pc"]
    bvrow = bufs["bv_row"]
    onesr = bufs["ones_r"]
    ones2 = bufs["ones2"]
    ones32 = bufs["ones32"]
    sel2 = bufs["sel2"]
    sqb = bufs["p4_sqn"]
    nq_all = bufs["p4_nqall"]
    nk_all = bufs["p4_nkall"]
    nrm1 = bufs["p4_nrm"]
    rows = bufs["p4_rows"]
    negbg = bufs["p4_negbg"]
    mask = bufs["p4_m0"]
    P = [bufs["p4_p0"], bufs["p4_p1"]]
    den = [bufs["p4_den0"], bufs["p4_den1"]]
    bcs = [bufs["p4_bcs0"], bufs["p4_bcs1"]]
    ps_p = [bufs["ps_a"], bufs["ps_b"]]
    ps_av = [bufs["ps_e"], bufs["ps_f"]]
    ps_bc = bufs["ps_g"]
    ps_n = bufs["ps_n"]
    ap_l1m = ap["l1m"]

    s_ld = ctx.sem("p4_ld")
    s_pm = ctx.sem("p4_pm")
    s_ev = ctx.sem("p4_ev")
    s_sq = ctx.sem("p4_sq")
    s_nm = ctx.sem("p4_nm")
    s_nr = ctx.sem("p4_nr")
    s_bg = ctx.sem("p4_bg")
    s_vf = ctx.sem("vfence")
    s_mk = ctx.sem("p4_mk")
    s_mm = ctx.sem("p4_mm")
    s_exp = ctx.sem("p4_exp")
    s_pr = ctx.sem("p4_pr")
    s_avm = ctx.sem("p4_avm")
    s_rc = ctx.sem("p4_rc")
    s_bc = ctx.sem("p4_bc")
    s_bce = ctx.sem("p4_bce")
    s_no = ctx.sem("p4_no")

    jobs = ([("q", nt, oc) for nt in range(NT_O) for oc in range(8)]
            + [("k", nt, oc) for nt in range(NT_N) for oc in range(2)]
            + [("v", t, 0) for t in range(NTOK // 128)])
    jidx = {job: j for j, job in enumerate(jobs)}
    norm_jobs = [job for job in jobs if job[0] in ("q", "k")]
    nsq = {job: i for i, job in enumerate(norm_jobs)}

    pm_after, ev_after = {}, {}
    sq_after, nm_after, nr_after = {}, {}, {}
    for job in jobs:
        pm_after[job] = s_pm.inc()
        ev_after[job] = s_ev.inc()
        if job[0] in ("q", "k"):
            sq_after[job] = s_sq.inc()
            nm_after[job] = s_nm.inc()
            nr_after[job] = s_nr.inc()
    last_job = jobs[-1]

    hqs = [(qt, h) for qt in range(NT_O) for h in range(NH)]
    hq_i = {hq: i for i, hq in enumerate(hqs)}
    mk_after = {qt: s_mk.inc(16) for qt in range(NT_O)}
    mm_after, exp_after, pr_after, avm_after = {}, {}, {}, {}
    rc_after, bc_after, bce_after, no_after = {}, {}, {}, {}
    gidx = {}
    g = 0
    for hq in hqs:
        for r in range(8):
            gidx[hq, r] = g
            mm_after[g] = s_mm.inc()
            exp_after[g] = s_exp.inc()
            pr_after[g] = s_pr.inc()
            avm_after[g] = s_avm.inc()
            g += 1
    for hq in hqs:
        rc_after[hq] = s_rc.inc()
        bc_after[hq] = s_bc.inc()
        bce_after[hq] = s_bce.inc()
        no_after[hq] = s_no.inc()

    @ctx.on("sync")
    def _(q):
        q.dma_start(out=bqc[:], in_=ap["bq_pc"]).then_inc(s_ld.h, 16)
        q.dma_start(out=bkc[:], in_=ap["bk_pc"]).then_inc(s_ld.h, 16)
        q.dma_start(out=sel2[:], in_=ap["sel2"]).then_inc(s_ld.h, 16)
        for qt in range(NT_O):
            if qt >= 1:
                q.wait_ge(s_pr.h, pr_after[gidx[(qt - 1, NH - 1), 7]])
            else:
                q.wait_ge(s_pm.h, pm_after[last_job])
            q.dma_start(out=mask[:],
                        in_=ap_l1m[qt].rearrange("r p t -> p r t")
                        ).then_inc(s_mk.h, 16)

    @ctx.on("tensor")
    def _(t):
        t.wait_ge(s_ld.h, 16 * 3)
        for j, job in enumerate(jobs):
            if j >= 2:
                prev = jobs[j - 2]
                t.wait_ge(s_ev.h, ev_after[prev])
                if prev[0] in ("q", "k"):
                    t.wait_ge(s_sq.h, sq_after[prev])
            kind, a, b_ = job
            if kind == "q":
                for dc in range(8):
                    ins = t.matmul(ps_p[j % 2][:],
                                   wqb[:, dc, b_ * 128:(b_ + 1) * 128],
                                   st2b[:, dc, HALO + a * TT:HALO + (a + 1) * TT],
                                   start=(dc == 0), stop=(dc == 7),
                                   skip_group_check=True)
            elif kind == "k":
                for dc in range(8):
                    ins = t.matmul(ps_p[j % 2][:],
                                   wkb[:, dc, b_ * 128:(b_ + 1) * 128],
                                   st2b[:, dc, a * TT:(a + 1) * TT],
                                   start=(dc == 0), stop=(dc == 7),
                                   skip_group_check=True)
            else:
                for dc in range(8):
                    t.matmul(ps_p[j % 2][:, 0:256],
                             st2b[:, dc, a * 128:(a + 1) * 128],
                             wvb[:, dc, :],
                             start=(dc == 0), stop=False, skip_group_check=True)
                ins = t.matmul(ps_p[j % 2][:, 0:256], onesr[0:1, :],
                               bvrow[0:1, :], start=False, stop=True,
                               skip_group_check=True)
            ins.then_inc(s_pm.h, 1)
            if kind in ("q", "k"):
                t.wait_ge(s_sq.h, sq_after[job])
                i = nsq[job]
                if i >= 1:
                    t.wait_ge(s_nr.h, i)
                t.matmul(ps_n[0:2, :], ones2[:], sqb[i % 2][:],
                         start=True, stop=True).then_inc(s_nm.h, 1)
        t.wait_ge(s_bg.h, 1)
        t.matmul(ps_n[0:1, 0:10], sel2[:, 0:1], nrm1[:, :],
                 start=True, stop=True)
        t.matmul(ps_bc[0:1, 0:10], sel2[:, 1:2], nrm1[:, :],
                 start=True, stop=True).then_inc(s_bg.h, 2)
        t.wait_ge(s_bg.h, 7)
        t.matmul(ps_bc[0:64, 0:16], ones32[0:1, 0:64], rows[0:1, 36:52],
                 start=True, stop=True).then_inc(s_bg.h, 1)
        t.wait_ge(s_bg.h, 9)
        for hq in hqs:
            qt, h = hq
            i = hq_i[hq]
            gq = h // 4
            cq, pq = l1_head_slot(h)
            for r in range(8):
                gg = gidx[hq, r]
                relk = qt * TT + 128 * r
                if gg >= 2:
                    t.wait_ge(s_exp.h, exp_after[gg - 2])
                else:
                    t.wait_ge(s_ev.h, ev_after[last_job])
                    t.wait_ge(s_sq.h, sq_after[norm_jobs[-1]])
                t.matmul(ps_p[gg % 2][:],
                         k1T[pq:pq + 64, gq // 2, relk:relk + 128],
                         q1T[pq:pq + 64, cq, qt * TT:(qt + 1) * TT],
                         start=True, stop=True,
                         skip_group_check=True).then_inc(s_mm.h, 1)
                t.wait_ge(s_pr.h, pr_after[gg])
                if r == 0 and i >= 2:
                    t.wait_ge(s_no.h, no_after[hqs[i - 2]])
                kc = 4 * qt + r
                t.matmul(ps_av[i % 2][0:65, :],
                         v1km[:, kc, gq * 65:(gq + 1) * 65],
                         P[gg % 2][:], start=(r == 0), stop=(r == 7),
                         skip_group_check=True).then_inc(s_avm.h, 1)
            t.wait_ge(s_rc.h, rc_after[hq])
            if i >= 1:
                t.wait_ge(s_bce.h, bce_after[hqs[i - 1]])
            t.matmul(ps_bc[:, :], ones32[0:1, 0:64], den[i % 2][:],
                     start=True, stop=True).then_inc(s_bc.h, 1)

    @ctx.on("scalar")
    def _(s):
        s.wait_ge(s_ld.h, 16 * 3)
        for j, job in enumerate(jobs):
            kind, a, b_ = job
            s.wait_ge(s_pm.h, pm_after[job])
            if kind == "q":
                s.activation(q1T[:, b_, a * TT:(a + 1) * TT],
                             ps_p[j % 2][:], AF.Identity,
                             bias=bqc[:, b_:b_ + 1]).then_inc(s_ev.h, 1)
                i = nsq[job]
                if i >= 2:
                    s.wait_ge(s_nm.h, nm_after[norm_jobs[i - 2]])
                s.activation(sqb[i % 2][:], q1T[:, b_, a * TT:(a + 1) * TT],
                             AF.Square).then_inc(s_sq.h, 1)
            elif kind == "k":
                s.activation(k1T[:, b_, a * TT:(a + 1) * TT],
                             ps_p[j % 2][:], AF.Identity,
                             bias=bkc[:, b_:b_ + 1]).then_inc(s_ev.h, 1)
                i = nsq[job]
                if i >= 2:
                    s.wait_ge(s_nm.h, nm_after[norm_jobs[i - 2]])
                s.activation(sqb[i % 2][:], k1T[:, b_, a * TT:(a + 1) * TT],
                             AF.Square).then_inc(s_sq.h, 1)
            else:
                s.activation(
                    v1km[:, a, :].rearrange("p (h j) -> p h j", j=65)[:, :, 0:64],
                    ps_p[j % 2][:, 0:256].rearrange("p (h j) -> p h j", j=64),
                    AF.Identity).then_inc(s_ev.h, 1)
        s.wait_ge(s_bg.h, 5)
        s.activation(rows[0:1, 36:52], rows[0:1, 20:36], AF.Sqrt)
        s.mul(rows[0:1, 36:52], rows[0:1, 36:52], -0.125).then_inc(s_bg.h, 2)
        s.wait_ge(s_bg.h, 8)
        s.copy(negbg[0:64, :], ps_bc[0:64, 0:16])
        s.copy(negbg[64:128, :], ps_bc[0:64, 0:16]).then_inc(s_bg.h, 1)
        for hq in hqs:
            qt, h = hq
            i = hq_i[hq]
            for r in range(8):
                gg = gidx[hq, r]
                s.wait_ge(s_mm.h, mm_after[gg])
                if gg >= 2:
                    s.wait_ge(s_avm.h, avm_after[gg - 2])
                s.activation(P[gg % 2][:], ps_p[gg % 2][:], AF.Exp,
                             bias=negbg[:, h:h + 1],
                             scale=0.125).then_inc(s_exp.h, 1)
            s.wait_ge(s_bc.h, bc_after[hq])
            if i >= 2:
                s.wait_ge(s_no.h, no_after[hqs[i - 2]])
            s.copy(bcs[i % 2][:], ps_bc[0:64, :]).then_inc(s_bce.h, 1)

    @ctx.on("vector")
    def _(v):
        v.memset(onesr[:], 1.0)
        v.memset(ones2[:], 0.0)
        v.memset(ones2[0:64, 0:1], 1.0)
        v.memset(ones2[64:128, 1:2], 1.0)
        for hh in range(NG):
            v.memset(v1km[:, :, hh * 65 + 64], 1.0)
        for job in norm_jobs:
            kind, a, b_ = job
            v.wait_ge(s_nm.h, nm_after[job])
            dst = (nq_all[0:2, b_, a:a + 1] if kind == "q"
                   else nk_all[0:2, b_, a:a + 1])
            v.reduce_max(dst, ps_n[0:2, :],
                         axis=mybir.AxisListType.X).then_inc(s_nr.h, 1)
        v.reduce_max(nrm1[0:2, 0:8], nq_all[0:2, :, :],
                     axis=mybir.AxisListType.X)
        v.reduce_max(nrm1[0:2, 8:10], nk_all[0:2, :, :],
                     axis=mybir.AxisListType.X).then_inc(s_bg.h, 1)
        v.wait_ge(s_bg.h, 3)
        v.tensor_copy(rows[0:1, 0:10], ps_n[0:1, 0:10])
        v.tensor_copy(rows[0:1, 10:20], ps_bc[0:1, 0:10]).then_inc(s_bg.h, 1)
        v.wait_ge(s_bg.h, 4)
        for h in range(NH):
            cq, pq = l1_head_slot(h)
            qsrc = (rows[0:1, cq:cq + 1] if pq == 0
                    else rows[0:1, 10 + cq:11 + cq])
            ins = v.tensor_copy(rows[0:1, 20 + h:21 + h], qsrc)
            if h == NH - 1:
                ins.then_inc(s_vf.h, 1)
        v.wait_ge(s_vf.h, s_vf.inc())
        for h in range(NH):
            gq = h // 4
            ksrc = (rows[0:1, 8 + gq // 2:9 + gq // 2] if gq % 2 == 0
                    else rows[0:1, 18 + gq // 2:19 + gq // 2])
            ins = v.tensor_mul(rows[0:1, 20 + h:21 + h],
                               rows[0:1, 20 + h:21 + h], ksrc)
            if h == NH - 1:
                ins.then_inc(s_bg.h, 1)
        for hq in hqs:
            qt, h = hq
            i = hq_i[hq]
            cq, pq = l1_head_slot(h)
            for r in range(8):
                gg = gidx[hq, r]
                v.wait_ge(s_exp.h, exp_after[gg])
                if h == 0 and r == 0:
                    v.wait_ge(s_mk.h, mk_after[qt])
                v.tensor_mul(P[gg % 2][:], P[gg % 2][:],
                             mask[:, r, :]).then_inc(s_pr.h, 1)
            v.wait_ge(s_avm.h, avm_after[gidx[hq, 7]])
            if i >= 2:
                v.wait_ge(s_bc.h, bc_after[hqs[i - 2]])
            v.reciprocal(den[i % 2][:], ps_av[i % 2][64:65, :]).then_inc(s_rc.h, 1)
            v.wait_ge(s_bce.h, bce_after[hq])
            v.tensor_mul(q1T[pq:pq + 64, cq, qt * TT:(qt + 1) * TT],
                         ps_av[i % 2][0:64, :], bcs[i % 2][:]).then_inc(s_no.h, 1)

    return dict(s_no=s_no, n_no=s_no.n, no_after=no_after, hqs=hqs)


def build_p4(core):
    import contextlib
    nc = bass.Bass("TRN2", target_bir_lowering=False)
    st2f_i = nc.dram_tensor("st2f_i", [128, 8, NTOK], F32, kind="ExternalInput").ap()
    st2b_i = nc.dram_tensor("st2b_i", [128, 8, NTOK], BF16, kind="ExternalInput").ap()
    wq1 = nc.dram_tensor("wq1", [D, D], F32, kind="ExternalInput").ap()
    wk1 = nc.dram_tensor("wk1", [D, 256], F32, kind="ExternalInput").ap()
    wv1 = nc.dram_tensor("wv1", [D, 256], F32, kind="ExternalInput").ap()
    wo1 = nc.dram_tensor("wo1", [D, D], F32, kind="ExternalInput").ap()
    bq_i = nc.dram_tensor("bq_pc", [128, 8], F32, kind="ExternalInput").ap()
    bk_i = nc.dram_tensor("bk_pc", [128, 2], F32, kind="ExternalInput").ap()
    bv_i = nc.dram_tensor("bv_row", [1, 256], F32, kind="ExternalInput").ap()
    bo_i = nc.dram_tensor("bo_pc", [128, 8], F32, kind="ExternalInput").ap()
    l1m = nc.dram_tensor("l1m", [NT_O, 8, 128, TT], BF16, kind="ExternalInput").ap()
    sel2_i = nc.dram_tensor("sel2_i", [2, 2], F32, kind="ExternalInput").ap()
    st3f_o = nc.dram_tensor("st3f_o", [128, 8, OWN], F32, kind="ExternalOutput").ap()
    st3b_o = nc.dram_tensor("st3b_o", [128, 8, OWN], BF16, kind="ExternalOutput").ap()

    KB = 1024
    with contextlib.ExitStack() as stack:
        ov1 = sbuf(stack, nc, "ov1", [128, 80 * KB], mybir.dt.uint8)
        ov2 = sbuf(stack, nc, "ov2", [128, 72 * KB], mybir.dt.uint8)
        bufs = {
            "ps_a": psum(stack, nc, "ps_a", [128, TT]),
            "ps_b": psum(stack, nc, "ps_b", [128, TT]),
            "ps_e": psum(stack, nc, "ps_e", [128, TT]),
            "ps_f": psum(stack, nc, "ps_f", [128, TT]),
            "ps_g": psum(stack, nc, "ps_g", [64, TT]),
            "ps_n": psum(stack, nc, "ps_n", [2, TT]),
            "q1T": sbuf(stack, nc, "q1T", [128, 8, OWN], BF16),
            "k1T": sbuf(stack, nc, "k1T", [128, 2, NTOK], BF16),
            "v1km": sbuf(stack, nc, "v1km", [128, NTOK // 128, NG * 65], BF16),
            "p4_nqall": sbuf(stack, nc, "p4_nqall", [2, 8, NT_O], F32),
            "p4_nkall": sbuf(stack, nc, "p4_nkall", [2, 2, NT_N], F32),
            "p4_nrm": sbuf(stack, nc, "p4_nrm", [2, 10], F32),
            "p4_rows": sbuf(stack, nc, "p4_rows", [1, 64], F32),
            "p4_negbg": sbuf(stack, nc, "p4_negbg", [128, 16], F32),
            "bq_pc": sbuf(stack, nc, "bq_pc_s", [128, 8], F32),
            "bk_pc": sbuf(stack, nc, "bk_pc_s", [128, 2], F32),
            "bv_f32": sbuf(stack, nc, "bv_f32", [1, 256], F32),
            "bv_row": sbuf(stack, nc, "bv_row_s", [1, 256], BF16),
            "bo_pc": sbuf(stack, nc, "bo_pc_s", [128, 8], F32),
            "ones_r": sbuf(stack, nc, "ones_r", [1, 128], BF16),
            "ones2": sbuf(stack, nc, "ones2", [128, 2], BF16),
            "sel2": sbuf(stack, nc, "sel2", [2, 2], F32),
        }
        bufs["st2b"] = aview(ov1, 0, [128, 8, NTOK], BF16)
        bufs["wq1b"] = aview(ov1, 40 * KB, [128, 8, D], BF16)
        bufs["wk1b"] = aview(ov1, 56 * KB, [128, 8, 256], BF16)
        bufs["wv1b"] = aview(ov1, 60 * KB, [128, 8, 256], BF16)
        bufs["p4_m0"] = aview(ov1, 0, [128, 8, TT], BF16)
        bufs["p4_p0"] = aview(ov1, 8 * KB, [128, TT], BF16)
        bufs["p4_p1"] = aview(ov1, 9 * KB, [128, TT], BF16)
        bufs["p4_den0"] = aview(ov1, 10 * KB, [1, TT], F32)
        bufs["p4_den1"] = aview(ov1, 12 * KB, [1, TT], F32)
        bufs["p4_bcs0"] = aview(ov1, 14 * KB, [64, TT], F32)
        bufs["p4_bcs1"] = aview(ov1, 16 * KB, [64, TT], F32)
        bufs["p4c_x2"] = [aview(ov1, i * 16 * KB, [128, 8, TT], F32) for i in range(2)]
        bufs["p4c_rs"] = [aview(ov1, 32 * KB + i * 16 * KB, [128, 8, TT], F32)
                          for i in range(2)]
        bufs["wo1b"] = aview(ov1, 64 * KB, [128, 8, D], BF16)
        bufs["p4_sqn"] = [aview(ov2, i * KB, [128, TT], BF16) for i in range(2)]
        bufs["wld"] = [aview(ov2, 2 * KB, [128, D], F32)]
        bufs["p4c_xb"] = [aview(ov2, i * 8 * KB, [128, 8, TT], BF16) for i in range(2)]
        bufs["p4c_sq"] = [aview(ov2, 16 * KB + i * KB, [128, TT], BF16) for i in range(2)]
        off = 18 * KB
        for nm in ["p4c_rowm", "p4c_rowq", "p4c_rowsd", "p4c_rowr"]:
            bufs[nm] = [aview(ov2, off, [1, TT], F32),
                        aview(ov2, off + 2 * KB, [1, TT], F32)]
            off += 4 * KB
        bufs["p4c_rowv"] = bufs["p4c_rowq"]
        bufs["p4c_rowt"] = aview(ov2, off, [1, TT], F32)
        off += 2 * KB
        bufs["st3b"] = aview(ov2, off, [128, 8, OWN], BF16)
        bufs["ones1_bf"] = sbuf(stack, nc, "ones1_bf", [128, 1], BF16)
        bufs["ones32"] = sbuf(stack, nc, "ones32", [1, 128], F32)
        bufs["eps_col"] = sbuf(stack, nc, "eps_col", [1, 1], F32)
        bufs["ps_m"] = bufs["ps_n"][0:1, :]
        bufs["ps_q"] = bufs["ps_g"][0:1, :]
        bufs["ps_bm"] = bufs["ps_e"]
        bufs["ps_br"] = bufs["ps_f"]

        ctx = Ctx(nc, stack)
        emit_init_ln_consts(ctx, bufs)
        s_ld0 = ctx.sem("p4_ld0")

        @ctx.on("sync")
        def _(q):
            q.dma_start(out=bufs["st2b"], in_=st2b_i[:]).then_inc(s_ld0.h, 16)
            q.dma_start(out=bufs["bv_f32"][:], in_=bv_i[:]).then_inc(s_ld0.h, 16)
            q.dma_start(out=bufs["bo_pc"][:], in_=bo_i[:]).then_inc(s_ld0.h, 16)

        @ctx.on("vector")
        def _(v):
            v.wait_ge(s_ld0.h, 48)
            v.tensor_copy(bufs["bv_row"][:], bufs["bv_f32"][:]).then_inc(s_ld0.h, 1)

        cq_, nq_ = emit_weight_load(ctx, "p4wq", wq1, bufs["wq1b"], bufs["wld"], 8, D)
        ck_, nk_ = emit_weight_load(ctx, "p4wk", wk1, bufs["wk1b"], bufs["wld"], 8, 256,
                                    prev=(cq_, nq_))
        cv_, nv_ = emit_weight_load(ctx, "p4wv", wv1, bufs["wv1b"], bufs["wld"], 8, 256,
                                    prev=(ck_, nk_))

        ap = {"l1m": l1m, "bq_pc": bq_i, "bk_pc": bk_i, "sel2": sel2_i}
        ctx.streams["tensor"].append(
            lambda t: (t.wait_ge(cq_.h, nq_), t.wait_ge(ck_.h, nk_),
                       t.wait_ge(cv_.h, nv_), t.wait_ge(s_ld0.h, 49)))
        h4 = emit_p4(ctx, bufs, ap)
        s_no, n_no = h4["s_no"], h4["n_no"]

        co_, no_cnt = emit_weight_load(ctx, "p4wo", wo1, bufs["wo1b"], bufs["wld"],
                                       8, D, prev=(cv_, nv_))

        h = emit_proj_ln(
            ctx, "p4c", NT_O,
            lhsW=lambda dc, oc: bufs["wo1b"][:, dc, oc * 128:(oc + 1) * 128],
            rhs=lambda nt, dc: bufs["q1T"][:, dc, nt * TT:(nt + 1) * TT],
            rhs_load=None,
            res=lambda nt: st2f_i[:, :, HALO + nt * TT:HALO + (nt + 1) * TT],
            out_f32=lambda nt: st3f_o[:, :, nt * TT:(nt + 1) * TT],
            out_bf=lambda nt: bufs["st3b"][:, :, nt * TT:(nt + 1) * TT],
            bufs=bufs,
            bias_col=lambda oc: bufs["bo_pc"][:, oc:oc + 1],
            pre_wait=lambda t: (t.wait_ge(s_no.h, n_no), t.wait_ge(co_.h, no_cnt)),
            sync_gate=lambda q: q.wait_ge(s_no.h, n_no),
        )
        s_ap, n_ap = h["s_ap"], h["s_ap"].n
        so0, so1 = h["s_so0"], h["s_so1"]
        n_so0, n_so1 = so0.n, so1.n
        s_st = ctx.sem("p4_store")

        @ctx.on("gpsimd")
        def _(g):
            g.wait_ge(s_ap.h, n_ap)
            g.dma_start(out=st3b_o[:], in_=bufs["st3b"]).then_inc(s_st.h, 16)
            g.wait_ge(s_st.h, 16)
            g.wait_ge(so0.h, n_so0)
            g.wait_ge(so1.h, n_so1)

        with nc.Block() as block:
            ctx.finalize(block)
    return nc


def build_p5w(core):
    import contextlib
    nc = bass.Bass("TRN2", target_bir_lowering=False)
    mats = [
        ("we1", [NE + 1, D, HF], "w1b"),
        ("we2", [NE + 1, D, HF], "w2b"),
        ("we3", [NE + 1, HF, D], "w3b"),
    ]
    with contextlib.ExitStack() as stack:
        ctx = Ctx(nc, stack)
        lds = [sbuf(stack, nc, f"c{i}", [128, HF], F32) for i in range(2)]
        obs = [sbuf(stack, nc, f"o{i}", [128, HF], BF16) for i in range(2)]
        s_l = [ctx.sem("w_l0"), ctx.sem("w_l1")]
        s_c = ctx.sem("w_c")
        s_s = [ctx.sem("w_s0"), ctx.sem("w_s1")]
        chunks = []
        for nm, shape, onm in mats:
            ap_in = nc.dram_tensor(nm, shape, F32, kind="ExternalInput").ap()
            ap_out = nc.dram_tensor(onm, shape, BF16, kind="ExternalOutput").ap()
            rows = shape[0] * shape[1]
            width = shape[2]
            fi = ap_in.rearrange("e r c -> (e r) c")
            fo = ap_out.rearrange("e r c -> (e r) c")
            nch = rows // 128
            for i in range(nch):
                chunks.append((fi[i * 128:(i + 1) * 128, :],
                               fo[i * 128:(i + 1) * 128, :], width))
        l_after = [s_l[i % 2].inc(16) for i in range(len(chunks))]
        c_after = [s_c.inc() for _ in range(len(chunks))]
        st_after = [s_s[i % 2].inc(16) for i in range(len(chunks))]

        @ctx.on("sync")
        def _(q):
            for i, (fi, fo, w) in enumerate(chunks):
                if i >= 2:
                    q.wait_ge(s_c.h, c_after[i - 2])
                q.dma_start(out=lds[i % 2][:, 0:w], in_=fi).then_inc(s_l[i % 2].h, 16)

        @ctx.on("vector")
        def _(v):
            for i, (fi, fo, w) in enumerate(chunks):
                v.wait_ge(s_l[i % 2].h, l_after[i])
                if i >= 2:
                    v.wait_ge(s_s[i % 2].h, st_after[i - 2])
                v.tensor_copy(obs[i % 2][:, 0:w], lds[i % 2][:, 0:w]).then_inc(s_c.h, 1)

        @ctx.on("gpsimd")
        def _(g):
            for i, (fi, fo, w) in enumerate(chunks):
                g.wait_ge(s_c.h, c_after[i])
                g.dma_start(out=fo, in_=obs[i % 2][:, 0:w]).then_inc(s_s[i % 2].h, 16)
            g.wait_ge(s_s[0].h, s_s[0].n)
            g.wait_ge(s_s[1].h, s_s[1].n)

        with nc.Block() as block:
            ctx.finalize(block)
    return nc


HSC_LIST = [(0, 1024), (1024, 1024), (2048, 682)]


def _hcs(width):
    out = []
    off = 0
    while off < width:
        out.append((off, min(128, width - off)))
        off += 128
    return out


def emit_p5(ctx, bufs, ap):
    nc = ctx.nc
    st3b = bufs["st3b"]
    wgb = bufs["wgb"]
    wgf = bufs["wg_f32"]
    rbc = bufs["rbias_c"]
    eye8 = bufs["eye8"]
    eye128 = bufs["eye128"]
    sel8 = bufs["sel8"]
    gate8 = bufs["gate8"]
    gtm = bufs["gtm"]
    gsr = bufs["gsr"]
    gtmp = bufs["gtmp"]
    gexp = bufs["gexp"]
    gmsk = bufs["gmsk"]
    gden = bufs["gden"]
    wtsT = bufs["wtsT"]
    w1c, w2c, w3c = bufs["w1c"], bufs["w2c"], bufs["w3c"]
    gbuf = bufs["gbuf"]
    silu = bufs["silu"]
    wbc = bufs["wbc"]
    tmp5 = bufs["tmp5"]
    acc = bufs["acc"]
    ps_a, ps_b = bufs["ps_a"], bufs["ps_b"]
    ps_c = [bufs["ps_e"], bufs["ps_f"]]
    ps_w = bufs["ps_g"]
    ap_w1, ap_w2, ap_w3 = ap["w1b"], ap["w2b"], ap["w3b"]

    s_ld = ctx.sem("p5_ld")
    s_wgc = ctx.sem("p5_wgc")
    s_gmm = ctx.sem("p5_gmm")
    s_gev = ctx.sem("p5_gev")
    s_te1 = ctx.sem("p5_te1")
    s_srt = ctx.sem("p5_srt")
    s_te2 = ctx.sem("p5_te2")
    s_wl = [ctx.sem("p5_wl0"), ctx.sem("p5_wl1")]
    s_h1 = ctx.sem("p5_h1")
    s_sil = ctx.sem("p5_sil")
    s_h2 = ctx.sem("p5_h2")
    s_g = ctx.sem("p5_g")
    s_eo = ctx.sem("p5_eo")
    s_wb = ctx.sem("p5_wb")
    s_wbe = ctx.sem("p5_wbe")
    s_acc = ctx.sem("p5_acc")
    s_sub = ctx.sem("p5_sub")
    s_exp5 = ctx.sem("p5_exp")
    s_sl8 = ctx.sem("p5_sl8")
    s_vf = ctx.sem("vfence")

    NS = len(HSC_LIST)

    gmm_after = {nt: s_gmm.inc() for nt in range(NT_O)}
    gev_after = {nt: s_gev.inc() for nt in range(NT_O)}
    te1_after = {tt: s_te1.inc() for tt in range(16)}
    te2_after = {tt: s_te2.inc() for tt in range(16)}
    wtiles = [(e, si, m) for e in range(NE + 1) for si in range(NS)
              for m in range(3)]
    wl_after = {}
    wl_par = {}
    for i, (e, si, m) in enumerate(wtiles):
        hw = HSC_LIST[si][1]
        ndma = 2 if (m == 2 and hw % 128 != 0) else 1
        wl_par[e, si, m] = i % 2
        wl_after[e, si, m] = s_wl[i % 2].inc(16 * ndma)
    hgroups = [(e, si, nt, hi)
               for e in range(NE + 1) for si in range(NS)
               for nt in range(NT_O)
               for hi in range(len(_hcs(HSC_LIST[si][1])))]
    hg_i = {k: i for i, k in enumerate(hgroups)}
    h1_after, sil_after, h2_after, g_after = {}, {}, {}, {}
    for k in hgroups:
        h1_after[k] = s_h1.inc()
        sil_after[k] = s_sil.inc()
        h2_after[k] = s_h2.inc()
        g_after[k] = s_g.inc()
    gtiles = [(e, si, nt) for e in range(NE + 1) for si in range(NS)
              for nt in range(NT_O)]
    gt_i = {k: i for i, k in enumerate(gtiles)}
    eoops = [(e, si, nt, dc) for e in range(NE + 1) for si in range(NS)
             for nt in range(NT_O) for dc in range(8)]
    eo_i = {k: i for i, k in enumerate(eoops)}
    eo_after, acc_after = {}, {}
    for k in eoops:
        eo_after[k] = s_eo.inc()
        acc_after[k] = s_acc.inc()
    n_acc_total = s_acc.n
    wslots = [(e, nt) for e in range(1, NE + 1) for nt in range(NT_O)]
    wb_after = {k: s_wb.inc() for k in wslots}
    wbe_after = {k: s_wbe.inc() for k in wslots}

    def acc_done_for(nt, oc):
        return acc_after[NE, NS - 1, nt, oc]

    @ctx.on("sync")
    def _(q):
        q.dma_start(out=wgf[:], in_=ap["wgT"].rearrange("(dc p) e -> p dc e", p=128)
                    ).then_inc(s_ld.h, 16)
        q.dma_start(out=rbc[:], in_=ap["rbias_c"]).then_inc(s_ld.h, 16)
        q.dma_start(out=eye8[:], in_=ap["eye8"]).then_inc(s_ld.h, 16)
        q.dma_start(out=eye128[:], in_=ap["eye128"]).then_inc(s_ld.h, 16)
        q.wait_ge(s_te2.h, te2_after[15])
        q.dma_start(out=sel8[:], in_=ap["sel8"]).then_inc(s_sl8.h, 16)
        if _P5_NO_MOE:
            return
        for e in range(NE + 1):
            for si in range(NS):
                h0, hw = HSC_LIST[si]
                for m in range(3):
                    i = wtiles.index((e, si, m))
                    if i >= 3:
                        pe, psi, _ = wtiles[i - 3]
                        q.wait_ge(s_eo.h, eo_after[pe, psi, NT_O - 1, 7])
                    par = wl_par[e, si, m]
                    if m < 2:
                        w = ap_w1 if m == 0 else ap_w2
                        dst = w1c if m == 0 else w2c
                        q.dma_start(
                            out=dst[:, :, 0:hw],
                            in_=w[e].rearrange("(dc p) h -> p dc h", p=128)[:, :, h0:h0 + hw],
                        ).then_inc(s_wl[par].h, 16)
                    else:
                        full = (hw // 128) * 128
                        ins = q.dma_start(
                            out=w3c[:, 0:hw // 128, :],
                            in_=ap_w3[e][h0:h0 + full, :].rearrange(
                                "(hc p) d -> p hc d", p=128))
                        if hw % 128 != 0:
                            ins.then_inc(s_wl[par].h, 16)
                            q.dma_start(
                                out=w3c[0:hw - full, hw // 128, :],
                                in_=ap_w3[e][h0 + full:h0 + hw, :],
                            ).then_inc(s_wl[par].h, 16)
                        else:
                            ins.then_inc(s_wl[par].h, 16)

    @ctx.on("tensor")
    def _(t):
        pre = bufs.get("p5_pre_wait")
        if pre is not None:
            pre(t)
        t.wait_ge(s_ld.h, 16 * 4)
        t.wait_ge(s_wgc.h, 1)
        for nt in range(NT_O):
            if nt >= 1:
                t.wait_ge(s_gev.h, gev_after[nt - 1])
            for dc in range(8):
                ins = t.matmul(ps_w[0:8, :], wgb[:, dc, :],
                               st3b[:, dc, nt * TT:(nt + 1) * TT],
                               start=(dc == 0), stop=(dc == 7),
                               skip_group_check=True)
            ins.then_inc(s_gmm.h, 1)
        t.wait_ge(s_gev.h, gev_after[NT_O - 1])
        for tt in range(16):
            if tt >= 2:
                t.wait_ge(s_te1.h, te1_after[tt - 2])
            pbank = ps_a if tt % 2 == 0 else ps_b
            t.transpose(pbank[0:128, 0:8], gate8[0:8, tt * 128:(tt + 1) * 128],
                        eye8[:, :]).then_inc(s_gmm.h, 1)
        t.wait_ge(s_srt.h, 1)
        for tt in range(16):
            if tt >= 2:
                t.wait_ge(s_te2.h, te2_after[tt - 2])
            pbank = ps_a if tt % 2 == 0 else ps_b
            t.transpose(pbank[0:8, 0:128], gtm[:, tt, :],
                        eye128[:, :]).then_inc(s_gmm.h, 1)
        if _P5_NO_MOE:
            return
        t.wait_ge(s_te2.h, te2_after[15])
        t.wait_ge(s_sl8.h, 16)
        for e in range(NE + 1):
            for si in range(NS):
                h0, hw = HSC_LIST[si]
                hcs = _hcs(hw)
                t.wait_ge(s_wl[wl_par[e, si, 0]].h, wl_after[e, si, 0])
                t.wait_ge(s_wl[wl_par[e, si, 1]].h, wl_after[e, si, 1])
                t.wait_ge(s_wl[wl_par[e, si, 2]].h, wl_after[e, si, 2])
                for nt in range(NT_O):
                    if e >= 1 and si == 0:
                        wi = wslots.index((e, nt))
                        if wi >= 1:
                            t.wait_ge(s_wbe.h, wbe_after[wslots[wi - 1]])
                        t.matmul(ps_w[:, :], sel8[:, (e - 1) * 128:e * 128],
                                 wtsT[:, nt * TT:(nt + 1) * TT],
                                 start=True, stop=True,
                                 skip_group_check=True).then_inc(s_wb.h, 1)
                    for hi, (hco, hcw) in enumerate(hcs):
                        k = (e, si, nt, hi)
                        ih = hg_i[k]
                        if ih >= 1:
                            t.wait_ge(s_sil.h, sil_after[hgroups[ih - 1]])
                        for dc in range(8):
                            ins = t.matmul(
                                ps_a[0:hcw, :], w1c[:, dc, hco:hco + hcw],
                                st3b[:, dc, nt * TT:(nt + 1) * TT],
                                start=(dc == 0), stop=(dc == 7),
                                skip_group_check=True)
                        ins.then_inc(s_h1.h, 1)
                        if ih >= 1:
                            t.wait_ge(s_g.h, g_after[hgroups[ih - 1]])
                        for dc in range(8):
                            ins = t.matmul(
                                ps_b[0:hcw, :], w2c[:, dc, hco:hco + hcw],
                                st3b[:, dc, nt * TT:(nt + 1) * TT],
                                start=(dc == 0), stop=(dc == 7),
                                skip_group_check=True)
                        ins.then_inc(s_h2.h, 1)
                    for dc in range(8):
                        k = (e, si, nt, dc)
                        ie = eo_i[k]
                        if ie >= 2:
                            t.wait_ge(s_acc.h, acc_after[eoops[ie - 2]])
                        for hi, (hco, hcw) in enumerate(hcs):
                            t.wait_ge(s_g.h, g_after[e, si, nt, hi])
                            ins = t.matmul(
                                ps_c[ie % 2][0:128, :],
                                w3c[0:hcw, hi, dc * 128:(dc + 1) * 128],
                                gbuf[nt % 2][0:hcw, hi, :],
                                start=(hi == 0), stop=(hi == len(hcs) - 1),
                                skip_group_check=True)
                        ins.then_inc(s_eo.h, 1)

    @ctx.on("scalar")
    def _(s):
        s.wait_ge(s_ld.h, 16 * 4)
        for nt in range(NT_O):
            s.wait_ge(s_gmm.h, gmm_after[nt])
            s.activation(gate8[0:8, nt * TT:(nt + 1) * TT], ps_w[0:8, :],
                         AF.Identity, bias=rbc[0:8, 0:1]).then_inc(s_gev.h, 1)
        s.wait_ge(s_sub.h, 1)
        s.activation(gexp[:], gexp[:], AF.Exp).then_inc(s_exp5.h, 1)
        if _P5_NO_MOE:
            return
        for e in range(NE + 1):
            for si in range(NS):
                hcs = _hcs(HSC_LIST[si][1])
                for nt in range(NT_O):
                    if e >= 1 and si == 0:
                        s.wait_ge(s_wb.h, wb_after[e, nt])
                        if e >= 2:
                            s.wait_ge(s_acc.h, acc_after[e - 1, NS - 1, nt, 7])
                        s.copy(wbc[nt][:], ps_w[:, :]).then_inc(s_wbe.h, 1)
                    for hi, (hco, hcw) in enumerate(hcs):
                        k = (e, si, nt, hi)
                        ih = hg_i[k]
                        s.wait_ge(s_h1.h, h1_after[k])
                        if ih >= 2:
                            s.wait_ge(s_g.h, g_after[hgroups[ih - 2]])
                        s.activation(silu[ih % 2][0:hcw, :], ps_a[0:hcw, :],
                                     AF.Silu).then_inc(s_sil.h, 1)

    @ctx.on("vector")
    def _(v):
        v.wait_ge(s_ld.h, 16 * 4)
        v.tensor_copy(wgb[:], wgf[:]).then_inc(s_wgc.h, 1)
        for tt in range(16):
            v.wait_ge(s_gmm.h, gmm_after[NT_O - 1] + tt + 1)
            pbank = ps_a if tt % 2 == 0 else ps_b
            v.tensor_copy(gtm[:, tt, :], pbank[0:128, 0:8]).then_inc(s_te1.h, 1)
        v.tensor_copy(gsr[:], gtm[:]).then_inc(s_vf.h, 1)
        v.wait_ge(s_vf.h, s_vf.inc())
        for rnd in range(8):
            pairs = ([(0, 1), (2, 3), (4, 5), (6, 7)] if rnd % 2 == 0
                     else [(1, 2), (3, 4), (5, 6)])
            for (i, j) in pairs:
                v.tensor_max(gtmp[:], gsr[:, :, i], gsr[:, :, j])
                v.tensor_tensor(gsr[:, :, j], gsr[:, :, i], gsr[:, :, j],
                                ALU.min)
                v.tensor_copy(gsr[:, :, i], gtmp[:])
        v.tensor_copy(gtmp[:], gsr[:, :, 0]).then_inc(s_vf.h, 1)
        v.wait_ge(s_vf.h, s_vf.inc())
        for e in range(NE):
            v.tensor_tensor(gmsk[:, :, e], gtm[:, :, e], gsr[:, :, 3],
                            ALU.is_ge)
            ins = v.tensor_sub(gexp[:, :, e], gtm[:, :, e], gsr[:, :, 0])
            if e == NE - 1:
                ins.then_inc(s_sub.h, 1)
        v.wait_ge(s_exp5.h, 1)
        v.tensor_mul(gexp[:], gexp[:], gmsk[:]).then_inc(s_vf.h, 1)
        v.wait_ge(s_vf.h, s_vf.inc())
        v.reduce_sum(gden[:], gexp[:], axis=mybir.AxisListType.X).then_inc(s_vf.h, 1)
        v.wait_ge(s_vf.h, s_vf.inc())
        v.reciprocal(gden[:], gden[:]).then_inc(s_vf.h, 1)
        v.wait_ge(s_vf.h, s_vf.inc())
        for e in range(NE):
            v.tensor_mul(gtm[:, :, e], gexp[:, :, e], gden[:, :, 0]
                         ).then_inc(s_srt.h, 1) if e == NE - 1 else \
                v.tensor_mul(gtm[:, :, e], gexp[:, :, e], gden[:, :, 0])
        for tt in range(16):
            v.wait_ge(s_gmm.h, gmm_after[NT_O - 1] + 16 + tt + 1)
            pbank = ps_a if tt % 2 == 0 else ps_b
            v.tensor_copy(wtsT[0:8, tt * 128:(tt + 1) * 128],
                          pbank[0:8, 0:128]).then_inc(s_te2.h, 1)
        if _P5_NO_MOE:
            return
        for e in range(NE + 1):
            for si in range(NS):
                hcs = _hcs(HSC_LIST[si][1])
                for nt in range(NT_O):
                    for hi, (hco, hcw) in enumerate(hcs):
                        k = (e, si, nt, hi)
                        ih = hg_i[k]
                        v.wait_ge(s_sil.h, sil_after[k])
                        v.wait_ge(s_h2.h, h2_after[k])
                        ig = gt_i[e, si, nt]
                        if ig >= 2 and hi == 0:
                            v.wait_ge(s_eo.h, eo_after[gtiles[ig - 2] + (7,)])
                        v.tensor_mul(gbuf[nt % 2][0:hcw, hi, :],
                                     silu[ih % 2][0:hcw, :],
                                     ps_b[0:hcw, :]).then_inc(s_g.h, 1)
                    for dc in range(8):
                        k = (e, si, nt, dc)
                        ie = eo_i[k]
                        v.wait_ge(s_eo.h, eo_after[k])
                        cols = slice(nt * TT, (nt + 1) * TT)
                        if e == 0 and si == 0:
                            v.tensor_copy(acc[:, dc, cols],
                                          ps_c[ie % 2][:]).then_inc(s_acc.h, 1)
                        elif e == 0:
                            v.tensor_add(acc[:, dc, cols], acc[:, dc, cols],
                                         ps_c[ie % 2][:]).then_inc(s_acc.h, 1)
                        else:
                            if dc == 0:
                                v.wait_ge(s_wbe.h, wbe_after[e, nt])
                            v.tensor_mul(tmp5[dc % 2][:], ps_c[ie % 2][:],
                                         wbc[nt][:])
                            v.tensor_add(acc[:, dc, cols], acc[:, dc, cols],
                                         tmp5[dc % 2][:]).then_inc(s_acc.h, 1)

    return dict(s_acc=s_acc, n_acc=n_acc_total, acc_done_for=acc_done_for)


_P5_DEBUG = False
_P5_NO_MOE = False


def build_p5(core):
    import contextlib
    nc = bass.Bass("TRN2", target_bir_lowering=False)
    st3f_i = nc.dram_tensor("st3f_i", [128, 8, OWN], F32, kind="ExternalInput").ap()
    st3b_i = nc.dram_tensor("st3b_i", [128, 8, OWN], BF16, kind="ExternalInput").ap()
    wgT = nc.dram_tensor("wgT", [D, 8], F32, kind="ExternalInput").ap()
    rbias_c = nc.dram_tensor("rbias_c", [8, 1], F32, kind="ExternalInput").ap()
    eye8 = nc.dram_tensor("eye8", [8, 8], F32, kind="ExternalInput").ap()
    eye128 = nc.dram_tensor("eye128", [128, 128], F32, kind="ExternalInput").ap()
    sel8 = nc.dram_tensor("sel8", [8, 1024], F32, kind="ExternalInput").ap()
    w1b = nc.dram_tensor("w1b", [NE + 1, D, HF], BF16, kind="ExternalInput").ap()
    w2b = nc.dram_tensor("w2b", [NE + 1, D, HF], BF16, kind="ExternalInput").ap()
    w3b = nc.dram_tensor("w3b", [NE + 1, HF, D], BF16, kind="ExternalInput").ap()
    out_o = nc.dram_tensor("out_o", [128, 8, OWN], F32, kind="ExternalOutput").ap()

    KB = 1024
    with contextlib.ExitStack() as stack:
        ov = sbuf(stack, nc, "ov", [128, 96 * KB], mybir.dt.uint8)
        bufs = {
            "ps_a": psum(stack, nc, "ps_a", [128, TT]),
            "ps_b": psum(stack, nc, "ps_b", [128, TT]),
            "ps_e": psum(stack, nc, "ps_e", [128, TT]),
            "ps_f": psum(stack, nc, "ps_f", [128, TT]),
            "ps_g": psum(stack, nc, "ps_g", [128, TT]),
            "acc": sbuf(stack, nc, "acc", [128, 8, OWN], F32),
            "st3b": sbuf(stack, nc, "st3b", [128, 8, OWN], BF16),
            "wtsT": sbuf(stack, nc, "wtsT", [8, OWN], F32),
            "wgb": sbuf(stack, nc, "wgb", [128, 8, 8], BF16),
            "wg_f32": sbuf(stack, nc, "wg_f32", [128, 8, 8], F32),
            "rbias_c": sbuf(stack, nc, "rbias_s", [8, 1], F32),
            "ones1_bf": sbuf(stack, nc, "ones1_bf", [128, 1], BF16),
            "ones32": sbuf(stack, nc, "ones32", [1, 128], F32),
            "eps_col": sbuf(stack, nc, "eps_col", [1, 1], F32),
        }
        bufs["gate8"] = aview(ov, 0, [8, OWN], F32)
        bufs["gtm"] = aview(ov, 8 * KB, [128, 16, 8], F32)
        bufs["gsr"] = aview(ov, 9 * KB, [128, 16, 8], F32)
        bufs["gexp"] = aview(ov, 10 * KB, [128, 16, 8], F32)
        bufs["gmsk"] = aview(ov, 11 * KB, [128, 16, 8], F32)
        bufs["gtmp"] = aview(ov, 12 * KB, [128, 16], F32)
        bufs["gden"] = aview(ov, 13 * KB, [128, 16, 1], F32)
        bufs["eye128"] = aview(ov, 14 * KB, [128, 128], F32)
        bufs["eye8"] = aview(ov, 15 * KB, [8, 8], F32)
        bufs["sel8"] = aview(ov, 0, [8, 1024], F32)
        bufs["w1c"] = aview(ov, 16 * KB, [128, 8, 1024], BF16)
        bufs["w2c"] = aview(ov, 32 * KB, [128, 8, 1024], BF16)
        bufs["w3c"] = aview(ov, 48 * KB, [128, 8, 1024], BF16)
        bufs["gbuf"] = [aview(ov, 64 * KB + i * 8 * KB, [128, 8, TT], BF16)
                        for i in range(2)]
        bufs["silu"] = [aview(ov, 80 * KB + i * 2 * KB, [128, TT], F32)
                        for i in range(2)]
        bufs["wbc"] = [aview(ov, 84 * KB + i * 2 * KB, [128, TT], F32)
                       for i in range(4)]
        bufs["tmp5"] = [aview(ov, 92 * KB + i * 2 * KB, [128, TT], F32)
                        for i in range(2)]
        bufs["p5c_x2"] = [aview(ov, i * 16 * KB, [128, 8, TT], F32) for i in range(2)]
        bufs["p5c_rs"] = [aview(ov, 32 * KB + i * 16 * KB, [128, 8, TT], F32)
                          for i in range(2)]
        bufs["p5c_xb"] = [aview(ov, 64 * KB + i * 8 * KB, [128, 8, TT], BF16)
                          for i in range(2)]
        bufs["p5c_sq"] = [sbuf(stack, nc, f"p5c_sq{i}", [128, TT], BF16)
                          for i in range(2)]
        off = 80 * KB
        for nm in ["p5c_rowm", "p5c_rowq", "p5c_rowsd", "p5c_rowr"]:
            bufs[nm] = [aview(ov, off, [1, TT], F32),
                        aview(ov, off + 2 * KB, [1, TT], F32)]
            off += 4 * KB
        bufs["p5c_rowv"] = bufs["p5c_rowq"]
        bufs["p5c_rowt"] = sbuf(stack, nc, "p5c_rowt", [1, TT], F32)
        bufs["ps_m"] = bufs["ps_a"][0:1, :]
        bufs["ps_q"] = bufs["ps_b"][0:1, :]
        bufs["ps_bm"] = bufs["ps_e"]
        bufs["ps_br"] = bufs["ps_f"]

        ctx = Ctx(nc, stack)
        emit_init_ln_consts(ctx, bufs)
        s_ld0 = ctx.sem("p5_ld0")

        @ctx.on("sync")
        def _(q):
            q.dma_start(out=bufs["st3b"][:], in_=st3b_i[:]).then_inc(s_ld0.h, 16)

        ctx.streams["tensor"].append(lambda t: t.wait_ge(s_ld0.h, 16))

        ap = {"wgT": wgT, "rbias_c": rbias_c, "eye8": eye8, "eye128": eye128,
              "sel8": sel8, "w1b": w1b, "w2b": w2b, "w3b": w3b}
        h5 = emit_p5(ctx, bufs, ap)
        s_acc, n_acc = h5["s_acc"], h5["n_acc"]
        adf = h5["acc_done_for"]

        if _P5_NO_MOE:
            s_te2x = ctx.sem("p5_te2")
            nte2x = s_te2x.n
            dbg_g = nc.dram_tensor("dbg_gate", [8, OWN], F32,
                                   kind="ExternalOutput").ap()
            dbg_t = nc.dram_tensor("dbg_gtm", [128, 16, 8], F32,
                                   kind="ExternalOutput").ap()
            dbg_w = nc.dram_tensor("dbg_wts", [8, OWN], F32,
                                   kind="ExternalOutput").ap()
            dbg_sr = nc.dram_tensor("dbg_gsr", [128, 16, 8], F32,
                                    kind="ExternalOutput").ap()
            dbg_ex = nc.dram_tensor("dbg_gexp", [128, 16, 8], F32,
                                    kind="ExternalOutput").ap()
            dbg_mk = nc.dram_tensor("dbg_gmsk", [128, 16, 8], F32,
                                    kind="ExternalOutput").ap()
            dbg_dn = nc.dram_tensor("dbg_gden", [128, 16, 1], F32,
                                    kind="ExternalOutput").ap()
            s_dbg = ctx.sem("p5_dbg")
            s_gevd = ctx.sem("p5_gev")
            s_te1d = ctx.sem("p5_te1")
            s_srtd = ctx.sem("p5_srt")

            @ctx.on("gpsimd")
            def _(g):
                g.wait_ge(s_gevd.h, 4)
                g.dma_start(out=dbg_g[:], in_=bufs["gate8"]).then_inc(s_dbg.h, 16)
                g.wait_ge(s_te1d.h, 16)
                g.dma_start(out=dbg_t[:], in_=bufs["gtm"]).then_inc(s_dbg.h, 16)
                g.wait_ge(s_srtd.h, 1)
                g.dma_start(out=dbg_sr[:], in_=bufs["gsr"]).then_inc(s_dbg.h, 16)
                g.dma_start(out=dbg_ex[:], in_=bufs["gexp"]).then_inc(s_dbg.h, 16)
                g.dma_start(out=dbg_mk[:], in_=bufs["gmsk"]).then_inc(s_dbg.h, 16)
                g.dma_start(out=dbg_dn[:], in_=bufs["gden"]).then_inc(s_dbg.h, 16)
                g.wait_ge(s_te2x.h, nte2x)
                g.dma_start(out=dbg_w[:], in_=bufs["wtsT"][:]).then_inc(s_dbg.h, 16)
                g.wait_ge(s_dbg.h, 112)

            with nc.Block() as block:
                ctx.finalize(block)
            return nc

        h = emit_proj_ln(
            ctx, "p5c", NT_O,
            lhsW=None, rhs=None, rhs_load=None,
            res=lambda nt: st3f_i[:, :, nt * TT:(nt + 1) * TT],
            out_f32=lambda nt: out_o[:, :, nt * TT:(nt + 1) * TT],
            out_bf=None,
            bufs=bufs,
            proj=False,
            y_src=lambda nt, oc: bufs["acc"][:, oc, nt * TT:(nt + 1) * TT],
            y_ready=(s_acc, {(nt, oc): n_acc
                             for nt in range(NT_O) for oc in range(8)}),
            pre_wait=lambda t: t.wait_ge(s_acc.h, n_acc),
            sync_gate=lambda q: q.wait_ge(s_acc.h, n_acc),
        )
        so0, so1 = h["s_so0"], h["s_so1"]
        n_so0, n_so1 = so0.n, so1.n

        if _P5_DEBUG:
            dbg_g = nc.dram_tensor("dbg_gate", [8, OWN], F32,
                                   kind="ExternalOutput").ap()
            dbg_w = nc.dram_tensor("dbg_wts", [8, OWN], F32,
                                   kind="ExternalOutput").ap()
            dbg_a = nc.dram_tensor("dbg_acc", [128, 8, OWN], F32,
                                   kind="ExternalOutput").ap()
            dbg_s = nc.dram_tensor("dbg_gsr", [128, 16, 8], F32,
                                   kind="ExternalOutput").ap()
            dbg_t = nc.dram_tensor("dbg_gtm", [128, 16, 8], F32,
                                   kind="ExternalOutput").ap()
            s_dbg = ctx.sem("p5_dbg")
            s_te2d = ctx.sem("p5_te2")
            s_te1d = ctx.sem("p5_te1")
            s_gevd = ctx.sem("p5_gev")
            s_srtd = ctx.sem("p5_srt")
            s_accd = ctx.sem("p5_acc")
            nte2, nacc = s_te2d.n, s_accd.n

            @ctx.on("gpsimd")
            def _(g):
                g.wait_ge(s_gevd.h, 4)
                g.dma_start(out=dbg_g[:], in_=bufs["gate8"]).then_inc(s_dbg.h, 16)
                g.wait_ge(s_te1d.h, 16)
                g.dma_start(out=dbg_t[:], in_=bufs["gtm"]).then_inc(s_dbg.h, 16)
                g.wait_ge(s_srtd.h, 1)
                g.dma_start(out=dbg_s[:], in_=bufs["gsr"]).then_inc(s_dbg.h, 16)
                g.wait_ge(s_te2d.h, nte2)
                g.dma_start(out=dbg_w[:], in_=bufs["wtsT"][:]).then_inc(s_dbg.h, 16)
                g.wait_ge(s_accd.h, nacc)
                g.dma_start(out=dbg_a[:], in_=bufs["acc"][:]).then_inc(s_dbg.h, 16)
                g.wait_ge(s_dbg.h, 80)

        @ctx.on("gpsimd")
        def _(g):
            g.wait_ge(so0.h, n_so0)
            g.wait_ge(so1.h, n_so1)

        with nc.Block() as block:
            ctx.finalize(block)
    return nc


def build_p12(core):
    import contextlib
    nc = bass.Bass("TRN2", target_bir_lowering=False)
    ap = {
        "xt": nc.dram_tensor("xt", [D, SP], F32, kind="ExternalInput").ap(),
        "wqkvT": nc.dram_tensor("wqkvT", [D, 768], F32, kind="ExternalInput").ap(),
    }
    l0dm_i = nc.dram_tensor("l0dm_i", [4, 128, TT], BF16, kind="ExternalInput").ap()
    sel2_i = nc.dram_tensor("sel2_i", [2, 2], F32, kind="ExternalInput").ap()
    attn_o = nc.dram_tensor("attn_o", [2, 128, S], BF16, kind="ExternalOutput").ap()

    with contextlib.ExitStack() as stack:
        bufs = {
            "ktT": sbuf(stack, nc, "ktT", [128, 2, S], BF16),
            "qtT": sbuf(stack, nc, "qtT", [128, 2, S], BF16),
            "vkm": sbuf(stack, nc, "vkm", [128, S // 128, HPC * 65], BF16),
            "nrm_all": sbuf(stack, nc, "nrm_all", [2, 4, NT_S], F32),
            "nrm": sbuf(stack, nc, "nrm", [2, 4], F32),
            "p1_wbf": sbuf(stack, nc, "p1_wbf", [128, 8, 768], BF16),
            "p1_wld": [sbuf(stack, nc, f"p1_wld{i}", [128, 768], F32) for i in range(2)],
            "p1_xf": [sbuf(stack, nc, f"p1_xf{i}", [128, 8, TT], F32) for i in range(2)],
            "p1_xb": [sbuf(stack, nc, f"p1_xb{i}", [128, 8, TT], BF16) for i in range(2)],
            "p1_sqb": [sbuf(stack, nc, f"p1_sqb{i}", [128, TT], BF16) for i in range(2)],
            "ones2": sbuf(stack, nc, "ones2", [128, 2], BF16),
            "l0dm": sbuf(stack, nc, "l0dm", [128, 4, TT], BF16),
            "ones32": sbuf(stack, nc, "ones32", [1, 128], F32),
            "sel2": sbuf(stack, nc, "sel2", [2, 2], F32),
            "negbg": sbuf(stack, nc, "negbg", [128, 4], F32),
            "p2_rows": sbuf(stack, nc, "p2_rows", [1, 16], F32),
            "p2_den0": sbuf(stack, nc, "p2_den0", [1, TT], F32),
            "p2_den1": sbuf(stack, nc, "p2_den1", [1, TT], F32),
            "p2_bcs0": sbuf(stack, nc, "p2_bcs0", [64, TT], F32),
            "p2_bcs1": sbuf(stack, nc, "p2_bcs1", [64, TT], F32),
            "p2_p0": sbuf(stack, nc, "p2_p0", [128, TT], BF16),
            "p2_p1": sbuf(stack, nc, "p2_p1", [128, TT], BF16),
            "ps_a": psum(stack, nc, "ps_a", [128, TT]),
            "ps_b": psum(stack, nc, "ps_b", [128, TT]),
            "ps_c": psum(stack, nc, "ps_c", [128, HPC * 64]),
            "ps_d": psum(stack, nc, "ps_d", [128, HPC * 64]),
            "ps_e": psum(stack, nc, "ps_e", [65, TT]),
            "ps_f": psum(stack, nc, "ps_f", [65, TT]),
            "ps_g": psum(stack, nc, "ps_g", [64, TT]),
            "ps_n": psum(stack, nc, "ps_n", [2, TT]),
        }
        bufs["attnT"] = bufs["qtT"]
        ctx = Ctx(nc, stack)
        s2ld = ctx.sem("p12_ld")
        s_init = ctx.sem("p12_init")

        @ctx.on("sync")
        def _(q):
            q.dma_start(out=bufs["l0dm"][:],
                        in_=l0dm_i.rearrange("r p t -> p r t")).then_inc(s2ld.h, 16)
            q.dma_start(out=bufs["sel2"][:], in_=sel2_i[:]).then_inc(s2ld.h, 16)

        @ctx.on("vector")
        def _(v):
            v.memset(bufs["ones32"][:], 1.0).then_inc(s_init.h, 1)
        s_init.inc()

        emit_p1(ctx, bufs, ap)
        s_ev, s_evv = ctx.sem("p1_ev"), ctx.sem("p1_evv")
        s_nr, s_sq = ctx.sem("p1_nr"), ctx.sem("p1_sq")
        n_ev, n_evv, n_nr, n_sq = s_ev.n, s_evv.n, s_nr.n, s_sq.n

        def gate(e):
            e.wait_ge(s_ev.h, n_ev)
            e.wait_ge(s_evv.h, n_evv)
            e.wait_ge(s_nr.h, n_nr)
            e.wait_ge(s_sq.h, n_sq)
            e.wait_ge(s2ld.h, 32)
            e.wait_ge(s_init.h, 1)
        ctx.streams["tensor"].append(gate)
        ctx.streams["scalar"].append(gate)
        ctx.streams["vector"].append(gate)

        emit_p2(ctx, bufs, {})

        s_no = ctx.sem("p2_no")
        n_no = s_no.n
        s_st = ctx.sem("p12_st")

        @ctx.on("gpsimd")
        def _(g):
            g.wait_ge(s_no.h, n_no)
            g.dma_start(out=attn_o.rearrange("c p t -> p c t"),
                        in_=bufs["attnT"][:]).then_inc(s_st.h, 16)
            g.wait_ge(s_st.h, 16)

        with nc.Block() as block:
            ctx.finalize(block)
    return nc


_PROGS = {}


def _prog(name, builder):
    if name not in _PROGS:
        _PROGS[name] = builder(0)
    return _PROGS[name]


def kernel(**inputs):
    cores = list(range(8))
    im = [prep_core_inputs(c, inputs) for c in range(8)]

    r2 = run_bass_kernel_spmd(_prog("p12", build_p12), [
        {"xt": im[c]["xt"], "wqkvT": im[c]["wqkvT"],
         "l0dm_i": im[c]["l0dm"], "sel2_i": im[c]["sel2"]}
        for c in range(8)], core_ids=cores)

    ag = np.zeros((B, D, SP), dtype=BF16NP)
    for c in range(8):
        bb, q = c // 4, c % 4
        ag[bb, q * 256:(q + 1) * 256, PAD:] = \
            r2.results[c]["attn_o"].reshape(256, S)

    sub3 = []
    for c in range(8):
        bb, q = c // 4, c % 4
        own = q * OWN
        sub3.append({"ags": np.ascontiguousarray(ag[bb][:, own:own + NTOK]),
                     "xs": im[c]["xs"], "woutT": im[c]["woutT"]})
    r3a = run_bass_kernel_spmd(_prog("p3a", build_p3a), sub3, core_ids=cores)
    r3m = run_bass_kernel_spmd(_prog("p3m", build_p3m), [
        {k: im[c][k] for k in ("wr", "wi", "cb", "sb", "c2b", "ns2b")}
        for c in range(8)], core_ids=cores)
    r3c = run_bass_kernel_spmd(_prog("p3c", build_p3c), [
        {"m_i": r3m.results[c]["m_o"], "st1f_i": r3a.results[c]["st1f_o"],
         "st1b_i": r3a.results[c]["st1b_o"], "l0b_i": im[c]["l0bias_pc"]}
        for c in range(8)], core_ids=cores)

    r4 = run_bass_kernel_spmd(_prog("p4", build_p4), [
        {"st2f_i": r3c.results[c]["st2f_o"], "st2b_i": r3c.results[c]["st2b_o"],
         "wq1": im[c]["wq1"], "wk1": im[c]["wk1"], "wv1": im[c]["wv1"],
         "wo1": im[c]["wo1"], "bq_pc": im[c]["bq_pc"], "bk_pc": im[c]["bk_pc"],
         "bv_row": im[c]["bv_row"], "bo_pc": im[c]["bo_pc"],
         "l1m": im[c]["l1m"], "sel2_i": im[c]["sel2"]}
        for c in range(8)], core_ids=cores)

    r5w = run_bass_kernel_spmd(_prog("p5w", build_p5w), [
        {k: im[c][k] for k in ("we1", "we2", "we3")} for c in range(8)],
        core_ids=cores)
    r5 = run_bass_kernel_spmd(_prog("p5", build_p5), [
        {"st3f_i": r4.results[c]["st3f_o"], "st3b_i": r4.results[c]["st3b_o"],
         "wgT": im[c]["wgT"], "rbias_c": im[c]["rbias_c"], "eye8": im[c]["eye8"],
         "eye128": im[c]["eye128"], "sel8": im[c]["sel8"],
         "w1b": r5w.results[c]["w1b"], "w2b": r5w.results[c]["w2b"],
         "w3b": r5w.results[c]["w3b"]}
        for c in range(8)], core_ids=cores)

    out = np.empty((B, S, D), np.float32)
    for c in range(8):
        bb, q = c // 4, c % 4
        own = q * OWN
        oc = r5.results[c]["out_o"].reshape(128, 8, OWN)
        out[bb, own:own + OWN, :] = oc.transpose(2, 1, 0).reshape(OWN, D)
    return out
